# version 03
# baseline (speedup 1.0000x reference)
import numpy as np
import ml_dtypes
from contextlib import ExitStack

import concourse.bass as bass
import concourse.mybir as mybir
from concourse.bass_utils import run_bass_kernel_spmd

F32 = mybir.dt.float32
BF16 = mybir.dt.bfloat16
U32 = mybir.dt.uint32
AF = mybir.ActivationFunctionType
ALU = mybir.AluOpType
AX = mybir.AxisListType

T = 2048
D = 1024
NH = 16
HD = 64
NE = 8
FF = 2048
NCORE = 8
SEQ_PER_CORE = 2

PC_BQ, PC_BK, PC_BO, PC_GB = 0, 8, 16, 24
PC_N1G, PC_N1B = 32, 40
PC_TAU, PC_THR, PC_PHASE = 48, 49, 50
PC_SW, PC_RSC, PC_RSCP = 53, 69, 77
PC_B1, PC_LNG, PC_LNB = 78, 206, 334
PC_IOTA = 462
PC_N = 463

CC_03, CC_EPS6, CC_ONE, CC_NEG1, CC_NEG8, CC_NEGH = 0, 1, 2, 3, 4, 5
CC_I2048, CC_I1024, CC_EPS5, CC_QS1, CC_QS2 = 6, 7, 8, 9, 10
CC_VALS = {CC_03: 0.3, CC_EPS6: 1e-6, CC_ONE: 1.0, CC_NEG1: -1.0, CC_NEG8: -8.0,
           CC_NEGH: -0.5, CC_I2048: 1.0 / 2048.0, CC_I1024: 1.0 / 1024.0,
           CC_EPS5: 1e-5, CC_QS1: 1.0 / 2.4, CC_QS2: 0.125}

ENGS = ("sync", "tensor", "vector", "scalar", "gpsimd")


class Prog:

    def __init__(self, nc, stack):
        self.nc = nc
        self.stack = stack
        self.q = {e: [] for e in ENGS}
        self.sem = {}
        self.kind = {}
        self.owner = {}
        self.cnt = {}
        for e in ENGS:
            self._mk(e, e, "dma" if e == "sync" else "comp")
        for name in ("ld_w", "ld_m", "ld_sg", "st"):
            self._mk(name, None, "dma")
        self.waited = {c: {} for c in ENGS}

    def _mk(self, name, eng, kind):
        self.sem[name] = self.stack.enter_context(self.nc.semaphore(f"s_{name}"))
        self.kind[name] = kind
        self.owner[name] = eng
        self.cnt[name] = 0

    def on(self, eng, fn, wait=(), inc=0, tag="", sem=None, lowp=False):
        sn = sem or eng
        w = []
        for tok in wait:
            if tok is None:
                continue
            pe, v = tok
            assert v <= self.cnt[pe], f"future token {tag}"
            if self.kind[pe] == "dma":
                v = self.cnt[pe]
            if self.waited[eng].get(pe, 0) >= v:
                continue
            self.waited[eng][pe] = v
            w.append((pe, v))
        tok_out = None
        if inc:
            self.cnt[sn] += inc
            tok_out = (sn, self.cnt[sn])
        semd = self.sem

        nc_ = self.nc

        def run(engobj, w=w, fn=fn, inc=inc, sn=sn, tag=tag, lowp=lowp):
            for pe, v in w:
                engobj.wait_ge(semd[pe], v)
            if lowp:
                with nc_.allow_low_precision(reason="bf16 sigmoid reciprocal; within measured tolerance"):
                    inst = fn(engobj)
            else:
                inst = fn(engobj)
            if inc:
                assert inst is not None, f"step {tag} returned no instruction"
                inst.then_inc(semd[sn], inc)
        self.q[eng].append(run)
        return tok_out

    def tok(self, sn):
        return (sn, self.cnt[sn])

    def barrier(self):
        toks = [self.tok(e) for e in ENGS] + [self.tok(x) for x in ("ld_w", "ld_m", "ld_sg", "st")]
        out = []
        for e in ENGS:
            out.append(self.on(e, lambda en: en.nop(), wait=toks, inc=1, tag="barrier"))
        return out


def _f2b(a):
    return np.asarray(a, dtype=np.float32).astype(ml_dtypes.bfloat16)


def _host_consts():
    c = {}
    pos = np.arange(T, dtype=np.float64) / T * 2 * np.pi
    bases = np.stack([np.sin(pos), np.cos(pos), np.sin(pos / 2), np.cos(pos / 2),
                      np.ones_like(pos)])
    c["bases"] = _f2b(bases)
    c["identb"] = _f2b(np.eye(128))
    c["identf"] = np.eye(128, dtype=np.float32)
    E = np.zeros((128, 128))
    for h in range(16):
        for i in range(6):
            E[16 * i + h, 8 * h + i] = 1.0
    c["E128"] = _f2b(E)
    c["onesr"] = _f2b(np.ones((1, 128)))
    c["onesc"] = _f2b(np.ones((128, 1)))
    s2a = np.zeros((1, 128)); s2a[0, 0:64] = 1.0
    s2b = np.zeros((1, 128)); s2b[0, 64:128] = 1.0
    c["sel2a"] = _f2b(s2a)
    c["sel2b"] = _f2b(s2b)
    c["sel22"] = _f2b(np.concatenate([s2a, s2b], axis=0))
    c["iota8"] = np.arange(8, dtype=np.float32).reshape(8, 1)

    S1, C1, SH, CH, ON = range(5)
    def ent(h, j):
        return 7 * h + j
    E_NEGT, E_ONE = ent(0, 6), ent(1, 6)
    one = ("one", 1.0)

    def a_of(spec, h):
        kind, sg = spec
        if kind == "one":
            return E_ONE, sg
        if kind == "negt":
            return E_NEGT, sg
        trig, j = kind
        return ent(h, j if trig == "s" else 3 + j), sg

    sin_ = lambda j, s=1.0: (("s", j), s)
    cos_ = lambda j, s=1.0: (("c", j), s)
    TQ1 = [(S1, cos_(0), C1, sin_(0)),
           (CH, cos_(2), SH, sin_(2, -1.0)),
           (SH, cos_(2), CH, sin_(2)),
           (SH, cos_(2), CH, sin_(2)),
           (CH, cos_(2), SH, sin_(2, -1.0)),
           (ON, one, ON, None)]
    TQ2 = [(ON, one, ON, None), (ON, one, ON, None), (ON, one, ON, None),
           (C1, cos_(0), S1, sin_(0, -1.0)),
           (C1, cos_(0), S1, sin_(0, -1.0)),
           (ON, ("negt", 1.0), ON, None)]
    TK1 = [(C1, cos_(1), S1, sin_(1, -1.0)),
           (S1, cos_(1), C1, sin_(1)),
           (S1, cos_(1), C1, sin_(1)),
           (CH, one, ON, None), (SH, one, ON, None), (ON, one, ON, None)]
    TK2 = [(ON, one, ON, None), (CH, one, ON, None), (SH, ("one", -1.0), ON, None),
           (ON, one, ON, None), (ON, one, ON, None), (ON, one, ON, None)]
    tiles = [TQ1, TQ2, TK1, TK2]
    SELB1 = np.zeros((5, 4 * 128))
    SELB2 = np.zeros((5, 4 * 128))
    PA = np.zeros((112, 8 * 128))
    for b, tl in enumerate(tiles):
        for pp in range(128):
            g, h = pp // 16, pp % 16
            if g >= 6:
                continue
            b1, al, b2, be = tl[g]
            SELB1[b1, 128 * b + pp] = 1.0
            idx, sg = a_of(al, h)
            PA[idx, 128 * (2 * b) + pp] = sg
            if be is not None:
                SELB2[b2, 128 * b + pp] = 1.0
                idx2, sg2 = a_of(be, h)
                PA[idx2, 128 * (2 * b + 1) + pp] = sg2
    c["SELB1"] = _f2b(SELB1)
    c["SELB2"] = _f2b(SELB2)
    c["PA"] = _f2b(PA)
    return c


_HC = _host_consts()


def build_core(nc, taps=(), upto="all"):
    def din(name, shape, dtype=F32):
        return nc.dram_tensor(name, list(shape), dtype, kind="ExternalInput").ap()

    xTd = din("xT", [SEQ_PER_CORE, D, T])
    Wq, Wk, Wv, Wo = (din(n, [D, D]) for n in ("Wq", "Wk", "Wv", "Wo"))
    qw, kw, vw, ow, gw = (din(n, [D, D]) for n in ("qw", "kw", "vw", "ow", "gw"))
    mgw = din("mgw", [D, NE])
    w1d = din("w1", [NE, D, FF])
    w2d = din("w2", [NE, FF, D])
    b2d = din("b2", [NE, D])
    bvr_d = din("bvr", [D])
    n2g_d = din("n2g", [D])
    n2b_d = din("n2b", [D])
    mgb_d = din("mgb", [NE])
    pvd = din("pv", [128, PC_N])
    out_d = nc.dram_tensor("out", [SEQ_PER_CORE, T, D], F32, kind="ExternalOutput").ap()

    mk_kind = "ExternalOutput" if "mln" in taps else "Internal"
    mln_d = nc.dram_tensor("mln", [NH, T, T], BF16, kind=mk_kind).ap()
    sig_d = nc.dram_tensor("sig", [2, T, T], BF16, kind="Internal").ap()
    scb_d = nc.dram_tensor("scb", [112], BF16, kind="Internal").ap()
    tap_d = {}
    for tname, shape, dt in (("x2", [SEQ_PER_CORE, D, T], BF16),
                             ("x3", [SEQ_PER_CORE, D, T], BF16),
                             ("ao1", [SEQ_PER_CORE, D, T], BF16),
                             ("ao2", [SEQ_PER_CORE, D, T], BF16),
                             ("chosen", [SEQ_PER_CORE, 128, 16], F32)):
        if tname in taps:
            tap_d[tname] = nc.dram_tensor("tap_" + tname, shape, dt, kind="ExternalOutput").ap()

    cst = {k: nc.inline_tensor(v, "c_" + k).ap() for k, v in _HC.items()}

    with ExitStack() as ctx:
        def sb(name, shape, dtype=BF16):
            return ctx.enter_context(nc.sbuf_tensor(name, list(shape), dtype))
        xA = sb("xA", [128, 16384])
        xB = sb("xB", [128, 16384])
        ao = sb("ao", [128, 16384])
        wk = sb("wk", [128, 12288])
        mz = sb("mz", [128, 16384])
        qk = sb("qk", [128, 4096])
        vab = sb("vab", [128, 2128])
        pb = sb("pb", [128, 2048])
        hhB = sb("hhB", [128, 8192])
        sh1 = sb("sh1", [128, 2048])
        sh2 = sb("sh2", [128, 1024])
        wst = sb("wst", [128, 1024], F32)
        pv = sb("pvt", [128, PC_N], F32)
        cc = sb("cc", [128, 16], F32)
        bvr = sb("bvrt", [1, D])
        mgbr = sb("mgbrt", [1, NE])
        n2gbc = sh1[:, 0:1024]
        n2bbc = sh1[:, 1024:2048]
        b2rs = sb("b2rst", [NE, D])
        rowb1 = sb("rowb1", [1, 2048])
        rowb2 = sb("rowb2", [1, 1024])
        sct = sb("sct", [16, 7], F32)
        sc112 = sb("sc112", [112, 1])
        scw = sb("scw", [16, 3], F32)
        ab8 = sb("ab8", [128, 8], F32)
        invt = sb("invt", [128, 1], F32)
        bqs = sb("bqs", [128, 8], F32)
        gbn = sb("gbn", [128, 8], F32)
        swb = sb("swb", [128, 16])
        lgt = sb("lgt", [128, 8], F32)
        v8t = sb("v8t", [128, 8], F32)
        i8t = sb("i8t", [128, 8], U32)
        i8f = sb("i8f", [128, 8], F32)
        chosen = sb("chosen", [128, 16], F32)
        sctb = lgt[0:16, 0:8].bitcast(BF16)[:, 0:7]
        iota8 = pv[0:8, PC_IOTA:PC_IOTA + 1]
        chosb = i8f[:, 0:8].bitcast(BF16)[:, 0:2]
        colf = sb("colf", [128, 12], F32)
        identb = sb("identb", [128, 128])
        E128 = sb("E128", [128, 128])
        onesr = sb("onesr", [1, 128])
        onesc = sb("onesc", [128, 1])
        sel2b = sb("sel2b", [1, 128])
        sel22 = sb("sel22", [2, 128])
        sel2a = sel22[0:1, 0:128]

        UT = xB[0:8, 0:2048]
        VT = xB[0:8, 2048:4096]
        ch8 = sh2[0:8, 0:256].bitcast(F32)
        chrow = sh2[0:1, 256:384]
        m8b = sh2[0:8, 384:512]
        x3tm = qk[:, 0:1024]
        x2tm = qk[:, 1024:2048]
        ztt = qk[:, 2048:3072]
        f2t = qk[:, 3072:4096]
        sqt = pb[:, 0:1024]
        ot = vab[:, 0:2048].bitcast(F32)
        s2f = sh1[0:2, 0:2048]
        s2b_ = s2f
        basesb = mz[0:5, 12288:14336]
        PAc = mz[0:112, 14336:15360]
        SELB1 = mz[0:5, 15360:15872]
        SELB2 = mz[0:5, 15872:16384]
        psA = ctx.enter_context(nc.psum_tensor("psA", [128, 1024], F32))
        psB = ctx.enter_context(nc.psum_tensor("psB", [128, 1024], F32))
        psC = ctx.enter_context(nc.psum_tensor("psC", [128, 2048], F32))

        p = Prog(nc, ctx)
        blk = ctx.enter_context(nc.Block())
        mzsl = lambda o, n=2048: mz[:, o:o + n]

        ld = []
        for dst, src in ((pv, pvd), (identb, cst["identb"]),
                         (E128, cst["E128"]), (onesr, cst["onesr"]), (onesc, cst["onesc"]),
                         (sel2b, cst["sel2b"]), (sel22, cst["sel22"]),
                         (basesb, cst["bases"]),
                         (SELB1, cst["SELB1"]), (SELB2, cst["SELB2"]), (PAc, cst["PA"])):
            ld.append(p.on("sync", lambda e, d=dst, s=src: e.dma_start(out=d[:, :], in_=s),
                           inc=16, tag="ldc"))
        t_bv = p.on("sync", lambda e: e.dma_start(out=wst[0:1, 0:D], in_=bvr_d.rearrange("(a d) -> a d", a=1)), inc=16)
        t_bvc = p.on("gpsimd", lambda e: e.tensor_copy(bvr[:, :], wst[0:1, 0:D]), wait=[t_bv], inc=1)
        t_mgb = p.on("sync", lambda e: e.dma_start(out=wst[0:1, 0:NE], in_=mgb_d.rearrange("(a d) -> a d", a=1)), wait=[t_bvc], inc=16)
        t_mgbc = p.on("gpsimd", lambda e: e.tensor_copy(mgbr[:, :], wst[0:1, 0:NE]), wait=[t_mgb], inc=1)
        t_b2 = p.on("sync", lambda e: e.dma_start(out=wst[0:NE, 0:D], in_=b2d), wait=[t_mgbc], inc=16)
        for col, val in CC_VALS.items():
            p.on("vector", lambda e, c=col, v=val: e.memset(cc[:, c:c + 1], v), inc=1)
        t_cc = p.on("vector", lambda e: e.memset(cc[:, 15:16], 0.0), inc=1)
        wld = ld + [t_cc]

        t_b2rs = p.on("vector", lambda e: e.tensor_scalar_mul(b2rs[:, :], wst[0:NE, 0:D], pv[0:NE, PC_RSCP:PC_RSCP + 1]),
                      wait=wld + [t_b2], inc=1)
        t_bqs = p.on("vector", lambda e: e.tensor_scalar_mul(bqs[:, :], pv[:, PC_BQ:PC_BQ + 8], 1.0 / 2.4), wait=wld, inc=1)
        t_gbn = p.on("vector", lambda e: e.tensor_scalar_mul(gbn[:, :], pv[:, PC_GB:PC_GB + 8], -1.0), wait=wld, inc=1)
        t_swb = p.on("gpsimd", lambda e: e.tensor_copy(swb[:, :], pv[:, PC_SW:PC_SW + 16]), wait=wld, inc=1)
        t_i1 = p.on("vector", lambda e: e.tensor_scalar_add(colf[:, 0:1], pv[:, PC_TAU:PC_TAU + 1], 1e-6), wait=wld, inc=1)
        t_invt = p.on("vector", lambda e: e.reciprocal(invt[:, :], colf[:, 0:1]), wait=[t_i1], inc=1)

        psc_free = [t_b2rs]

        PI = float(np.pi)
        phs = pv[0:16, PC_PHASE:PC_PHASE + 3]
        t1 = p.on("vector", lambda e: e.tensor_single_scalar(scw[:, :], phs, PI, ALU.is_gt), wait=wld, inc=1)
        t2 = p.on("vector", lambda e: e.scalar_tensor_tensor(sct[:, 3:6], scw[:, :], -2 * PI, phs, ALU.mult, ALU.add), wait=[t1], inc=1)
        t3 = p.on("vector", lambda e: e.tensor_single_scalar(scw[:, :], sct[:, 3:6], -PI, ALU.is_lt), wait=[t2], inc=1)
        t4 = p.on("vector", lambda e: e.scalar_tensor_tensor(sct[:, 3:6], scw[:, :], 2 * PI, sct[:, 3:6], ALU.mult, ALU.add), wait=[t3], inc=1)
        t5 = p.on("vector", lambda e: e.tensor_scalar_add(scw[:, :], sct[:, 3:6], PI / 2), wait=[t4], inc=1)
        t6 = p.on("vector", lambda e: e.tensor_single_scalar(sct[:, 0:3], scw[:, :], PI, ALU.is_gt), wait=[t5], inc=1)
        t7 = p.on("vector", lambda e: e.scalar_tensor_tensor(scw[:, :], sct[:, 0:3], -2 * PI, scw[:, :], ALU.mult, ALU.add), wait=[t6], inc=1)
        t8 = p.on("scalar", lambda e: e.activation(sct[:, 3:6], sct[:, 3:6], AF.Sin), wait=[t7], inc=1)
        t9 = p.on("scalar", lambda e: e.activation(sct[:, 0:3], scw[:, :], AF.Sin), wait=[t8], inc=1)
        t10 = p.on("vector", lambda e: e.tensor_copy(scw[:, :], sct[:, 3:6]), wait=[t9], inc=1)
        t11 = p.on("vector", lambda e: e.tensor_copy(sct[:, 3:6], sct[:, 0:3]), wait=[t10], inc=1)
        t12 = p.on("vector", lambda e: e.tensor_copy(sct[:, 0:3], scw[:, :]), wait=[t11], inc=1)
        t13 = p.on("vector", lambda e: e.memset(sct[:, 6:7], 1.0), wait=[t12], inc=1)
        t14 = p.on("vector", lambda e: e.tensor_scalar_mul(sct[0:1, 6:7], pv[0:1, PC_THR:PC_THR + 1], -1.0), wait=[t13], inc=1)
        t14b = p.on("gpsimd", lambda e: e.tensor_copy(sctb[:, :], sct[:, :]), wait=[t14], inc=1)
        t15 = p.on("gpsimd", lambda e: e.dma_start(out=scb_d.rearrange("(h j) -> h j", j=7), in_=sctb[:, :]), wait=[t14b], inc=16, sem="st")
        t16 = p.on("sync", lambda e: e.dma_start(out=sc112[:, :], in_=scb_d.rearrange("(p o) -> p o", o=1)), wait=[t15], inc=16)
        tab = [p.on("tensor", lambda e, b=b: e.matmul(psB[:, b:b + 1], PAc[:, 128 * b:128 * (b + 1)], sc112[:, :], start=True, stop=True),
               wait=[t16] + wld, inc=1) for b in range(8)]
        t_ab = p.on("vector", lambda e: e.tensor_copy(ab8[:, :], psB[:, 0:8]), wait=tab, inc=1)

        t_prev = t_ab
        tdone = []
        for b in range(4):
            m1 = [p.on("tensor", lambda e, b=b, j=j: e.matmul((psA if j < 2 else psB)[:, 512 * (j % 2):512 * (j % 2 + 1)], SELB1[:, 128 * b:128 * (b + 1)], basesb[:, 512 * j:512 * (j + 1)], start=True, stop=True),
                  wait=[t_prev] + wld, inc=(1 if j in (1, 3) else 0), tag=f"selb1_{b}") for j in range(4)]
            c1 = p.on("vector", lambda e, b=b: e.tensor_scalar_mul(mzsl(8192)[:, 0:1024], psA[:, :], ab8[:, 2 * b:2 * b + 1]), wait=[m1[1], m1[3]], inc=1)
            c2 = p.on("vector", lambda e, b=b: e.tensor_scalar_mul(mzsl(8192)[:, 1024:2048], psB[:, :], ab8[:, 2 * b:2 * b + 1]), wait=[m1[1], m1[3]], inc=1)
            m2 = [p.on("tensor", lambda e, b=b, j=j: e.matmul((psA if j < 2 else psB)[:, 512 * (j % 2):512 * (j % 2 + 1)], SELB2[:, 128 * b:128 * (b + 1)], basesb[:, 512 * j:512 * (j + 1)], start=True, stop=True),
                  wait=[c1, c2], inc=(1 if j in (1, 3) else 0), tag=f"selb2_{b}") for j in range(4)]
            d1 = p.on("vector", lambda e, b=b: e.scalar_tensor_tensor(mzsl(2048 * b)[:, 0:1024], psA[:, :], ab8[:, 2 * b + 1:2 * b + 2], mzsl(8192)[:, 0:1024], ALU.mult, ALU.add), wait=[m2[1], m2[3]], inc=1)
            d2 = p.on("vector", lambda e, b=b: e.scalar_tensor_tensor(mzsl(2048 * b)[:, 1024:2048], psB[:, :], ab8[:, 2 * b + 1:2 * b + 2], mzsl(8192)[:, 1024:2048], ALU.mult, ALU.add), wait=[m2[1], m2[3]], inc=1)
            t_prev = d2
            tdone += [d1, d2]
        tU = p.on("vector", lambda e: e.tensor_mul(mzsl(0)[:, :], mzsl(0)[:, :], mzsl(2048)[:, :]), wait=tdone[0:4], inc=1)
        tU2 = p.on("vector", lambda e: e.tensor_scalar_mul(mzsl(0)[:, :], mzsl(0)[:, :], invt[:, :]), wait=[tU, t_invt], inc=1)
        tV = p.on("vector", lambda e: e.tensor_mul(mzsl(4096)[:, :], mzsl(4096)[:, :], mzsl(6144)[:, :]), wait=tdone[4:8], inc=1)
        psab_free = [tU2, tV]

        mask_chunk_tok = {}
        mask_state = {"uvfree": [tU2, tV], "psab": psab_free, "st_prev": None, "ln_prev": tdone[-1]}

        def emit_mask_head(h):
            sl = h % 2
            exu = [p.on("tensor", lambda e, j=j: e.matmul((psA if j < 2 else psB)[0:8, 512 * (j % 2):512 * (j % 2 + 1)], E128[:, 8 * h:8 * h + 8], mzsl(0)[:, 512 * j:512 * (j + 1)], start=True, stop=True),
                   wait=[tU2] + mask_state["uvfree"] + mask_state["psab"], inc=(1 if j in (1, 3) else 0), tag=f"exu{h}") for j in range(4)]
            cu1 = p.on("scalar", lambda e: e.activation(UT[:, 0:1024], psA[0:8, :], AF.Copy), wait=[exu[1], exu[3]], inc=1)
            cu2 = p.on("scalar", lambda e: e.activation(UT[:, 1024:2048], psB[0:8, :], AF.Copy), wait=[exu[1], exu[3]], inc=1)
            exv = [p.on("tensor", lambda e, j=j: e.matmul((psA if j < 2 else psB)[0:8, 512 * (j % 2):512 * (j % 2 + 1)], E128[:, 8 * h:8 * h + 8], mzsl(4096)[:, 512 * j:512 * (j + 1)], start=True, stop=True),
                   wait=[tV, cu1, cu2], inc=(1 if j in (1, 3) else 0), tag=f"exv{h}") for j in range(4)]
            cv1 = p.on("scalar", lambda e: e.activation(VT[:, 0:1024], psA[0:8, :], AF.Copy), wait=[exv[1], exv[3]], inc=1)
            cv2 = p.on("scalar", lambda e: e.activation(VT[:, 1024:2048], psB[0:8, :], AF.Copy), wait=[exv[1], exv[3]], inc=1)
            prev_sig = [cv1, cv2]
            st_prev = mask_state["st_prev"]
            sg_store = []
            for kc in range(16):
                ms = [p.on("tensor", lambda e, kc=kc, j=j: e.matmul((psA if j < 2 else psB)[:, 512 * (j % 2):512 * (j % 2 + 1)], VT[:, 128 * kc:128 * (kc + 1)], UT[:, 512 * j:512 * (j + 1)], start=True, stop=True),
                      wait=prev_sig, inc=(1 if j in (1, 3) else 0), tag=f"vmm{h}_{kc}") for j in range(4)]
                s1t = p.on("scalar", lambda e: e.activation(mzsl(10240)[:, 0:1024], psA[:, :], AF.Sigmoid), wait=[ms[1], ms[3], st_prev], inc=1)
                s2t = p.on("scalar", lambda e: e.activation(mzsl(10240)[:, 1024:2048], psB[:, :], AF.Sigmoid), wait=[ms[1], ms[3], st_prev], inc=1)
                st_prev = p.on("gpsimd", lambda e, kc=kc, sl=sl: e.dma_start(out=sig_d[sl, 128 * kc:128 * (kc + 1), :], in_=mzsl(10240)[:, :]), wait=[s1t, s2t], inc=16, tag=f"sgst{h}_{kc}", sem="st")
                sg_store.append(st_prev)
                prev_sig = [s1t, s2t]
            mask_state["uvfree"] = [ms[1], ms[3]]
            mask_state["psab"] = prev_sig
            mask_state["st_prev"] = st_prev
            ln_prev = mask_state["ln_prev"]
            lnld = {}
            for kc in range(2):
                lnld[kc] = p.on("sync", lambda e, kc=kc, sl=sl: e.dma_start(out=mzsl(12288 + 2048 * (kc % 2))[:, :], in_=sig_d[sl, 128 * kc:128 * (kc + 1), :]), wait=[sg_store[kc], ln_prev], inc=16, sem="ld_sg")
            for kc in range(16):
                lt = p.on("scalar", lambda e, kc=kc: e.activation(mzsl(2048)[:, :], mzsl(12288 + 2048 * (kc % 2))[:, :], AF.Ln, bias=cc[:, CC_EPS6:CC_EPS6 + 1]),
                          wait=[lnld[kc], ln_prev], inc=1, tag=f"ln{h}_{kc}")
                ln_prev = p.on("gpsimd", lambda e, kc=kc: e.dma_start(out=mln_d[h, 128 * kc:128 * (kc + 1), :], in_=mzsl(2048)[:, :]), wait=[lt], inc=16, tag=f"mlnst{h}_{kc}", sem="st")
                mask_chunk_tok[(h, kc)] = ln_prev
                if kc + 2 < 16:
                    lnld[kc + 2] = p.on("sync", lambda e, kc=kc, sl=sl: e.dma_start(out=mzsl(12288 + 2048 * (kc % 2))[:, :], in_=sig_d[sl, 128 * (kc + 2):128 * (kc + 3), :]), wait=[sg_store[kc + 2], lt], inc=16, sem="ld_sg")
            mask_state["ln_prev"] = ln_prev

        def emit_xload(s, waits):
            prev = list(waits) + [st_ctx.get("wstfree")]
            for c in range(8):
                for hlf in range(2):
                    tl = p.on("sync", lambda e, c=c, h=hlf, s=s: e.dma_start(out=wst[:, :], in_=xTd[s, 128 * c:128 * (c + 1), 1024 * h:1024 * (h + 1)]), wait=prev, inc=16, tag="xld")
                    tc_ = p.on("gpsimd", lambda e, c=c, h=hlf: e.tensor_copy(xA[:, 2048 * c + 1024 * h:2048 * c + 1024 * (h + 1)], wst[:, :]), wait=[tl], inc=1)
                    prev = [tc_]
            st_ctx["wstfree"] = prev[0]
            return prev

        st_ctx = {"psab": psab_free, "psc": psc_free, "mbf": [tV, tU2], "wkfree": None, "wstfree": None}

        def emit_attention(s, masked, Wq_d, Wk_d, Wv_d, x_t, waits, pre_pair_hook=None):
            qscale = (1.0 / 2.4) if masked else 0.125
            wbuf_free = {0: list(waits), 1: list(waits)}
            prev_wcast = [st_ctx.get("wstfree")]
            pair_done = []
            vset = None
            for tc in range(16):
                vset = p.on("vector", lambda e, tc=tc: e.memset(vab[:, 65 * tc + 64:65 * tc + 65], 1.0), wait=list(waits) + [vset], inc=1)
                vset = p.on("vector", lambda e, tc=tc: e.memset(vab[:, 1088 + 65 * tc + 64:1088 + 65 * tc + 65], 1.0), wait=[vset], inc=1)
            t_von = vset
            for pr in range(8):
                if pre_pair_hook is not None:
                    pre_pair_hook(pr)
                par = pr % 2
                wo_ = 3072 * par
                wts = []
                for wi, (Wd, scl) in enumerate(((Wq_d, qscale), (Wk_d, None), (Wv_d, None))):
                    tl = p.on("sync", lambda e, Wd=Wd, pr=pr: e.dma_start(out=wst[:, :].rearrange("p (a j) -> p a j", a=8), in_=Wd[:, 128 * pr:128 * (pr + 1)].rearrange("(a p) j -> p a j", p=128)),
                              wait=wbuf_free[par] + list(waits) + prev_wcast + ([wts[-1]] if wts else []), inc=16, tag=f"wld{pr}_{wi}", sem="ld_w")
                    if scl is None:
                        tc_ = p.on("gpsimd", lambda e, wi=wi, wo_=wo_: e.tensor_copy(wk[:, wo_ + 1024 * wi:wo_ + 1024 * (wi + 1)], wst[:, :]), wait=[tl], inc=1)
                    else:
                        tc_ = p.on("gpsimd", lambda e, wi=wi, wo_=wo_, scl=scl: e.tensor_scalar_mul(wk[:, wo_ + 1024 * wi:wo_ + 1024 * (wi + 1)], wst[:, :], scl), wait=[tl], inc=1)
                    wts.append(tc_)
                    prev_wcast = [tc_]
                proj_prev = st_ctx["psab"]
                proj_last = None
                for mi in (0, 1):
                    for tc4 in range(4):
                        ps = psA if tc4 % 2 == 0 else psB
                        mm = [p.on("tensor", lambda e, a=a, mi=mi, tc4=tc4, ps=ps, wo_=wo_: e.matmul(ps[:, 0:512], wk[:, wo_ + 1024 * mi + 128 * a:wo_ + 1024 * mi + 128 * (a + 1)], x_t[:, 2048 * a + 512 * tc4:2048 * a + 512 * (tc4 + 1)], start=(a == 0), stop=(a == 7)),
                              wait=[wts[mi]] + proj_prev, inc=(1 if a == 7 else 0), tag=f"prj{pr}_{mi}_{tc4}") for a in range(8)]
                        if masked:
                            bias = bqs[:, pr:pr + 1] if mi == 0 else pv[:, PC_BK + pr:PC_BK + pr + 1]
                        else:
                            bias = 0.0
                        ev = p.on("scalar", lambda e, mi=mi, tc4=tc4, ps=ps, bias=bias: e.activation(qk[:, 2048 * mi + 512 * tc4:2048 * mi + 512 * (tc4 + 1)], ps[:, 0:512], AF.Identity, bias=bias),
                                  wait=[mm[-1]], inc=1, tag=f"prjev{pr}_{mi}_{tc4}")
                        proj_prev = [ev]
                        proj_last = ev
                vprev = [proj_last]
                vdone = []
                for tc in range(16):
                    ps = psA if tc % 2 == 0 else psB
                    mm = [p.on("tensor", lambda e, a=a, tc=tc, ps=ps, wo_=wo_: e.matmul(ps[:, 0:128], x_t[:, 2048 * a + 128 * tc:2048 * a + 128 * (tc + 1)], wk[:, wo_ + 2048 + 128 * a:wo_ + 2048 + 128 * (a + 1)], start=(a == 0), stop=(a == 7 and not masked)),
                          wait=[wts[2]] + vprev, inc=(1 if (a == 7 and not masked) else 0), tag=f"vprj{pr}_{tc}") for a in range(8)]
                    if masked:
                        mm.append(p.on("tensor", lambda e, tc=tc, ps=ps, pr=pr: e.matmul(ps[:, 0:128], onesr[:, :], bvr[:, 128 * pr:128 * (pr + 1)], start=False, stop=True),
                                       wait=[t_bvc], inc=1, tag=f"vbias{pr}_{tc}"))
                    ev1 = p.on("vector", lambda e, tc=tc, ps=ps: e.tensor_copy(vab[:, 65 * tc:65 * tc + 64], ps[:, 0:64]), wait=[mm[-1]], inc=1)
                    ev2 = p.on("vector", lambda e, tc=tc, ps=ps: e.tensor_copy(vab[:, 1088 + 65 * tc:1088 + 65 * tc + 64], ps[:, 64:128]), wait=[mm[-1]], inc=1)
                    vprev = [ev1, ev2]
                    vdone += [ev1, ev2]
                vready = t_von if False else vdone[-1]
                qkready = proj_last
                psab_now = vdone[-2:]

                if not masked:
                    sdot = [p.on("tensor", lambda e, j=j, pr=pr: e.matmul((psA if j < 2 else psB)[0:2, 512 * (j % 2):512 * (j % 2 + 1)], swb[:, 2 * pr:2 * pr + 2], qk[:, 512 * j:512 * (j + 1)], start=True, stop=True),
                            wait=[qkready, t_swb] + psab_now, inc=(1 if j in (1, 3) else 0), tag=f"sdot{pr}") for j in range(4)]
                    e1 = p.on("scalar", lambda e: e.activation(s2f[:, 0:1024], psA[0:2, :], AF.Exp, scale=cc[0:2, CC_NEG8:CC_NEG8 + 1]), wait=[sdot[1], sdot[3]], inc=1)
                    e2 = p.on("scalar", lambda e: e.activation(s2f[:, 1024:2048], psB[0:2, :], AF.Exp, scale=cc[0:2, CC_NEG8:CC_NEG8 + 1]), wait=[sdot[1], sdot[3]], inc=1)
                    a1 = p.on("vector", lambda e: e.tensor_scalar_add(s2f[:, :], s2f[:, :], 1.0), wait=[e1, e2], inc=1)
                    r1 = p.on("vector", lambda e: e.reciprocal(s2b_[:, :], s2f[:, :]), wait=[a1], inc=1, lowp=True)
                    sbm = [p.on("tensor", lambda e, j=j: e.matmul((psA if j < 2 else psB)[:, 512 * (j % 2):512 * (j % 2 + 1)], sel22[:, :], s2b_[:, 512 * j:512 * (j + 1)], start=True, stop=True),
                           wait=[r1, e1, e2], inc=(1 if j in (1, 3) else 0), tag=f"sbc{pr}") for j in range(4)]
                    c1 = p.on("vector", lambda e: e.tensor_copy(sh1[:, 0:1024], psA[:, :]), wait=[sbm[1], sbm[3]], inc=1)
                    c2 = p.on("vector", lambda e: e.tensor_copy(sh1[:, 1024:2048], psB[:, :]), wait=[sbm[1], sbm[3]], inc=1)
                    qkready = p.on("vector", lambda e: e.scalar_tensor_tensor(qk[:, 0:2048], qk[:, 0:2048], 2.0, sh1[:, :], ALU.mult, ALU.mult), wait=[c1, c2], inc=1, tag=f"qs{pr}")
                    psab_now = [c1, c2]

                recip_rows = []
                head_ev = []
                psc_prev = st_ctx["psc"]
                for h2 in range(2):
                    h = 2 * pr + h2
                    rb = 64 * h2
                    vo = 1088 * h2
                    exp_tok = {}
                    v_tok = {}
                    mln_ld = {}
                    mmM_tok = {}
                    if masked:
                        for kc in range(2):
                            mln_ld[kc] = p.on("sync", lambda e, kc=kc, h=h: e.dma_start(out=mz[:, 6144 + 2048 * (kc % 2):6144 + 2048 * (kc % 2 + 1)], in_=mln_d[h, 128 * kc:128 * (kc + 1), :]),
                                              wait=[mask_chunk_tok[(h, kc)], qkready] + st_ctx["mbf"], inc=16, tag=f"mld{h}_{kc}", sem="ld_m")
                    for kc in range(16):
                        for half in range(2):
                            ps = psA if half == 0 else psB
                            w0 = [qkready, vready, t_von, exp_tok.get((kc - 1, half))] + (psab_now if kc == 0 else [])
                            smm = [p.on("tensor", lambda e, kc=kc, half=half, j=j, ps=ps, rb=rb: e.matmul(ps[:, 512 * j:512 * (j + 1)], qk[rb:rb + 64, 2048 + 128 * kc:2048 + 128 * (kc + 1)], qk[rb:rb + 64, 1024 * half + 512 * j:1024 * half + 512 * (j + 1)], start=True, stop=(not masked)),
                                   wait=w0, inc=(0 if masked else (1 if j == 1 else 0)), tag=f"smm{h}_{kc}_{half}") for j in range(2)]
                            if masked:
                                for j in range(2):
                                    smm.append(p.on("tensor", lambda e, kc=kc, half=half, j=j, ps=ps: e.matmul(ps[:, 512 * j:512 * (j + 1)], identb[:, :], mz[:, 6144 + 2048 * (kc % 2) + 1024 * half + 512 * j:6144 + 2048 * (kc % 2) + 1024 * half + 512 * (j + 1)], start=False, stop=True),
                                                   wait=[mln_ld[kc]], inc=(1 if j == 1 else 0), tag=f"mmm{h}_{kc}_{half}"))
                                mmM_tok[(kc, half)] = smm[-1]
                            et = p.on("scalar", lambda e, half=half, ps=ps, masked=masked: e.activation(pb[:, 1024 * half:1024 * (half + 1)], ps[:, :], AF.Exp, scale=(cc[:, CC_03:CC_03 + 1] if masked else 1.0)),
                                      wait=[smm[-1], v_tok.get((kc - 1, half))], inc=1, tag=f"exp{h}_{kc}_{half}")
                            exp_tok[(kc, half)] = et
                            vmm = [p.on("tensor", lambda e, kc=kc, half=half, j=j, vo=vo: e.matmul(psC[0:65, 1024 * half + 512 * j:1024 * half + 512 * (j + 1)], vab[:, vo + 65 * kc:vo + 65 * (kc + 1)], pb[:, 1024 * half + 512 * j:1024 * half + 512 * (j + 1)], start=(kc == 0), stop=(kc == 15)),
                                   wait=[et] + (psc_prev if kc == 0 else []), inc=(1 if j == 1 else 0), tag=f"vmm2{h}_{kc}_{half}") for j in range(2)]
                            v_tok[(kc, half)] = vmm[-1]
                        if masked and kc + 2 < 16:
                            mln_ld[kc + 2] = p.on("sync", lambda e, kc=kc, h=h: e.dma_start(out=mz[:, 6144 + 2048 * (kc % 2):6144 + 2048 * (kc % 2 + 1)], in_=mln_d[h, 128 * (kc + 2):128 * (kc + 3), :]),
                                                  wait=[mask_chunk_tok[(h, kc + 2)], mmM_tok[(kc, 1)]], inc=16, tag=f"mld{h}_{kc + 2}", sem="ld_m")
                    if masked:
                        st_ctx["mbf"] = [mmM_tok[(14, 0)], mmM_tok[(14, 1)], mmM_tok[(15, 0)], mmM_tok[(15, 1)]]
                    vfin = [v_tok[(15, 0)], v_tok[(15, 1)]]
                    ev = p.on("vector", lambda e, pr=pr, rb=rb: e.tensor_copy(ao[rb:rb + 64, 2048 * pr:2048 * (pr + 1)], psC[0:64, :]), wait=vfin, inc=1, tag=f"aoev{h}")
                    rbt = rowb1 if h2 == 0 else sh1[0:1, 0:2048]
                    rc = None
                    for hf in range(2):
                        rc = p.on("vector", lambda e, hf=hf: e.reciprocal(psA[0:1, 0:1024], psC[64:65, 1024 * hf:1024 * (hf + 1)]), wait=vfin + [ev, rc, exp_tok[(15, 0)], exp_tok[(15, 1)]], inc=1, tag=f"recip{h}")
                        rc = p.on("vector", lambda e, hf=hf, rbt=rbt: e.tensor_copy(rbt[:, 1024 * hf:1024 * (hf + 1)], psA[0:1, 0:1024]), wait=[rc], inc=1)
                    rcb = rc
                    recip_rows.append(rcb)
                    head_ev.append(ev)
                    psc_prev = [rc, ev]
                    psab_now = [exp_tok[(15, 0)], exp_tok[(15, 1)], rcb]
                bmm = []
                for h2 in range(2):
                    sel = sel2a if h2 == 0 else sel2b
                    rbt = rowb1 if h2 == 0 else sh1[0:1, 0:2048]
                    for j in range(4):
                        bmm.append(p.on("tensor", lambda e, j=j, sel=sel, rbt=rbt, h2=h2: e.matmul(psC[:, 512 * j:512 * (j + 1)], sel[:, :], rbt[:, 512 * j:512 * (j + 1)], start=(h2 == 0), stop=(h2 == 1)),
                                        wait=recip_rows + psc_prev, inc=(1 if j == 3 else 0), tag=f"rbc{pr}_{h2}"))
                nm = None
                for hf in range(2):
                    eb = p.on("vector", lambda e, hf=hf: e.tensor_copy(sh2[:, 0:1024], psC[:, 1024 * hf:1024 * (hf + 1)]), wait=[t for t in bmm if t] + [nm], inc=1, tag=f"rbev{pr}_{hf}")
                    nm = p.on("vector", lambda e, pr=pr, hf=hf: e.tensor_mul(ao[:, 2048 * pr + 1024 * hf:2048 * pr + 1024 * (hf + 1)], ao[:, 2048 * pr + 1024 * hf:2048 * pr + 1024 * (hf + 1)], sh2[:, 0:1024]), wait=[eb] + head_ev, inc=1, tag=f"nrm{pr}_{hf}")
                eb = nm
                st_ctx["psc"] = [eb]
                st_ctx["psab"] = psab_now
                wbuf_free[par] = [nm]
                pair_done.append(nm)
            st_ctx["wstfree"] = prev_wcast[0]
            return pair_done


        def emit_projphase(s, W_d, src_t, res_t, dst_t, bias_base, waits, tag):
            wk_used = {0: st_ctx["wkfree"], 1: st_ctx["wkfree"]}
            prev_cast = st_ctx.get("wstfree")
            prev_ps = st_ctx["psab"]
            evs = []
            for c in range(8):
                par = c % 2
                tl = p.on("sync", lambda e, W_d=W_d, c=c: e.dma_start(out=wst[:, :].rearrange("p (a j) -> p a j", a=8), in_=W_d[:, 128 * c:128 * (c + 1)].rearrange("(a p) j -> p a j", p=128)),
                          wait=list(waits) + [prev_cast], inc=16, tag=f"{tag}wl{c}", sem="ld_w")
                tcst = p.on("gpsimd", lambda e, par=par: e.tensor_copy(wk[:, 1024 * par:1024 * (par + 1)], wst[:, :]),
                            wait=[tl, wk_used[par]], inc=1, tag=f"{tag}wc{c}")
                prev_cast = tcst
                last_mm = None
                for tc4 in range(4):
                    ps = psA if tc4 % 2 == 0 else psB
                    mm = [p.on("tensor", lambda e, a=a, c=c, tc4=tc4, ps=ps, par=par: e.matmul(ps[:, 0:512], wk[:, 1024 * par + 128 * a:1024 * par + 128 * (a + 1)], src_t[:, 2048 * a + 512 * tc4:2048 * a + 512 * (tc4 + 1)], start=(a == 0), stop=False),
                          wait=[tcst] + prev_ps + list(waits), inc=0, tag=f"{tag}mm{c}_{tc4}") for a in range(8)]
                    mm.append(p.on("tensor", lambda e, c=c, tc4=tc4, ps=ps: e.matmul(ps[:, 0:512], identb[:, :], res_t[:, 2048 * c + 512 * tc4:2048 * c + 512 * (tc4 + 1)], start=False, stop=True),
                                   wait=list(waits), inc=1, tag=f"{tag}res{c}_{tc4}"))
                    bias = pv[:, bias_base + c:bias_base + c + 1] if bias_base is not None else 0.0
                    ev = p.on("scalar", lambda e, c=c, tc4=tc4, ps=ps, bias=bias: e.activation(dst_t[:, 2048 * c + 512 * tc4:2048 * c + 512 * (tc4 + 1)], ps[:, 0:512], AF.Identity, bias=bias) if bias_base is not None else e.activation(dst_t[:, 2048 * c + 512 * tc4:2048 * c + 512 * (tc4 + 1)], ps[:, 0:512], AF.Copy),
                              wait=[mm[-1]], inc=1, tag=f"{tag}ev{c}_{tc4}")
                    prev_ps = [ev]
                    last_mm = mm[-1]
                    evs.append(ev)
                wk_used[par] = last_mm
            st_ctx["psab"] = prev_ps
            st_ctx["wkfree"] = last_mm
            st_ctx["wstfree"] = prev_cast
            return evs

        def emit_gate(s, waits):
            wk_used = {0: st_ctx["wkfree"], 1: st_ctx["wkfree"]}
            prev_cast = st_ctx.get("wstfree")
            prev_ps = st_ctx["psab"]
            done = []
            for c in range(8):
                par = c % 2
                tl = p.on("sync", lambda e, c=c: e.dma_start(out=wst[:, :].rearrange("p (a j) -> p a j", a=8), in_=gw[:, 128 * c:128 * (c + 1)].rearrange("(a p) j -> p a j", p=128)),
                          wait=list(waits) + [prev_cast], inc=16, tag=f"gw{c}", sem="ld_w")
                tcst = p.on("gpsimd", lambda e, par=par: e.tensor_copy(wk[:, 1024 * par:1024 * (par + 1)], wst[:, :]),
                            wait=[tl, wk_used[par]], inc=1)
                prev_cast = tcst
                last_mm = None
                for tc4 in range(4):
                    ps = psA if tc4 % 2 == 0 else psB
                    sl = 512 * (tc4 % 2)
                    mm = [p.on("tensor", lambda e, a=a, tc4=tc4, ps=ps, par=par: e.matmul(ps[:, 0:512], wk[:, 1024 * par + 128 * a:1024 * par + 128 * (a + 1)], ao[:, 2048 * a + 512 * tc4:2048 * a + 512 * (tc4 + 1)], start=(a == 0), stop=(a == 7)),
                          wait=[tcst] + prev_ps, inc=(1 if a == 7 else 0), tag=f"gmm{c}_{tc4}") for a in range(8)]
                    ex = p.on("scalar", lambda e, c=c, ps=ps, sl=sl: e.activation(sh1[:, sl:sl + 512], ps[:, 0:512], AF.Exp, scale=cc[:, CC_NEG1:CC_NEG1 + 1], bias=gbn[:, c:c + 1]),
                              wait=[mm[-1]] + done[-1:], inc=1, tag=f"gex{c}_{tc4}")
                    ad = p.on("vector", lambda e, sl=sl: e.tensor_scalar_add(sh1[:, sl:sl + 512], sh1[:, sl:sl + 512], 1.0), wait=[ex], inc=1)
                    rc = p.on("vector", lambda e, sl=sl: e.reciprocal(sh2[:, sl:sl + 512], sh1[:, sl:sl + 512]), wait=[ad], inc=1, lowp=True)
                    mu = p.on("vector", lambda e, c=c, tc4=tc4, sl=sl: e.tensor_mul(ao[:, 2048 * c + 512 * tc4:2048 * c + 512 * (tc4 + 1)], ao[:, 2048 * c + 512 * tc4:2048 * c + 512 * (tc4 + 1)], sh2[:, sl:sl + 512]),
                              wait=[rc], inc=1, tag=f"gmu{c}_{tc4}")
                    prev_ps = [ex]
                    done.append(mu)
                    last_mm = mm[-1]
                wk_used[par] = last_mm
            st_ctx["psab"] = prev_ps
            st_ctx["wkfree"] = last_mm
            st_ctx["wstfree"] = prev_cast
            return done[-4:]

        def emit_ln_feature(s, t_t, affine, waits, tag):
            prev_ps = st_ctx["psab"]
            prev_psc = st_ctx["psc"]
            prev_psab_ln = st_ctx["psab"]
            out_toks = []
            sq_prev = None
            for hf in range(2):
                mmy, mms_ = [], []
                sq_toks = []
                for c in range(8):
                    sq = p.on("scalar", lambda e, c=c, hf=hf: e.activation(sh1[:, 0:1024], t_t[:, 2048 * c + 1024 * hf:2048 * c + 1024 * (hf + 1)], AF.Square),
                              wait=list(waits) + [sq_prev] + (mms_[-1:] if mms_ else []), inc=1, tag=f"{tag}sq{c}")
                    sq_prev = sq
                    for j in range(2):
                        mmy.append(p.on("tensor", lambda e, c=c, j=j, hf=hf: e.matmul(psC[0:1, 512 * j:512 * (j + 1)], onesc[:, :], t_t[:, 2048 * c + 1024 * hf + 512 * j:2048 * c + 1024 * hf + 512 * (j + 1)], start=(c == 0), stop=(c == 7)),
                                   wait=list(waits) + prev_psc, inc=1, tag=f"{tag}my{c}"))
                        mms_.append(p.on("tensor", lambda e, c=c, j=j: e.matmul(psC[0:1, 1024 + 512 * j:1024 + 512 * (j + 1)], onesc[:, :], sh1[:, 512 * j:512 * (j + 1)], start=(c == 0), stop=(c == 7)),
                                    wait=[sq], inc=1, tag=f"{tag}ms{c}"))
                r1 = p.on("vector", lambda e: e.tensor_scalar_mul(psC[0:1, 0:1024], psC[0:1, 0:1024], 1.0 / 1024.0), wait=mmy[-2:] + mms_[-2:], inc=1)
                r2 = p.on("vector", lambda e: e.tensor_scalar_mul(psC[0:1, 1024:2048], psC[0:1, 1024:2048], 1.0 / 1024.0), wait=[r1], inc=1)
                cmu = p.on("vector", lambda e: e.tensor_copy(rowb2[:, 0:1024], psC[0:1, 0:1024]), wait=[r2], inc=1)
                r3 = p.on("vector", lambda e: e.tensor_mul(rowb1[:, 1024:2048], psC[0:1, 0:1024], rowb2[0:1, 0:1024]), wait=[cmu], inc=1)
                r4 = p.on("vector", lambda e: e.tensor_sub(psC[0:1, 1024:2048], psC[0:1, 1024:2048], rowb1[0:1, 1024:2048]), wait=[r3], inc=1)
                r5 = p.on("scalar", lambda e: e.activation(psC[0:1, 1024:2048], psC[0:1, 1024:2048], AF.Ln, bias=cc[0:1, CC_EPS5:CC_EPS5 + 1]), wait=[r4], inc=1)
                r6 = p.on("scalar", lambda e: e.activation(psC[0:1, 1024:2048], psC[0:1, 1024:2048], AF.Exp, scale=cc[0:1, CC_NEGH:CC_NEGH + 1]), wait=[r5], inc=1)
                c1 = p.on("vector", lambda e: e.tensor_copy(rowb1[:, 0:1024], psC[0:1, 1024:2048]), wait=[r6], inc=1)
                c2 = p.on("vector", lambda e: e.tensor_mul(rowb2[:, 0:1024], psC[0:1, 0:1024], rowb1[0:1, 0:1024]), wait=[c1], inc=1)
                bca = [p.on("tensor", lambda e, j=j: e.matmul(psC[:, 512 * j:512 * (j + 1)], onesr[:, :], rowb1[0:1, 512 * j:512 * (j + 1)], start=True, stop=True),
                       wait=[c2, r3], inc=(1 if j else 0), tag=f"{tag}ba") for j in range(2)]
                bcb = [p.on("tensor", lambda e, j=j: e.matmul(psC[:, 1024 + 512 * j:1024 + 512 * (j + 1)], onesr[:, :], rowb2[0:1, 512 * j:512 * (j + 1)], start=True, stop=True),
                       wait=[c2], inc=(1 if j else 0), tag=f"{tag}bb") for j in range(2)]
                e1 = p.on("vector", lambda e: e.tensor_copy(sh1[:, 0:1024], psC[:, 0:1024]), wait=[bca[1]], inc=1)
                e2 = p.on("vector", lambda e: e.tensor_copy(sh2[:, 0:1024], psC[:, 1024:2048]), wait=[bcb[1]], inc=1)
                ap_prev = [e1, e2]
                for c in range(8):
                    a1 = p.on("vector", lambda e, c=c, hf=hf: e.tensor_mul(t_t[:, 2048 * c + 1024 * hf:2048 * c + 1024 * (hf + 1)], t_t[:, 2048 * c + 1024 * hf:2048 * c + 1024 * (hf + 1)], sh1[:, 0:1024]),
                              wait=ap_prev, inc=1, tag=f"{tag}a1{c}")
                    a2 = p.on("vector", lambda e, c=c, hf=hf: e.tensor_sub(t_t[:, 2048 * c + 1024 * hf:2048 * c + 1024 * (hf + 1)], t_t[:, 2048 * c + 1024 * hf:2048 * c + 1024 * (hf + 1)], sh2[:, 0:1024]),
                              wait=[a1], inc=1)
                    tk = a2
                    if affine is not None:
                        gb_, bb_ = affine
                        tk = p.on("vector", lambda e, c=c, hf=hf, gb_=gb_, bb_=bb_: e.tensor_scalar(t_t[:, 2048 * c + 1024 * hf:2048 * c + 1024 * (hf + 1)], t_t[:, 2048 * c + 1024 * hf:2048 * c + 1024 * (hf + 1)], pv[:, gb_ + c:gb_ + c + 1], pv[:, bb_ + c:bb_ + c + 1], ALU.mult, ALU.add),
                                  wait=[a2], inc=1, tag=f"{tag}af{c}")
                    ap_prev = [tk]
                    out_toks.append(tk)
                prev_psc = [e1, e2]
                prev_psab_ln = [r4]
                sq_prev = out_toks[-1]
            st_ctx["psc"] = prev_psc
            return out_toks

        def emit_routing(s, waits):
            tl = p.on("sync", lambda e: e.dma_start(out=wst[:, 0:64].rearrange("p (a j) -> p a j", a=8), in_=mgw.rearrange("(a p) j -> p a j", p=128)),
                      wait=list(waits) + [st_ctx.get("wstfree")], inc=16, sem="ld_w")
            mgwb = wk[:, 0:64]
            tc_ = p.on("gpsimd", lambda e: e.tensor_copy(mgwb[:, :], wst[:, 0:64]), wait=[tl, st_ctx["wkfree"]], inc=1)
            st_ctx["wstfree"] = tc_
            prev_ps = st_ctx["psab"]
            toks = []
            for tc in range(16):
                mm = [p.on("tensor", lambda e, a=a, tc=tc: e.matmul(psB[:, 0:8], xA[:, 2048 * a + 128 * tc:2048 * a + 128 * (tc + 1)], mgwb[:, 8 * a:8 * (a + 1)], start=(a == 0), stop=False),
                      wait=[tc_, t_mgbc] + prev_ps + list(waits), inc=0, tag=f"rt{tc}") for a in range(8)]
                mm.append(p.on("tensor", lambda e, tc=tc: e.matmul(psB[:, 0:8], onesr[:, :], mgbr[:, :], start=False, stop=True), inc=1, tag=f"rtb{tc}"))
                cp = p.on("vector", lambda e: e.tensor_copy(lgt[:, :], psB[:, 0:8]), wait=[mm[-1]] + toks[-1:], inc=1)
                mx = p.on("vector", lambda e: e.max(v8t[:, :], lgt[:, :]), wait=[cp], inc=1)
                mi = p.on("vector", lambda e: e.max_index(i8t[:, :], v8t[:, :], lgt[:, :]), wait=[mx], inc=1)
                cf = p.on("vector", lambda e: e.tensor_copy(i8f[:, :], i8t[:, :]), wait=[mi], inc=1)
                ch = p.on("vector", lambda e, tc=tc: e.tensor_max(chosen[:, tc:tc + 1], i8f[:, 0:1], i8f[:, 1:2]), wait=[cf], inc=1, tag=f"ch{tc}")
                prev_ps = [cp]
                toks.append(ch)
            st_ctx["psab"] = prev_ps
            if "chosen" in tap_d:
                p.on("gpsimd", lambda e: e.dma_start(out=tap_d["chosen"][s], in_=chosen[:, :]), wait=toks, inc=16, sem="st")
            return toks

        def hh_ap(fc, lo, n, bank=0):
            if bank:
                return hhB[:, 512 * fc + lo:512 * fc + lo + n]
            if fc < 8:
                return qk[:, 512 * fc + lo:512 * fc + lo + n]
            if fc < 12:
                return pb[:, 512 * (fc - 8) + lo:512 * (fc - 8) + lo + n]
            return vab[:, 512 * (fc - 12) + lo:512 * (fc - 12) + lo + n]

        def emit_moe(s, waits, route_toks):
            z0 = p.on("vector", lambda e: e.memset(mz[:, 0:8192], 0.0), wait=list(waits), inc=1)
            z1 = p.on("vector", lambda e: e.memset(mz[:, 8192:16384], 0.0), wait=[z0], inc=1)
            acc_tok = {tc: z1 for tc in range(16)}
            mst = {"cast": st_ctx.get("wstfree"), "w1r": route_toks[-1], "w1s": {0: None, 1: None}, "w2": None,
                   "hh": {0: list(waits), 1: list(waits)}, "statcp": None, "stt": None, "apply": None}
            prev_ps = st_ctx["psab"] + st_ctx["psc"]
            last_ev = {0: None, 1: None}
            pend = {"w2": None}

            def flush_w2():
                if pend["w2"] is not None:
                    f = pend["w2"]
                    pend["w2"] = None
                    f()

            gidx = 0
            for ee in range(NE):
                flush_w2()
                res_cast = {}
                for jj in range(8):
                    tl = p.on("sync", lambda e, ee=ee, jj=jj: e.dma_start(out=wst[:, :].rearrange("p (a j) -> p a j", a=8), in_=w1d[ee, :, 128 * jj:128 * (jj + 1)].rearrange("(a p) f -> p a f", p=128)),
                              wait=[mst["cast"], mst["w1r"]], inc=16, tag=f"w1r{ee}_{jj}", sem="ld_w")
                    ct = p.on("gpsimd", lambda e, jj=jj: e.tensor_copy(wk[:, 1024 * jj:1024 * (jj + 1)], wst[:, :]), wait=[tl, mst["w1r"]], inc=1)
                    mst["cast"] = ct
                    res_cast[jj] = ct
                w2_cast = {}
                for fc in range(16):
                    tl = p.on("sync", lambda e, ee=ee, fc=fc: e.dma_start(out=wst[:, :], in_=w2d[ee, 128 * fc:128 * (fc + 1), :]),
                              wait=[mst["cast"], mst["w2"]], inc=16, tag=f"w2l{ee}_{fc}", sem="ld_w")
                    ct = p.on("gpsimd", lambda e, fc=fc: e.tensor_copy(ao[:, 1024 * fc:1024 * (fc + 1)], wst[:, :]), wait=[tl, mst["w2"]], inc=1)
                    mst["cast"] = ct
                    w2_cast[fc] = ct

                def stream_eighth(ei, ee):
                    par = ei % 2
                    toks = {}
                    for fco in range(2):
                        fc = 8 + 2 * ei + fco
                        tl = p.on("sync", lambda e, ee=ee, fc=fc: e.dma_start(out=wst[:, :].rearrange("p (a j) -> p a j", a=8), in_=w1d[ee, :, 128 * fc:128 * (fc + 1)].rearrange("(a p) f -> p a f", p=128)),
                                  wait=[mst["cast"], mst["w1s"][par]], inc=16, tag=f"w1s{ee}_{fc}", sem="ld_w")
                        ct = p.on("gpsimd", lambda e, par=par, fco=fco: e.tensor_copy(wk[:, 8192 + 2048 * par + 1024 * fco:8192 + 2048 * par + 1024 * (fco + 1)], wst[:, :]),
                                  wait=[tl, mst["w1s"][par]], inc=1)
                        mst["cast"] = ct
                        toks[fc] = ct
                    return toks

                for tc5 in range(4):
                    bank = gidx % 2
                    gidx += 1
                    scast = {}
                    scast.update(stream_eighth(0, ee))
                    scast.update(stream_eighth(1, ee))
                    sq_prev = None
                    my = ms = None
                    myt = {}
                    sqt_ = {}
                    evs = {}
                    for fc in range(16):
                        par = fc % 2
                        psl = psA[:, 512 * par:512 * (par + 1)]
                        if fc < 8:
                            base = 1024 * fc
                            wt = res_cast[fc]
                        else:
                            ei = (fc - 8) // 2
                            base = 8192 + 2048 * (ei % 2) + 1024 * ((fc - 8) % 2)
                            wt = scast[fc]
                        mm = [p.on("tensor", lambda e, a=a, base=base, tc5=tc5, psl=psl: e.matmul(psl, wk[:, base + 128 * a:base + 128 * (a + 1)], xA[:, 2048 * a + 512 * tc5:2048 * a + 512 * (tc5 + 1)], start=(a == 0), stop=(a == 7)),
                              wait=[wt, last_ev[par]] + (([mst["statcp"]] + prev_ps + list(waits)) if fc == 0 else []), inc=(1 if a == 7 else 0), tag=f"h{ee}_{tc5}_{fc}") for a in range(8)]
                        if fc == 9:
                            mst["w1s"][0] = mm[-1]
                            scast.update(stream_eighth(2, ee))
                        if fc == 11:
                            mst["w1s"][1] = mm[-1]
                            scast.update(stream_eighth(3, ee))
                        if fc == 13:
                            mst["w1s"][0] = mm[-1]
                        if fc == 15:
                            mst["w1s"][1] = mm[-1]
                        if fc == 7 and tc5 == 3:
                            mst["w1r"] = mm[-1]
                        ev = p.on("scalar", lambda e, fc=fc, ee=ee, psl=psl, bank=bank: e.activation(hh_ap(fc, 0, 512, bank), psl, AF.Identity, bias=pv[:, PC_B1 + 16 * ee + fc:PC_B1 + 16 * ee + fc + 1]),
                                  wait=[mm[-1]] + mst["hh"][bank], inc=1, tag=f"hev{ee}_{tc5}_{fc}")
                        evs[fc] = ev
                        last_ev[par] = ev
                        sq = p.on("vector", lambda e, fc=fc, bank=bank: e.tensor_mul(sh2[:, 0:512], hh_ap(fc, 0, 512, bank), hh_ap(fc, 0, 512, bank)), wait=[ev, sq_prev, ms], inc=1)
                        sqt_[fc] = sq
                        my = p.on("tensor", lambda e, fc=fc, bank=bank: e.matmul(psB[0:1, 0:512], onesc[:, :], hh_ap(fc, 0, 512, bank), start=(fc == 0), stop=(fc == 15)),
                                  wait=[ev] + ([mst["statcp"]] if fc == 0 else []), inc=1, tag=f"my{ee}_{tc5}_{fc}")
                        myt[fc] = my
                        ms = p.on("tensor", lambda e, fc=fc: e.matmul(psB[0:1, 512:1024], onesc[:, :], sh2[:, 0:512], start=(fc == 0), stop=(fc == 15)),
                                  wait=[sq], inc=1, tag=f"ms{ee}_{tc5}_{fc}")
                        sq_prev = sq
                    flush_w2()
                    r1 = p.on("vector", lambda e: e.tensor_scalar_mul(psB[0:1, 0:512], psB[0:1, 0:512], 1.0 / 2048.0), wait=[my, ms], inc=1)
                    r2 = p.on("vector", lambda e: e.tensor_scalar_mul(psB[0:1, 512:1024], psB[0:1, 512:1024], 1.0 / 2048.0), wait=[r1], inc=1)
                    cmu = p.on("vector", lambda e: e.tensor_copy(rowb1[:, 1024:1536], psB[0:1, 0:512]), wait=[r2], inc=1)
                    r3 = p.on("vector", lambda e: e.tensor_mul(rowb1[:, 1536:2048], psB[0:1, 0:512], rowb1[0:1, 1024:1536]), wait=[cmu], inc=1)
                    r4 = p.on("vector", lambda e: e.tensor_sub(psB[0:1, 512:1024], psB[0:1, 512:1024], rowb1[0:1, 1536:2048]), wait=[r3], inc=1)
                    r5 = p.on("scalar", lambda e: e.activation(psB[0:1, 512:1024], psB[0:1, 512:1024], AF.Ln, bias=cc[0:1, CC_EPS5:CC_EPS5 + 1]), wait=[r4], inc=1)
                    r6 = p.on("scalar", lambda e: e.activation(psB[0:1, 512:1024], psB[0:1, 512:1024], AF.Exp, scale=cc[0:1, CC_NEGH:CC_NEGH + 1]), wait=[r5], inc=1)
                    c1 = p.on("vector", lambda e: e.tensor_copy(rowb1[:, 0:512], psB[0:1, 512:1024]), wait=[r6], inc=1)
                    c2 = p.on("vector", lambda e: e.tensor_mul(rowb1[:, 512:1024], psB[0:1, 0:512], rowb1[0:1, 0:512]), wait=[c1], inc=1)
                    ba = p.on("tensor", lambda e: e.matmul(psB[:, 0:512], onesr[:, :], rowb1[0:1, 0:512], start=True, stop=True), wait=[c2], inc=1, tag="moeba")
                    bb = p.on("tensor", lambda e: e.matmul(psB[:, 512:1024], onesr[:, :], rowb1[0:1, 512:1024], start=True, stop=True), wait=[c2], inc=1, tag="moebb")
                    e1 = p.on("vector", lambda e: e.tensor_copy(sh1[:, 0:512], psB[:, 0:512]), wait=[ba, mst["statcp"], mst["apply"]], inc=1)
                    e2 = p.on("vector", lambda e: e.tensor_copy(sh1[:, 512:1024], psB[:, 512:1024]), wait=[bb, e1], inc=1)
                    mst["statcp"] = e2
                    glu = {}
                    for fc in range(16):
                        a1 = p.on("vector", lambda e, fc=fc, bank=bank: e.tensor_mul(hh_ap(fc, 0, 512, bank), hh_ap(fc, 0, 512, bank), sh1[:, 0:512]), wait=[e1, e2, myt[fc], sqt_[fc]], inc=1)
                        a2 = p.on("vector", lambda e, fc=fc, bank=bank: e.tensor_sub(hh_ap(fc, 0, 512, bank), hh_ap(fc, 0, 512, bank), sh1[:, 512:1024]), wait=[a1], inc=1)
                        gl = p.on("scalar", lambda e, fc=fc, ee=ee, bank=bank: e.activation(hh_ap(fc, 0, 512, bank), hh_ap(fc, 0, 512, bank), AF.Gelu, scale=pv[:, PC_LNG + 16 * ee + fc:PC_LNG + 16 * ee + fc + 1], bias=pv[:, PC_LNB + 16 * ee + fc:PC_LNB + 16 * ee + fc + 1]),
                                  wait=[a2], inc=1, tag=f"gl{ee}_{tc5}_{fc}")
                        glu[fc] = gl
                        mst["apply"] = a2

                    def do_w2(ee=ee, tc5=tc5, bank=bank, glu=glu, w2_cast=w2_cast):
                        wmm = None
                        for tsub in range(4):
                            tcg = 4 * tc5 + tsub
                            par2 = tsub % 2
                            mc = p.on("vector", lambda e, tcg=tcg, ee=ee: e.tensor_single_scalar(colf[:, 1:2], chosen[:, tcg:tcg + 1], float(ee), ALU.is_equal),
                                      wait=list(route_toks[-1:]) + [mst["stt"]], inc=1)
                            mr = p.on("vector", lambda e, ee=ee: e.tensor_mul(colf[:, 2:3], colf[:, 1:2], pv[:, PC_RSC + ee:PC_RSC + ee + 1]), wait=[mc], inc=1)
                            for fc in range(16):
                                for dh in range(2):
                                    wmm = p.on("tensor", lambda e, fc=fc, dh=dh, tsub=tsub, par2=par2, bank=bank: e.matmul(psC[:, 1024 * par2 + 512 * dh:1024 * par2 + 512 * (dh + 1)], hh_ap(fc, 128 * tsub, 128, bank), ao[:, 1024 * fc + 512 * dh:1024 * fc + 512 * (dh + 1)], start=(fc == 0), stop=(fc == 15)),
                                              wait=[glu[fc], w2_cast[fc]] + ([acc_tok[(tcg - 2) % 16]] if fc == 0 else []), inc=(1 if (fc == 15 and dh == 1) else 0), tag=f"w2{ee}_{tcg}")
                            stt = p.on("vector", lambda e, tcg=tcg, par2=par2: e.scalar_tensor_tensor(mz[:, 1024 * tcg:1024 * (tcg + 1)], psC[:, 1024 * par2:1024 * (par2 + 1)], colf[:, 2:3], mz[:, 1024 * tcg:1024 * (tcg + 1)], ALU.mult, ALU.add),
                                      wait=[wmm, mr, acc_tok[tcg]], inc=1, tag=f"stt{ee}_{tcg}")
                            acc_tok[tcg] = stt
                            mst["stt"] = stt
                        mst["w2"] = wmm
                        mst["hh"][bank] = [wmm]
                    pend["w2"] = do_w2
            flush_w2()
            st_ctx["psab"] = [last_ev[0], last_ev[1]]
            st_ctx["psc"] = [mst["stt"]]
            st_ctx["wstfree"] = mst["cast"]
            st_ctx["wkfree"] = mst["w1r"]
            return acc_tok

        def emit_final(s, waits, acc_tok):
            psAb = psA[:, :].bitcast(BF16)
            psCb = psC[:, :].bitcast(BF16)
            prev_psab = st_ctx["psab"]
            prev_psc = st_ctx["psc"]
            tn1 = p.on("sync", lambda e: e.dma_start(out=wst[0:1, 0:D], in_=n2g_d.rearrange("(a d) -> a d", a=1)), wait=list(waits) + [st_ctx.get("wstfree")], inc=16)
            tn2 = p.on("gpsimd", lambda e: e.tensor_copy(rowb1[0:1, 0:1024], wst[0:1, 0:D]), wait=[tn1, acc_tok[15]] + list(waits), inc=1)
            bg = [p.on("tensor", lambda e, j=j: e.matmul(psC[:, 512 * j:512 * (j + 1)], onesr[:, :], rowb1[0:1, 512 * j:512 * (j + 1)], start=True, stop=True), wait=[tn2] + prev_psc, inc=1, tag="n2g") for j in range(2)]
            eg = p.on("vector", lambda e: e.tensor_copy(n2gbc[:, :], psC[:, 0:1024]), wait=bg + list(waits), inc=1)
            tn3 = p.on("sync", lambda e: e.dma_start(out=wst[0:1, 0:D], in_=n2b_d.rearrange("(a d) -> a d", a=1)), wait=[tn2], inc=16)
            tn4 = p.on("gpsimd", lambda e: e.tensor_copy(rowb1[0:1, 0:1024], wst[0:1, 0:D]), wait=[tn3, eg], inc=1)
            bb_ = [p.on("tensor", lambda e, j=j: e.matmul(psC[:, 1024 + 512 * j:1024 + 512 * (j + 1)], onesr[:, :], rowb1[0:1, 512 * j:512 * (j + 1)], start=True, stop=True), wait=[tn4], inc=1, tag="n2b") for j in range(2)]
            ebb = p.on("vector", lambda e: e.tensor_copy(n2bbc[:, :], psC[:, 1024:2048]), wait=bb_, inc=1)
            prev_psc = [eg, ebb]
            st_ctx["wstfree"] = tn4
            prev_evs = []
            st_prev = None
            out_toks = []
            for tcg in range(16):
                t3 = [p.on("tensor", lambda e, c=c, tcg=tcg: e.transpose(psAb[:, 128 * c:128 * (c + 1)], xA[:, 2048 * c + 128 * tcg:2048 * c + 128 * (tcg + 1)], identb[:, :]),
                      wait=list(waits) + prev_psab + prev_evs, inc=(1 if c == 7 else 0), tag=f"t3_{tcg}") for c in range(8)]
                t2 = [p.on("tensor", lambda e, c=c, tcg=tcg: e.transpose(psAb[:, 1024 + 128 * c:1024 + 128 * (c + 1)], xB[:, 2048 * c + 128 * tcg:2048 * c + 128 * (tcg + 1)], identb[:, :]),
                      wait=prev_evs + list(waits), inc=(1 if c == 7 else 0), tag=f"t2_{tcg}") for c in range(8)]
                ev3 = p.on("vector", lambda e: e.tensor_scalar_mul(x3tm[:, :], psAb[:, 0:1024], 2.0), wait=[t3[-1]], inc=1)
                ev2 = p.on("vector", lambda e: e.tensor_copy(x2tm[:, :], psAb[:, 1024:2048]), wait=[t2[-1]], inc=1)
                prev_evs = [ev3, ev2]
                cb = p.on("vector", lambda e, tcg=tcg: e.tensor_copy(chosb[:, 0:1], chosen[:, tcg:tcg + 1]), wait=[st_prev], inc=1)
                tch = p.on("tensor", lambda e: e.transpose(psCb[0:1, 0:128], chosb[:, 0:1], identb[:, :]), wait=[cb] + prev_psc, inc=1, tag=f"tch{tcg}")
                cr = p.on("vector", lambda e: e.tensor_copy(chrow[:, :], psCb[0:1, 0:128]), wait=[tch], inc=1)
                b8 = p.on("tensor", lambda e: e.matmul(psC[0:8, 512:640], onesr[0:1, 0:8], chrow[:, :], start=True, stop=True), wait=[cr], inc=1, tag=f"b8{tcg}")
                c8t = p.on("vector", lambda e: e.tensor_copy(ch8[:, :], psC[0:8, 512:640]), wait=[b8], inc=1)
                m8 = p.on("vector", lambda e: e.tensor_single_scalar(m8b[:, :], ch8[:, :], iota8[:, 0:1], ALU.is_equal), wait=[c8t], inc=1)
                bsel = [p.on("tensor", lambda e, dh=dh: e.matmul(psC[:, 1024 + 512 * dh:1024 + 512 * (dh + 1)], m8b[:, :], b2rs[:, 512 * dh:512 * (dh + 1)], start=True, stop=True),
                        wait=[m8, t_b2rs], inc=(1 if dh else 0), tag=f"bsel{tcg}") for dh in range(2)]
                zt1 = p.on("vector", lambda e, tcg=tcg: e.tensor_add(ztt[:, :], mz[:, 1024 * tcg:1024 * (tcg + 1)], x3tm[:, :]), wait=[ev3, acc_tok[tcg]], inc=1)
                zt2 = p.on("vector", lambda e: e.tensor_add(ztt[:, :], ztt[:, :], psC[:, 1024:2048]), wait=[zt1, bsel[-1]], inc=1)
                def ln_tok(src_tile, waitt, tagp):
                    sq = p.on("scalar", lambda e: e.activation(sqt[:, :], src_tile[:, :], AF.Square, accum_out=colf[:, 3:4]), wait=waitt, inc=1, tag=tagp + "sq")
                    idn = p.on("scalar", lambda e: e.activation(sqt[:, :], src_tile[:, :], AF.Identity, accum_out=colf[:, 4:5]), wait=[sq], inc=1)
                    m1_ = p.on("vector", lambda e: e.tensor_scalar_mul(colf[:, 5:6], colf[:, 4:5], 1.0 / 1024.0), wait=[idn], inc=1)
                    m2_ = p.on("vector", lambda e: e.tensor_mul(colf[:, 6:7], colf[:, 5:6], colf[:, 5:6]), wait=[m1_], inc=1)
                    v_ = p.on("vector", lambda e: e.scalar_tensor_tensor(colf[:, 6:7], colf[:, 3:4], 1.0 / 1024.0, colf[:, 6:7], ALU.mult, ALU.subtract), wait=[m2_], inc=1)
                    l_ = p.on("scalar", lambda e: e.activation(colf[:, 6:7], colf[:, 6:7], AF.Ln, bias=cc[:, CC_EPS5:CC_EPS5 + 1]), wait=[v_], inc=1)
                    r_ = p.on("scalar", lambda e: e.activation(colf[:, 6:7], colf[:, 6:7], AF.Exp, scale=cc[:, CC_NEGH:CC_NEGH + 1]), wait=[l_], inc=1)
                    b_ = p.on("vector", lambda e: e.tensor_mul(colf[:, 7:8], colf[:, 5:6], colf[:, 6:7]), wait=[r_], inc=1)
                    return b_
                b_ = ln_tok(ztt, [zt2], f"n2_{tcg}")
                f1 = p.on("vector", lambda e: e.tensor_scalar(f2t[:, :], ztt[:, :], colf[:, 6:7], colf[:, 7:8], ALU.mult, ALU.subtract), wait=[b_], inc=1)
                f2_ = p.on("vector", lambda e: e.tensor_mul(f2t[:, :], f2t[:, :], n2gbc[:, :]), wait=[f1, eg], inc=1)
                f3 = p.on("vector", lambda e: e.tensor_add(f2t[:, :], f2t[:, :], n2bbc[:, :]), wait=[f2_, ebb], inc=1)
                f4 = p.on("vector", lambda e: e.tensor_add(f2t[:, :], f2t[:, :], x2tm[:, :]), wait=[f3, ev2], inc=1)
                b2_ = ln_tok(f2t, [f4], f"fin_{tcg}")
                fo = p.on("vector", lambda e: e.tensor_scalar(ot[:, :], f2t[:, :], colf[:, 6:7], colf[:, 7:8], ALU.mult, ALU.subtract), wait=[b2_, st_prev, (acc_tok[15] if tcg == 0 else None)], inc=1, tag=f"fo{tcg}")
                st_prev = p.on("gpsimd", lambda e, s=s, tcg=tcg: e.dma_start(out=out_d[s, 128 * tcg:128 * (tcg + 1), :], in_=ot[:, :]), wait=[fo], inc=16, sem="st", tag=f"ost{tcg}")
                out_toks.append(st_prev)
                prev_psc = [zt2, c8t, cr]
            st_ctx["psab"] = prev_evs
            st_ctx["psc"] = prev_psc
            st_ctx["wstfree"] = out_toks[-1]
            return out_toks

        def emit_tap(name, s_, src_handle, waits):
            if name not in tap_d:
                return waits
            tprev = list(waits)
            for c in range(8):
                tprev = [p.on("gpsimd", lambda e, c=c, s_=s_, name=name: e.dma_start(out=tap_d[name][s_, 128 * c:128 * (c + 1), :], in_=src_handle[:, 2048 * c:2048 * (c + 1)]), wait=list(waits) + tprev, inc=16, sem="st")]
            return tprev

        if upto == "mask":
            xload0 = emit_xload(0, wld)
            for h in range(NH):
                emit_mask_head(h)
            p.on("sync", lambda e: e.nop(), wait=[p.tok("st"), p.tok("gpsimd")], inc=0)
        else:
            prev_fin = []
            for s_ in range(SEQ_PER_CORE):
                xl = emit_xload(s_, wld + prev_fin + [t_b2rs])
                if s_ == 0:
                    def hook(pr):
                        mask_state["psab"] = st_ctx["psab"]
                        emit_mask_head(2 * pr)
                        emit_mask_head(2 * pr + 1)
                        st_ctx["psab"] = mask_state["psab"]
                    ao1 = emit_attention(0, True, Wq, Wk, Wv, xA, xl, pre_pair_hook=hook)
                    wo_w = ao1[-1:] + mask_state["uvfree"] + [mask_state["ln_prev"]]
                else:
                    st_ctx["mbf"] = prev_fin
                    ao1 = emit_attention(s_, True, Wq, Wk, Wv, xA, xl)
                    wo_w = ao1[-1:]
                if upto == "ao1":
                    break
                evs = emit_projphase(s_, Wo, ao, xA, xB, PC_BO, wo_w, f"wo{s_}")
                ln1 = emit_ln_feature(s_, xB, None, evs[-1:], f"ln1_{s_}")
                ln1 = emit_tap("x2", s_, xB, ln1)
                if upto == "x2":
                    break
                at2 = emit_attention(s_, False, qw, kw, vw, xB, ln1[-1:] + ln1)
                gt = emit_gate(s_, at2[-1:])
                at2t = emit_tap("ao2", s_, ao, gt)
                evs2 = emit_projphase(s_, ow, ao, xB, xA, None, at2t[-1:] + gt[-1:], f"ow{s_}")
                lnn1 = emit_ln_feature(s_, xA, (PC_N1G, PC_N1B), evs2[-1:], f"lnn1_{s_}")
                lnn1 = emit_tap("x3", s_, xA, lnn1)
                if upto == "x3":
                    break
                rt = emit_routing(s_, lnn1[-1:] + lnn1)
                acc = emit_moe(s_, lnn1[-1:] + [rt[-1]], rt)
                fin = emit_final(s_, lnn1[-1:] + [rt[-1]], acc)
                prev_fin = p.barrier()
            p.on("sync", lambda e: e.nop(), wait=[p.tok("st"), p.tok("gpsimd"), p.tok("vector"), p.tok("scalar"), p.tok("tensor"), p.tok("sync"), p.tok("ld_w"), p.tok("ld_m"), p.tok("ld_sg")], inc=0)

        @blk.sync
        def _(eng):
            for fn in p.q["sync"]:
                fn(eng)
        @blk.tensor
        def _(eng):
            for fn in p.q["tensor"]:
                fn(eng)
        @blk.vector
        def _(eng):
            for fn in p.q["vector"]:
                fn(eng)
        @blk.scalar
        def _(eng):
            for fn in p.q["scalar"]:
                fn(eng)
        @blk.gpsimd
        def _(eng):
            for fn in p.q["gpsimd"]:
                fn(eng)
    return nc


def _pack_vec(i):
    pvv = np.zeros((128, PC_N), dtype=np.float32)
    def chunks(v):
        return np.ascontiguousarray(v.reshape(8, 128).T)
    pvv[:, PC_BQ:PC_BQ + 8] = chunks(np.asarray(i["L0_attn_bq"], np.float32))
    pvv[:, PC_BK:PC_BK + 8] = chunks(np.asarray(i["L0_attn_bk"], np.float32))
    pvv[:, PC_BO:PC_BO + 8] = chunks(np.asarray(i["L0_attn_bo"], np.float32))
    pvv[:, PC_GB:PC_GB + 8] = chunks(np.asarray(i["L0_ffn_gate_b"], np.float32))
    pvv[:, PC_N1G:PC_N1G + 8] = chunks(np.asarray(i["L0_ffn_n1_g"], np.float32))
    pvv[:, PC_N1B:PC_N1B + 8] = chunks(np.asarray(i["L0_ffn_n1_b"], np.float32))
    pvv[:, PC_TAU] = np.float32(i["L0_attn_temperature"])
    pvv[:, PC_THR] = np.float32(i["L0_attn_threshold"])
    pvv[0:16, PC_PHASE:PC_PHASE + 3] = np.asarray(i["L0_attn_phase"], np.float32)
    sw = np.asarray(i["L0_ffn_scale_w"], np.float32)
    for h in range(16):
        r = (h % 2) * 64
        pvv[r:r + 64, PC_SW + h] = sw[h]
    rsc = np.asarray(i["L0_ffn_res_scale"], np.float32)
    pvv[:, PC_RSC:PC_RSC + 8] = rsc[None, :]
    pvv[0:8, PC_RSCP] = rsc
    pvv[0:8, PC_IOTA] = np.arange(8, dtype=np.float32)
    b1 = np.asarray(i["L0_ffn_b1"], np.float32)
    lng = np.asarray(i["L0_ffn_ln_g"], np.float32)
    lnb = np.asarray(i["L0_ffn_ln_b"], np.float32)
    for e in range(8):
        pvv[:, PC_B1 + 16 * e:PC_B1 + 16 * (e + 1)] = np.ascontiguousarray(b1[e].reshape(16, 128).T)
        pvv[:, PC_LNG + 16 * e:PC_LNG + 16 * (e + 1)] = np.ascontiguousarray(lng[e].reshape(16, 128).T)
        pvv[:, PC_LNB + 16 * e:PC_LNB + 16 * (e + 1)] = np.ascontiguousarray(lnb[e].reshape(16, 128).T)
    return pvv


def make_in_maps(inputs):
    i = inputs
    x = np.asarray(i["L0_attn_x"], np.float32)
    xT = np.ascontiguousarray(x.reshape(NCORE, SEQ_PER_CORE, T, D).transpose(0, 1, 3, 2))
    pvv = _pack_vec(i)
    shared = {
        "Wq": np.asarray(i["L0_attn_Wq"], np.float32), "Wk": np.asarray(i["L0_attn_Wk"], np.float32),
        "Wv": np.asarray(i["L0_attn_Wv"], np.float32), "Wo": np.asarray(i["L0_attn_Wo"], np.float32),
        "qw": np.asarray(i["L0_ffn_q_w"], np.float32), "kw": np.asarray(i["L0_ffn_k_w"], np.float32),
        "vw": np.asarray(i["L0_ffn_v_w"], np.float32), "ow": np.asarray(i["L0_ffn_out_w"], np.float32),
        "gw": np.asarray(i["L0_ffn_gate_w"], np.float32),
        "mgw": np.asarray(i["L0_ffn_moe_gate_w"], np.float32),
        "w1": np.asarray(i["L0_ffn_w1"], np.float32), "w2": np.asarray(i["L0_ffn_w2"], np.float32),
        "b2": np.asarray(i["L0_ffn_b2"], np.float32),
        "bvr": np.asarray(i["L0_attn_bv"], np.float32),
        "n2g": np.asarray(i["L0_ffn_n2_g"], np.float32), "n2b": np.asarray(i["L0_ffn_n2_b"], np.float32),
        "mgb": np.asarray(i["L0_ffn_moe_gate_b"], np.float32),
        "pv": pvv,
    }
    return [dict(shared, xT=xT[c]) for c in range(NCORE)]


def kernel(**inputs):
    in_maps = make_in_maps(inputs)
    nc = bass.Bass("TRN2", target_bir_lowering=False)
    build_core(nc)
    res = run_bass_kernel_spmd(nc, in_maps, core_ids=list(range(NCORE)))
    outs = [np.asarray(r["out"], np.float32) for r in res.results]
    return np.concatenate(outs, axis=0).reshape(16, T, D)


# version 04
# speedup vs baseline: 1.0043x; 1.0043x over previous
import numpy as np
import ml_dtypes
from contextlib import ExitStack

import concourse.bass as bass
import concourse.mybir as mybir
from concourse.bass_utils import run_bass_kernel_spmd

F32 = mybir.dt.float32
BF16 = mybir.dt.bfloat16
U32 = mybir.dt.uint32
AF = mybir.ActivationFunctionType
ALU = mybir.AluOpType
AX = mybir.AxisListType

T = 2048
D = 1024
NH = 16
HD = 64
NE = 8
FF = 2048
NCORE = 8
SEQ_PER_CORE = 2

PC_BQ, PC_BK, PC_BO, PC_GB = 0, 8, 16, 24
PC_N1G, PC_N1B = 32, 40
PC_TAU, PC_THR, PC_PHASE = 48, 49, 50
PC_SW, PC_RSC, PC_RSCP = 53, 69, 77
PC_B1, PC_LNG, PC_LNB = 78, 206, 334
PC_IOTA = 462
PC_N = 463

CC_03, CC_EPS6, CC_ONE, CC_NEG1, CC_NEG8, CC_NEGH = 0, 1, 2, 3, 4, 5
CC_I2048, CC_I1024, CC_EPS5, CC_QS1, CC_QS2 = 6, 7, 8, 9, 10
CC_VALS = {CC_03: 0.3, CC_EPS6: 1e-6, CC_ONE: 1.0, CC_NEG1: -1.0, CC_NEG8: -8.0,
           CC_NEGH: -0.5, CC_I2048: 1.0 / 2048.0, CC_I1024: 1.0 / 1024.0,
           CC_EPS5: 1e-5, CC_QS1: 1.0 / 2.4, CC_QS2: 0.125}

ENGS = ("sync", "tensor", "vector", "scalar", "gpsimd")


class Prog:

    def __init__(self, nc, stack):
        self.nc = nc
        self.stack = stack
        self.q = {e: [] for e in ENGS}
        self.sem = {}
        self.kind = {}
        self.owner = {}
        self.cnt = {}
        for e in ENGS:
            self._mk(e, e, "dma" if e == "sync" else "comp")
        for name in ("ld_w", "ld_w0", "ld_w1", "ld_m", "ld_sg", "st"):
            self._mk(name, None, "dma")
        self.waited = {c: {} for c in ENGS}

    def _mk(self, name, eng, kind):
        self.sem[name] = self.stack.enter_context(self.nc.semaphore(f"s_{name}"))
        self.kind[name] = kind
        self.owner[name] = eng
        self.cnt[name] = 0

    def on(self, eng, fn, wait=(), inc=0, tag="", sem=None, lowp=False):
        sn = sem or eng
        w = []
        for tok in wait:
            if tok is None:
                continue
            pe, v = tok
            assert v <= self.cnt[pe], f"future token {tag}"
            if self.kind[pe] == "dma":
                v = self.cnt[pe]
            if self.waited[eng].get(pe, 0) >= v:
                continue
            self.waited[eng][pe] = v
            w.append((pe, v))
        tok_out = None
        if inc:
            self.cnt[sn] += inc
            tok_out = (sn, self.cnt[sn])
        semd = self.sem

        nc_ = self.nc

        def run(engobj, w=w, fn=fn, inc=inc, sn=sn, tag=tag, lowp=lowp):
            for pe, v in w:
                engobj.wait_ge(semd[pe], v)
            if lowp:
                with nc_.allow_low_precision(reason="bf16 sigmoid reciprocal; within measured tolerance"):
                    inst = fn(engobj)
            else:
                inst = fn(engobj)
            if inc:
                assert inst is not None, f"step {tag} returned no instruction"
                inst.then_inc(semd[sn], inc)
        self.q[eng].append(run)
        return tok_out

    def tok(self, sn):
        return (sn, self.cnt[sn])

    def barrier(self):
        toks = [self.tok(e) for e in ENGS] + [self.tok(x) for x in ("ld_w", "ld_w0", "ld_w1", "ld_m", "ld_sg", "st")]
        out = []
        for e in ENGS:
            out.append(self.on(e, lambda en: en.nop(), wait=toks, inc=1, tag="barrier"))
        return out


def _f2b(a):
    return np.asarray(a, dtype=np.float32).astype(ml_dtypes.bfloat16)


def _host_consts():
    c = {}
    pos = np.arange(T, dtype=np.float64) / T * 2 * np.pi
    bases = np.stack([np.sin(pos), np.cos(pos), np.sin(pos / 2), np.cos(pos / 2),
                      np.ones_like(pos)])
    c["bases"] = _f2b(bases)
    c["identb"] = _f2b(np.eye(128))
    c["identf"] = np.eye(128, dtype=np.float32)
    E = np.zeros((128, 128))
    for h in range(16):
        for i in range(6):
            E[16 * i + h, 8 * h + i] = 1.0
    c["E128"] = _f2b(E)
    c["onesr"] = _f2b(np.ones((1, 128)))
    c["onesc"] = _f2b(np.ones((128, 1)))
    s2a = np.zeros((1, 128)); s2a[0, 0:64] = 1.0
    s2b = np.zeros((1, 128)); s2b[0, 64:128] = 1.0
    c["sel2a"] = _f2b(s2a)
    c["sel2b"] = _f2b(s2b)
    c["sel22"] = _f2b(np.concatenate([s2a, s2b], axis=0))
    c["iota8"] = np.arange(8, dtype=np.float32).reshape(8, 1)

    S1, C1, SH, CH, ON = range(5)
    def ent(h, j):
        return 7 * h + j
    E_NEGT, E_ONE = ent(0, 6), ent(1, 6)
    one = ("one", 1.0)

    def a_of(spec, h):
        kind, sg = spec
        if kind == "one":
            return E_ONE, sg
        if kind == "negt":
            return E_NEGT, sg
        trig, j = kind
        return ent(h, j if trig == "s" else 3 + j), sg

    sin_ = lambda j, s=1.0: (("s", j), s)
    cos_ = lambda j, s=1.0: (("c", j), s)
    TQ1 = [(S1, cos_(0), C1, sin_(0)),
           (CH, cos_(2), SH, sin_(2, -1.0)),
           (SH, cos_(2), CH, sin_(2)),
           (SH, cos_(2), CH, sin_(2)),
           (CH, cos_(2), SH, sin_(2, -1.0)),
           (ON, one, ON, None)]
    TQ2 = [(ON, one, ON, None), (ON, one, ON, None), (ON, one, ON, None),
           (C1, cos_(0), S1, sin_(0, -1.0)),
           (C1, cos_(0), S1, sin_(0, -1.0)),
           (ON, ("negt", 1.0), ON, None)]
    TK1 = [(C1, cos_(1), S1, sin_(1, -1.0)),
           (S1, cos_(1), C1, sin_(1)),
           (S1, cos_(1), C1, sin_(1)),
           (CH, one, ON, None), (SH, one, ON, None), (ON, one, ON, None)]
    TK2 = [(ON, one, ON, None), (CH, one, ON, None), (SH, ("one", -1.0), ON, None),
           (ON, one, ON, None), (ON, one, ON, None), (ON, one, ON, None)]
    tiles = [TQ1, TQ2, TK1, TK2]
    SELB1 = np.zeros((5, 4 * 128))
    SELB2 = np.zeros((5, 4 * 128))
    PA = np.zeros((112, 8 * 128))
    for b, tl in enumerate(tiles):
        for pp in range(128):
            g, h = pp // 16, pp % 16
            if g >= 6:
                continue
            b1, al, b2, be = tl[g]
            SELB1[b1, 128 * b + pp] = 1.0
            idx, sg = a_of(al, h)
            PA[idx, 128 * (2 * b) + pp] = sg
            if be is not None:
                SELB2[b2, 128 * b + pp] = 1.0
                idx2, sg2 = a_of(be, h)
                PA[idx2, 128 * (2 * b + 1) + pp] = sg2
    c["SELB1"] = _f2b(SELB1)
    c["SELB2"] = _f2b(SELB2)
    c["PA"] = _f2b(PA)
    return c


_HC = _host_consts()


def build_core(nc, taps=(), upto="all"):
    def din(name, shape, dtype=F32):
        return nc.dram_tensor(name, list(shape), dtype, kind="ExternalInput").ap()

    xTd = din("xT", [SEQ_PER_CORE, D, T])
    Wq, Wk, Wv, Wo = (din(n, [D, D]) for n in ("Wq", "Wk", "Wv", "Wo"))
    qw, kw, vw, ow, gw = (din(n, [D, D]) for n in ("qw", "kw", "vw", "ow", "gw"))
    mgw = din("mgw", [D, NE])
    w1d = din("w1", [NE, D, FF])
    w2d = din("w2", [NE, FF, D])
    b2d = din("b2", [NE, D])
    bvr_d = din("bvr", [D])
    n2g_d = din("n2g", [D])
    n2b_d = din("n2b", [D])
    mgb_d = din("mgb", [NE])
    pvd = din("pv", [128, PC_N])
    out_d = nc.dram_tensor("out", [SEQ_PER_CORE, T, D], F32, kind="ExternalOutput").ap()

    mk_kind = "ExternalOutput" if "mln" in taps else "Internal"
    mln_d = nc.dram_tensor("mln", [NH, T, T], BF16, kind=mk_kind).ap()
    sig_d = nc.dram_tensor("sig", [2, T, T], BF16, kind="Internal").ap()
    scb_d = nc.dram_tensor("scb", [112], BF16, kind="Internal").ap()
    tap_d = {}
    for tname, shape, dt in (("x2", [SEQ_PER_CORE, D, T], BF16),
                             ("x3", [SEQ_PER_CORE, D, T], BF16),
                             ("ao1", [SEQ_PER_CORE, D, T], BF16),
                             ("ao2", [SEQ_PER_CORE, D, T], BF16),
                             ("chosen", [SEQ_PER_CORE, 128, 16], F32)):
        if tname in taps:
            tap_d[tname] = nc.dram_tensor("tap_" + tname, shape, dt, kind="ExternalOutput").ap()

    cst = {k: nc.inline_tensor(v, "c_" + k).ap() for k, v in _HC.items()}

    with ExitStack() as ctx:
        def sb(name, shape, dtype=BF16):
            return ctx.enter_context(nc.sbuf_tensor(name, list(shape), dtype))
        xA = sb("xA", [128, 16384])
        xB = sb("xB", [128, 16384])
        ao = sb("ao", [128, 16384])
        wk = sb("wk", [128, 12288])
        mz = sb("mz", [128, 16384])
        qk = sb("qk", [128, 4096])
        vab = sb("vab", [128, 2128])
        pb = sb("pb", [128, 2048])
        hhB = sb("hhB", [128, 8192])
        sh1 = sb("sh1", [128, 2048])
        sh2 = sb("sh2", [128, 1024])
        wst = sb("wst", [128, 1024], F32)
        pv = sb("pvt", [128, PC_N], F32)
        cc = sb("cc", [128, 16], F32)
        bvr = sb("bvrt", [1, D])
        mgbr = sb("mgbrt", [1, NE])
        n2gbc = sh1[:, 0:1024]
        n2bbc = sh1[:, 1024:2048]
        b2rs = sb("b2rst", [NE, D])
        rowb1 = sb("rowb1", [1, 2048])
        rowb2 = sb("rowb2", [1, 1024])
        sct = sb("sct", [16, 7], F32)
        sc112 = sb("sc112", [112, 1])
        scw = sb("scw", [16, 3], F32)
        ab8 = sb("ab8", [128, 8], F32)
        invt = sb("invt", [128, 1], F32)
        bqs = sb("bqs", [128, 8], F32)
        gbn = sb("gbn", [128, 8], F32)
        swb = sb("swb", [128, 16])
        lgt = sb("lgt", [128, 8], F32)
        v8t = sb("v8t", [128, 8], F32)
        i8t = sb("i8t", [128, 8], U32)
        i8f = sb("i8f", [128, 8], F32)
        chosen = sb("chosen", [128, 16], F32)
        sctb = lgt[0:16, 0:8].bitcast(BF16)[:, 0:7]
        iota8 = pv[0:8, PC_IOTA:PC_IOTA + 1]
        chosb = i8f[:, 0:8].bitcast(BF16)[:, 0:2]
        colf = sb("colf", [128, 12], F32)
        identb = sb("identb", [128, 128])
        E128 = sb("E128", [128, 128])
        onesr = sb("onesr", [1, 128])
        onesc = sb("onesc", [128, 1])
        sel2b = sb("sel2b", [1, 128])
        sel22 = sb("sel22", [2, 128])
        sel2a = sel22[0:1, 0:128]

        UT = xB[0:8, 0:2048]
        VT = xB[0:8, 2048:4096]
        ch8 = sh2[0:8, 0:256].bitcast(F32)
        chrow = sh2[0:1, 256:384]
        m8b = sh2[0:8, 384:512]
        x3tm = qk[:, 0:1024]
        x2tm = qk[:, 1024:2048]
        ztt = qk[:, 2048:3072]
        f2t = qk[:, 3072:4096]
        sqt = pb[:, 0:1024]
        ot = vab[:, 0:2048].bitcast(F32)
        s2f = sh1[0:2, 0:2048]
        s2b_ = s2f
        basesb = mz[0:5, 12288:14336]
        PAc = mz[0:112, 14336:15360]
        SELB1 = mz[0:5, 15360:15872]
        SELB2 = mz[0:5, 15872:16384]
        psA = ctx.enter_context(nc.psum_tensor("psA", [128, 1024], F32))
        psB = ctx.enter_context(nc.psum_tensor("psB", [128, 1024], F32))
        psC = ctx.enter_context(nc.psum_tensor("psC", [128, 2048], F32))

        p = Prog(nc, ctx)
        wpp = {0: None, 1: None, "n": 0}

        def wload2(src_fn, dst_fn, waits, scl=None, tag="wl"):
            toks = []
            for hf in (0, 1):
                par = wpp["n"] % 2
                wpp["n"] += 1
                tl = p.on("sync", lambda e, hf=hf, par=par: e.dma_start(out=wst[:, 512 * par:512 * (par + 1)], in_=src_fn(hf)),
                          wait=list(waits) + [wpp[par]], inc=16, sem="ld_w0" if par == 0 else "ld_w1", tag=tag + "d")
                if scl is None:
                    ct = p.on("vector", lambda e, hf=hf, par=par: e.tensor_copy(dst_fn(hf), wst[:, 512 * par:512 * (par + 1)]), wait=[tl], inc=1, tag=tag + "c")
                else:
                    ct = p.on("vector", lambda e, hf=hf, par=par, scl=scl: e.tensor_scalar_mul(dst_fn(hf), wst[:, 512 * par:512 * (par + 1)], scl), wait=[tl], inc=1, tag=tag + "c")
                wpp[par] = ct
                toks.append(ct)
            return toks
        blk = ctx.enter_context(nc.Block())
        mzsl = lambda o, n=2048: mz[:, o:o + n]

        ld = []
        for dst, src in ((pv, pvd), (identb, cst["identb"]),
                         (E128, cst["E128"]), (onesr, cst["onesr"]), (onesc, cst["onesc"]),
                         (sel2b, cst["sel2b"]), (sel22, cst["sel22"]),
                         (basesb, cst["bases"]),
                         (SELB1, cst["SELB1"]), (SELB2, cst["SELB2"]), (PAc, cst["PA"])):
            ld.append(p.on("sync", lambda e, d=dst, s=src: e.dma_start(out=d[:, :], in_=s),
                           inc=16, tag="ldc"))
        t_bv = p.on("sync", lambda e: e.dma_start(out=wst[0:1, 0:D], in_=bvr_d.rearrange("(a d) -> a d", a=1)), inc=16)
        t_bvc = p.on("gpsimd", lambda e: e.tensor_copy(bvr[:, :], wst[0:1, 0:D]), wait=[t_bv], inc=1)
        t_mgb = p.on("sync", lambda e: e.dma_start(out=wst[0:1, 0:NE], in_=mgb_d.rearrange("(a d) -> a d", a=1)), wait=[t_bvc], inc=16)
        t_mgbc = p.on("gpsimd", lambda e: e.tensor_copy(mgbr[:, :], wst[0:1, 0:NE]), wait=[t_mgb], inc=1)
        t_b2 = p.on("sync", lambda e: e.dma_start(out=wst[0:NE, 0:D], in_=b2d), wait=[t_mgbc], inc=16)
        for col, val in CC_VALS.items():
            p.on("vector", lambda e, c=col, v=val: e.memset(cc[:, c:c + 1], v), inc=1)
        t_cc = p.on("vector", lambda e: e.memset(cc[:, 15:16], 0.0), inc=1)
        wld = ld + [t_cc]

        t_b2rs = p.on("vector", lambda e: e.tensor_scalar_mul(b2rs[:, :], wst[0:NE, 0:D], pv[0:NE, PC_RSCP:PC_RSCP + 1]),
                      wait=wld + [t_b2], inc=1)
        wpp[0] = t_b2rs
        wpp[1] = t_b2rs
        t_bqs = p.on("vector", lambda e: e.tensor_scalar_mul(bqs[:, :], pv[:, PC_BQ:PC_BQ + 8], 1.0 / 2.4), wait=wld, inc=1)
        t_gbn = p.on("vector", lambda e: e.tensor_scalar_mul(gbn[:, :], pv[:, PC_GB:PC_GB + 8], -1.0), wait=wld, inc=1)
        t_swb = p.on("gpsimd", lambda e: e.tensor_copy(swb[:, :], pv[:, PC_SW:PC_SW + 16]), wait=wld, inc=1)
        t_i1 = p.on("vector", lambda e: e.tensor_scalar_add(colf[:, 0:1], pv[:, PC_TAU:PC_TAU + 1], 1e-6), wait=wld, inc=1)
        t_invt = p.on("vector", lambda e: e.reciprocal(invt[:, :], colf[:, 0:1]), wait=[t_i1], inc=1)

        psc_free = [t_b2rs]

        PI = float(np.pi)
        phs = pv[0:16, PC_PHASE:PC_PHASE + 3]
        t1 = p.on("vector", lambda e: e.tensor_single_scalar(scw[:, :], phs, PI, ALU.is_gt), wait=wld, inc=1)
        t2 = p.on("vector", lambda e: e.scalar_tensor_tensor(sct[:, 3:6], scw[:, :], -2 * PI, phs, ALU.mult, ALU.add), wait=[t1], inc=1)
        t3 = p.on("vector", lambda e: e.tensor_single_scalar(scw[:, :], sct[:, 3:6], -PI, ALU.is_lt), wait=[t2], inc=1)
        t4 = p.on("vector", lambda e: e.scalar_tensor_tensor(sct[:, 3:6], scw[:, :], 2 * PI, sct[:, 3:6], ALU.mult, ALU.add), wait=[t3], inc=1)
        t5 = p.on("vector", lambda e: e.tensor_scalar_add(scw[:, :], sct[:, 3:6], PI / 2), wait=[t4], inc=1)
        t6 = p.on("vector", lambda e: e.tensor_single_scalar(sct[:, 0:3], scw[:, :], PI, ALU.is_gt), wait=[t5], inc=1)
        t7 = p.on("vector", lambda e: e.scalar_tensor_tensor(scw[:, :], sct[:, 0:3], -2 * PI, scw[:, :], ALU.mult, ALU.add), wait=[t6], inc=1)
        t8 = p.on("scalar", lambda e: e.activation(sct[:, 3:6], sct[:, 3:6], AF.Sin), wait=[t7], inc=1)
        t9 = p.on("scalar", lambda e: e.activation(sct[:, 0:3], scw[:, :], AF.Sin), wait=[t8], inc=1)
        t10 = p.on("vector", lambda e: e.tensor_copy(scw[:, :], sct[:, 3:6]), wait=[t9], inc=1)
        t11 = p.on("vector", lambda e: e.tensor_copy(sct[:, 3:6], sct[:, 0:3]), wait=[t10], inc=1)
        t12 = p.on("vector", lambda e: e.tensor_copy(sct[:, 0:3], scw[:, :]), wait=[t11], inc=1)
        t13 = p.on("vector", lambda e: e.memset(sct[:, 6:7], 1.0), wait=[t12], inc=1)
        t14 = p.on("vector", lambda e: e.tensor_scalar_mul(sct[0:1, 6:7], pv[0:1, PC_THR:PC_THR + 1], -1.0), wait=[t13], inc=1)
        t14b = p.on("gpsimd", lambda e: e.tensor_copy(sctb[:, :], sct[:, :]), wait=[t14], inc=1)
        t15 = p.on("gpsimd", lambda e: e.dma_start(out=scb_d.rearrange("(h j) -> h j", j=7), in_=sctb[:, :]), wait=[t14b], inc=16, sem="st")
        t16 = p.on("sync", lambda e: e.dma_start(out=sc112[:, :], in_=scb_d.rearrange("(p o) -> p o", o=1)), wait=[t15], inc=16)
        tab = [p.on("tensor", lambda e, b=b: e.matmul(psB[:, b:b + 1], PAc[:, 128 * b:128 * (b + 1)], sc112[:, :], start=True, stop=True),
               wait=[t16] + wld, inc=1) for b in range(8)]
        t_ab = p.on("vector", lambda e: e.tensor_copy(ab8[:, :], psB[:, 0:8]), wait=tab, inc=1)

        t_prev = t_ab
        tdone = []
        for b in range(4):
            m1 = [p.on("tensor", lambda e, b=b, j=j: e.matmul((psA if j < 2 else psB)[:, 512 * (j % 2):512 * (j % 2 + 1)], SELB1[:, 128 * b:128 * (b + 1)], basesb[:, 512 * j:512 * (j + 1)], start=True, stop=True),
                  wait=[t_prev] + wld, inc=(1 if j in (1, 3) else 0), tag=f"selb1_{b}") for j in range(4)]
            c1 = p.on("vector", lambda e, b=b: e.tensor_scalar_mul(mzsl(8192)[:, 0:1024], psA[:, :], ab8[:, 2 * b:2 * b + 1]), wait=[m1[1], m1[3]], inc=1)
            c2 = p.on("vector", lambda e, b=b: e.tensor_scalar_mul(mzsl(8192)[:, 1024:2048], psB[:, :], ab8[:, 2 * b:2 * b + 1]), wait=[m1[1], m1[3]], inc=1)
            m2 = [p.on("tensor", lambda e, b=b, j=j: e.matmul((psA if j < 2 else psB)[:, 512 * (j % 2):512 * (j % 2 + 1)], SELB2[:, 128 * b:128 * (b + 1)], basesb[:, 512 * j:512 * (j + 1)], start=True, stop=True),
                  wait=[c1, c2], inc=(1 if j in (1, 3) else 0), tag=f"selb2_{b}") for j in range(4)]
            d1 = p.on("vector", lambda e, b=b: e.scalar_tensor_tensor(mzsl(2048 * b)[:, 0:1024], psA[:, :], ab8[:, 2 * b + 1:2 * b + 2], mzsl(8192)[:, 0:1024], ALU.mult, ALU.add), wait=[m2[1], m2[3]], inc=1)
            d2 = p.on("vector", lambda e, b=b: e.scalar_tensor_tensor(mzsl(2048 * b)[:, 1024:2048], psB[:, :], ab8[:, 2 * b + 1:2 * b + 2], mzsl(8192)[:, 1024:2048], ALU.mult, ALU.add), wait=[m2[1], m2[3]], inc=1)
            t_prev = d2
            tdone += [d1, d2]
        tU = p.on("vector", lambda e: e.tensor_mul(mzsl(0)[:, :], mzsl(0)[:, :], mzsl(2048)[:, :]), wait=tdone[0:4], inc=1)
        tU2 = p.on("vector", lambda e: e.tensor_scalar_mul(mzsl(0)[:, :], mzsl(0)[:, :], invt[:, :]), wait=[tU, t_invt], inc=1)
        tV = p.on("vector", lambda e: e.tensor_mul(mzsl(4096)[:, :], mzsl(4096)[:, :], mzsl(6144)[:, :]), wait=tdone[4:8], inc=1)
        psab_free = [tU2, tV]

        mask_chunk_tok = {}
        mask_state = {"uvfree": [tU2, tV], "psab": psab_free, "st_prev": None, "ln_prev": tdone[-1]}

        def emit_mask_head(h):
            sl = h % 2
            exu = [p.on("tensor", lambda e, j=j: e.matmul((psA if j < 2 else psB)[0:8, 512 * (j % 2):512 * (j % 2 + 1)], E128[:, 8 * h:8 * h + 8], mzsl(0)[:, 512 * j:512 * (j + 1)], start=True, stop=True),
                   wait=[tU2] + mask_state["uvfree"] + mask_state["psab"], inc=(1 if j in (1, 3) else 0), tag=f"exu{h}") for j in range(4)]
            cu1 = p.on("scalar", lambda e: e.activation(UT[:, 0:1024], psA[0:8, :], AF.Copy), wait=[exu[1], exu[3]], inc=1)
            cu2 = p.on("scalar", lambda e: e.activation(UT[:, 1024:2048], psB[0:8, :], AF.Copy), wait=[exu[1], exu[3]], inc=1)
            exv = [p.on("tensor", lambda e, j=j: e.matmul((psA if j < 2 else psB)[0:8, 512 * (j % 2):512 * (j % 2 + 1)], E128[:, 8 * h:8 * h + 8], mzsl(4096)[:, 512 * j:512 * (j + 1)], start=True, stop=True),
                   wait=[tV, cu1, cu2], inc=(1 if j in (1, 3) else 0), tag=f"exv{h}") for j in range(4)]
            cv1 = p.on("scalar", lambda e: e.activation(VT[:, 0:1024], psA[0:8, :], AF.Copy), wait=[exv[1], exv[3]], inc=1)
            cv2 = p.on("scalar", lambda e: e.activation(VT[:, 1024:2048], psB[0:8, :], AF.Copy), wait=[exv[1], exv[3]], inc=1)
            prev_sig = [cv1, cv2]
            st_prev = mask_state["st_prev"]
            sg_store = []
            for kc in range(16):
                ms = [p.on("tensor", lambda e, kc=kc, j=j: e.matmul((psA if j < 2 else psB)[:, 512 * (j % 2):512 * (j % 2 + 1)], VT[:, 128 * kc:128 * (kc + 1)], UT[:, 512 * j:512 * (j + 1)], start=True, stop=True),
                      wait=prev_sig, inc=(1 if j in (1, 3) else 0), tag=f"vmm{h}_{kc}") for j in range(4)]
                s1t = p.on("scalar", lambda e: e.activation(mzsl(10240)[:, 0:1024], psA[:, :], AF.Sigmoid), wait=[ms[1], ms[3], st_prev], inc=1)
                s2t = p.on("scalar", lambda e: e.activation(mzsl(10240)[:, 1024:2048], psB[:, :], AF.Sigmoid), wait=[ms[1], ms[3], st_prev], inc=1)
                st_prev = p.on("gpsimd", lambda e, kc=kc, sl=sl: e.dma_start(out=sig_d[sl, 128 * kc:128 * (kc + 1), :], in_=mzsl(10240)[:, :]), wait=[s1t, s2t], inc=16, tag=f"sgst{h}_{kc}", sem="st")
                sg_store.append(st_prev)
                prev_sig = [s1t, s2t]
            mask_state["uvfree"] = [ms[1], ms[3]]
            mask_state["psab"] = prev_sig
            mask_state["st_prev"] = st_prev
            ln_prev = mask_state["ln_prev"]
            lnld = {}
            for kc in range(2):
                lnld[kc] = p.on("sync", lambda e, kc=kc, sl=sl: e.dma_start(out=mzsl(12288 + 2048 * (kc % 2))[:, :], in_=sig_d[sl, 128 * kc:128 * (kc + 1), :]), wait=[sg_store[kc], ln_prev], inc=16, sem="ld_sg")
            for kc in range(16):
                lt = p.on("scalar", lambda e, kc=kc: e.activation(mzsl(2048)[:, :], mzsl(12288 + 2048 * (kc % 2))[:, :], AF.Ln, bias=cc[:, CC_EPS6:CC_EPS6 + 1]),
                          wait=[lnld[kc], ln_prev], inc=1, tag=f"ln{h}_{kc}")
                ln_prev = p.on("gpsimd", lambda e, kc=kc: e.dma_start(out=mln_d[h, 128 * kc:128 * (kc + 1), :], in_=mzsl(2048)[:, :]), wait=[lt], inc=16, tag=f"mlnst{h}_{kc}", sem="st")
                mask_chunk_tok[(h, kc)] = ln_prev
                if kc + 2 < 16:
                    lnld[kc + 2] = p.on("sync", lambda e, kc=kc, sl=sl: e.dma_start(out=mzsl(12288 + 2048 * (kc % 2))[:, :], in_=sig_d[sl, 128 * (kc + 2):128 * (kc + 3), :]), wait=[sg_store[kc + 2], lt], inc=16, sem="ld_sg")
            mask_state["ln_prev"] = ln_prev

        def emit_xload(s, waits):
            prev = list(waits)
            for c in range(8):
                for hlf in range(2):
                    tk = wload2(lambda hf, c=c, h=hlf, s=s: xTd[s, 128 * c:128 * (c + 1), 1024 * h + 512 * hf:1024 * h + 512 * (hf + 1)],
                                lambda hf, c=c, h=hlf: xA[:, 2048 * c + 1024 * h + 512 * hf:2048 * c + 1024 * h + 512 * (hf + 1)],
                                prev, tag="xld")
                    prev = tk[-1:]
            return prev

        st_ctx = {"psab": psab_free, "psc": psc_free, "mbf": [tV, tU2], "wkfree": None, "wstfree": None}

        def emit_attention(s, masked, Wq_d, Wk_d, Wv_d, x_t, waits, pre_pair_hook=None):
            qscale = (1.0 / 2.4) if masked else 0.125
            wbuf_free = {0: list(waits), 1: list(waits)}
            pair_done = []
            vset = None
            for tc in range(16):
                vset = p.on("vector", lambda e, tc=tc: e.memset(vab[:, 65 * tc + 64:65 * tc + 65], 1.0), wait=list(waits) + [vset], inc=1)
                vset = p.on("vector", lambda e, tc=tc: e.memset(vab[:, 1088 + 65 * tc + 64:1088 + 65 * tc + 65], 1.0), wait=[vset], inc=1)
            t_von = vset
            for pr in range(8):
                if pre_pair_hook is not None:
                    pre_pair_hook(pr)
                par = pr % 2
                wo_ = 3072 * par
                wts = []
                for wi, (Wd, scl) in enumerate(((Wq_d, qscale), (Wk_d, None), (Wv_d, None))):
                    tk = wload2(lambda hf, Wd=Wd, pr=pr: Wd[512 * hf:512 * (hf + 1), 128 * pr:128 * (pr + 1)].rearrange("(a p) j -> p a j", p=128),
                                lambda hf, wi=wi, wo_=wo_: wk[:, wo_ + 1024 * wi + 512 * hf:wo_ + 1024 * wi + 512 * (hf + 1)],
                                wbuf_free[par] + list(waits), scl=scl, tag=f"wld{pr}_{wi}")
                    wts.append(tk[-1])
                proj_prev = st_ctx["psab"]
                proj_last = None
                for mi in (0, 1):
                    for tc4 in range(4):
                        ps = psA if tc4 % 2 == 0 else psB
                        mm = [p.on("tensor", lambda e, a=a, mi=mi, tc4=tc4, ps=ps, wo_=wo_: e.matmul(ps[:, 0:512], wk[:, wo_ + 1024 * mi + 128 * a:wo_ + 1024 * mi + 128 * (a + 1)], x_t[:, 2048 * a + 512 * tc4:2048 * a + 512 * (tc4 + 1)], start=(a == 0), stop=(a == 7)),
                              wait=[wts[mi]] + proj_prev, inc=(1 if a == 7 else 0), tag=f"prj{pr}_{mi}_{tc4}") for a in range(8)]
                        if masked:
                            bias = bqs[:, pr:pr + 1] if mi == 0 else pv[:, PC_BK + pr:PC_BK + pr + 1]
                        else:
                            bias = 0.0
                        ev = p.on("scalar", lambda e, mi=mi, tc4=tc4, ps=ps, bias=bias: e.activation(qk[:, 2048 * mi + 512 * tc4:2048 * mi + 512 * (tc4 + 1)], ps[:, 0:512], AF.Identity, bias=bias),
                                  wait=[mm[-1]], inc=1, tag=f"prjev{pr}_{mi}_{tc4}")
                        proj_prev = [ev]
                        proj_last = ev
                vprev = [proj_last]
                vdone = []
                for tc in range(16):
                    ps = psA if tc % 2 == 0 else psB
                    mm = [p.on("tensor", lambda e, a=a, tc=tc, ps=ps, wo_=wo_: e.matmul(ps[:, 0:128], x_t[:, 2048 * a + 128 * tc:2048 * a + 128 * (tc + 1)], wk[:, wo_ + 2048 + 128 * a:wo_ + 2048 + 128 * (a + 1)], start=(a == 0), stop=(a == 7 and not masked)),
                          wait=[wts[2]] + vprev, inc=(1 if (a == 7 and not masked) else 0), tag=f"vprj{pr}_{tc}") for a in range(8)]
                    if masked:
                        mm.append(p.on("tensor", lambda e, tc=tc, ps=ps, pr=pr: e.matmul(ps[:, 0:128], onesr[:, :], bvr[:, 128 * pr:128 * (pr + 1)], start=False, stop=True),
                                       wait=[t_bvc], inc=1, tag=f"vbias{pr}_{tc}"))
                    ev1 = p.on("vector", lambda e, tc=tc, ps=ps: e.tensor_copy(vab[:, 65 * tc:65 * tc + 64], ps[:, 0:64]), wait=[mm[-1]], inc=1)
                    ev2 = p.on("vector", lambda e, tc=tc, ps=ps: e.tensor_copy(vab[:, 1088 + 65 * tc:1088 + 65 * tc + 64], ps[:, 64:128]), wait=[mm[-1]], inc=1)
                    vprev = [ev1, ev2]
                    vdone += [ev1, ev2]
                vready = t_von if False else vdone[-1]
                qkready = proj_last
                psab_now = vdone[-2:]

                if not masked:
                    sdot = [p.on("tensor", lambda e, j=j, pr=pr: e.matmul((psA if j < 2 else psB)[0:2, 512 * (j % 2):512 * (j % 2 + 1)], swb[:, 2 * pr:2 * pr + 2], qk[:, 512 * j:512 * (j + 1)], start=True, stop=True),
                            wait=[qkready, t_swb] + psab_now, inc=(1 if j in (1, 3) else 0), tag=f"sdot{pr}") for j in range(4)]
                    e1 = p.on("scalar", lambda e: e.activation(s2f[:, 0:1024], psA[0:2, :], AF.Exp, scale=cc[0:2, CC_NEG8:CC_NEG8 + 1]), wait=[sdot[1], sdot[3]], inc=1)
                    e2 = p.on("scalar", lambda e: e.activation(s2f[:, 1024:2048], psB[0:2, :], AF.Exp, scale=cc[0:2, CC_NEG8:CC_NEG8 + 1]), wait=[sdot[1], sdot[3]], inc=1)
                    a1 = p.on("vector", lambda e: e.tensor_scalar_add(s2f[:, :], s2f[:, :], 1.0), wait=[e1, e2], inc=1)
                    r1 = p.on("vector", lambda e: e.reciprocal(s2b_[:, :], s2f[:, :]), wait=[a1], inc=1, lowp=True)
                    sbm = [p.on("tensor", lambda e, j=j: e.matmul((psA if j < 2 else psB)[:, 512 * (j % 2):512 * (j % 2 + 1)], sel22[:, :], s2b_[:, 512 * j:512 * (j + 1)], start=True, stop=True),
                           wait=[r1, e1, e2], inc=(1 if j in (1, 3) else 0), tag=f"sbc{pr}") for j in range(4)]
                    c1 = p.on("vector", lambda e: e.tensor_copy(sh1[:, 0:1024], psA[:, :]), wait=[sbm[1], sbm[3]], inc=1)
                    c2 = p.on("vector", lambda e: e.tensor_copy(sh1[:, 1024:2048], psB[:, :]), wait=[sbm[1], sbm[3]], inc=1)
                    qkready = p.on("vector", lambda e: e.scalar_tensor_tensor(qk[:, 0:2048], qk[:, 0:2048], 2.0, sh1[:, :], ALU.mult, ALU.mult), wait=[c1, c2], inc=1, tag=f"qs{pr}")
                    psab_now = [c1, c2]

                recip_rows = []
                head_ev = []
                psc_prev = st_ctx["psc"]
                for h2 in range(2):
                    h = 2 * pr + h2
                    rb = 64 * h2
                    vo = 1088 * h2
                    exp_tok = {}
                    v_tok = {}
                    mln_ld = {}
                    mmM_tok = {}
                    if masked:
                        for kc in range(2):
                            mln_ld[kc] = p.on("sync", lambda e, kc=kc, h=h: e.dma_start(out=mz[:, 6144 + 2048 * (kc % 2):6144 + 2048 * (kc % 2 + 1)], in_=mln_d[h, 128 * kc:128 * (kc + 1), :]),
                                              wait=[mask_chunk_tok[(h, kc)], qkready] + st_ctx["mbf"], inc=16, tag=f"mld{h}_{kc}", sem="ld_m")
                    for kc in range(16):
                        for half in range(2):
                            ps = psA if half == 0 else psB
                            w0 = [qkready, vready, t_von, exp_tok.get((kc - 1, half))] + (psab_now if kc == 0 else [])
                            smm = [p.on("tensor", lambda e, kc=kc, half=half, j=j, ps=ps, rb=rb: e.matmul(ps[:, 512 * j:512 * (j + 1)], qk[rb:rb + 64, 2048 + 128 * kc:2048 + 128 * (kc + 1)], qk[rb:rb + 64, 1024 * half + 512 * j:1024 * half + 512 * (j + 1)], start=True, stop=(not masked)),
                                   wait=w0, inc=(0 if masked else (1 if j == 1 else 0)), tag=f"smm{h}_{kc}_{half}") for j in range(2)]
                            if masked:
                                for j in range(2):
                                    smm.append(p.on("tensor", lambda e, kc=kc, half=half, j=j, ps=ps: e.matmul(ps[:, 512 * j:512 * (j + 1)], identb[:, :], mz[:, 6144 + 2048 * (kc % 2) + 1024 * half + 512 * j:6144 + 2048 * (kc % 2) + 1024 * half + 512 * (j + 1)], start=False, stop=True),
                                                   wait=[mln_ld[kc]], inc=(1 if j == 1 else 0), tag=f"mmm{h}_{kc}_{half}"))
                                mmM_tok[(kc, half)] = smm[-1]
                            et = p.on("scalar", lambda e, half=half, ps=ps, masked=masked: e.activation(pb[:, 1024 * half:1024 * (half + 1)], ps[:, :], AF.Exp, scale=(cc[:, CC_03:CC_03 + 1] if masked else 1.0)),
                                      wait=[smm[-1], v_tok.get((kc - 1, half))], inc=1, tag=f"exp{h}_{kc}_{half}")
                            exp_tok[(kc, half)] = et
                            vmm = [p.on("tensor", lambda e, kc=kc, half=half, j=j, vo=vo: e.matmul(psC[0:65, 1024 * half + 512 * j:1024 * half + 512 * (j + 1)], vab[:, vo + 65 * kc:vo + 65 * (kc + 1)], pb[:, 1024 * half + 512 * j:1024 * half + 512 * (j + 1)], start=(kc == 0), stop=(kc == 15)),
                                   wait=[et] + (psc_prev if kc == 0 else []), inc=(1 if j == 1 else 0), tag=f"vmm2{h}_{kc}_{half}") for j in range(2)]
                            v_tok[(kc, half)] = vmm[-1]
                        if masked and kc + 2 < 16:
                            mln_ld[kc + 2] = p.on("sync", lambda e, kc=kc, h=h: e.dma_start(out=mz[:, 6144 + 2048 * (kc % 2):6144 + 2048 * (kc % 2 + 1)], in_=mln_d[h, 128 * (kc + 2):128 * (kc + 3), :]),
                                                  wait=[mask_chunk_tok[(h, kc + 2)], mmM_tok[(kc, 1)]], inc=16, tag=f"mld{h}_{kc + 2}", sem="ld_m")
                    if masked:
                        st_ctx["mbf"] = [mmM_tok[(14, 0)], mmM_tok[(14, 1)], mmM_tok[(15, 0)], mmM_tok[(15, 1)]]
                    vfin = [v_tok[(15, 0)], v_tok[(15, 1)]]
                    ev = p.on("vector", lambda e, pr=pr, rb=rb: e.tensor_copy(ao[rb:rb + 64, 2048 * pr:2048 * (pr + 1)], psC[0:64, :]), wait=vfin, inc=1, tag=f"aoev{h}")
                    rbt = rowb1 if h2 == 0 else sh1[0:1, 0:2048]
                    rc = None
                    for hf in range(2):
                        rc = p.on("vector", lambda e, hf=hf: e.reciprocal(psA[0:1, 0:1024], psC[64:65, 1024 * hf:1024 * (hf + 1)]), wait=vfin + [ev, rc, exp_tok[(15, 0)], exp_tok[(15, 1)]], inc=1, tag=f"recip{h}")
                        rc = p.on("vector", lambda e, hf=hf, rbt=rbt: e.tensor_copy(rbt[:, 1024 * hf:1024 * (hf + 1)], psA[0:1, 0:1024]), wait=[rc], inc=1)
                    rcb = rc
                    recip_rows.append(rcb)
                    head_ev.append(ev)
                    psc_prev = [rc, ev]
                    psab_now = [exp_tok[(15, 0)], exp_tok[(15, 1)], rcb]
                bmm = []
                for h2 in range(2):
                    sel = sel2a if h2 == 0 else sel2b
                    rbt = rowb1 if h2 == 0 else sh1[0:1, 0:2048]
                    for j in range(4):
                        bmm.append(p.on("tensor", lambda e, j=j, sel=sel, rbt=rbt, h2=h2: e.matmul(psC[:, 512 * j:512 * (j + 1)], sel[:, :], rbt[:, 512 * j:512 * (j + 1)], start=(h2 == 0), stop=(h2 == 1)),
                                        wait=recip_rows + psc_prev, inc=(1 if j == 3 else 0), tag=f"rbc{pr}_{h2}"))
                nm = None
                for hf in range(2):
                    eb = p.on("vector", lambda e, hf=hf: e.tensor_copy(sh2[:, 0:1024], psC[:, 1024 * hf:1024 * (hf + 1)]), wait=[t for t in bmm if t] + [nm], inc=1, tag=f"rbev{pr}_{hf}")
                    nm = p.on("vector", lambda e, pr=pr, hf=hf: e.tensor_mul(ao[:, 2048 * pr + 1024 * hf:2048 * pr + 1024 * (hf + 1)], ao[:, 2048 * pr + 1024 * hf:2048 * pr + 1024 * (hf + 1)], sh2[:, 0:1024]), wait=[eb] + head_ev, inc=1, tag=f"nrm{pr}_{hf}")
                eb = nm
                st_ctx["psc"] = [eb]
                st_ctx["psab"] = psab_now
                wbuf_free[par] = [nm]
                pair_done.append(nm)
            return pair_done


        def emit_projphase(s, W_d, src_t, res_t, dst_t, bias_base, waits, tag):
            wk_used = {0: st_ctx["wkfree"], 1: st_ctx["wkfree"]}
            prev_ps = st_ctx["psab"]
            evs = []
            for c in range(8):
                par = c % 2
                tcst = wload2(lambda hf, W_d=W_d, c=c: W_d[512 * hf:512 * (hf + 1), 128 * c:128 * (c + 1)].rearrange("(a p) j -> p a j", p=128),
                              lambda hf, par=par: wk[:, 1024 * par + 512 * hf:1024 * par + 512 * (hf + 1)],
                              list(waits) + [wk_used[par]], tag=f"{tag}wc{c}")[-1]
                last_mm = None
                for tc4 in range(4):
                    ps = psA if tc4 % 2 == 0 else psB
                    mm = [p.on("tensor", lambda e, a=a, c=c, tc4=tc4, ps=ps, par=par: e.matmul(ps[:, 0:512], wk[:, 1024 * par + 128 * a:1024 * par + 128 * (a + 1)], src_t[:, 2048 * a + 512 * tc4:2048 * a + 512 * (tc4 + 1)], start=(a == 0), stop=False),
                          wait=[tcst] + prev_ps + list(waits), inc=0, tag=f"{tag}mm{c}_{tc4}") for a in range(8)]
                    mm.append(p.on("tensor", lambda e, c=c, tc4=tc4, ps=ps: e.matmul(ps[:, 0:512], identb[:, :], res_t[:, 2048 * c + 512 * tc4:2048 * c + 512 * (tc4 + 1)], start=False, stop=True),
                                   wait=list(waits), inc=1, tag=f"{tag}res{c}_{tc4}"))
                    bias = pv[:, bias_base + c:bias_base + c + 1] if bias_base is not None else 0.0
                    ev = p.on("scalar", lambda e, c=c, tc4=tc4, ps=ps, bias=bias: e.activation(dst_t[:, 2048 * c + 512 * tc4:2048 * c + 512 * (tc4 + 1)], ps[:, 0:512], AF.Identity, bias=bias) if bias_base is not None else e.activation(dst_t[:, 2048 * c + 512 * tc4:2048 * c + 512 * (tc4 + 1)], ps[:, 0:512], AF.Copy),
                              wait=[mm[-1]], inc=1, tag=f"{tag}ev{c}_{tc4}")
                    prev_ps = [ev]
                    last_mm = mm[-1]
                    evs.append(ev)
                wk_used[par] = last_mm
            st_ctx["psab"] = prev_ps
            st_ctx["wkfree"] = last_mm
            return evs

        def emit_gate(s, waits):
            wk_used = {0: st_ctx["wkfree"], 1: st_ctx["wkfree"]}
            prev_ps = st_ctx["psab"]
            done = []
            for c in range(8):
                par = c % 2
                tcst = wload2(lambda hf, c=c: gw[512 * hf:512 * (hf + 1), 128 * c:128 * (c + 1)].rearrange("(a p) j -> p a j", p=128),
                              lambda hf, par=par: wk[:, 1024 * par + 512 * hf:1024 * par + 512 * (hf + 1)],
                              list(waits) + [wk_used[par]], tag=f"gw{c}")[-1]
                last_mm = None
                for tc4 in range(4):
                    ps = psA if tc4 % 2 == 0 else psB
                    sl = 512 * (tc4 % 2)
                    mm = [p.on("tensor", lambda e, a=a, tc4=tc4, ps=ps, par=par: e.matmul(ps[:, 0:512], wk[:, 1024 * par + 128 * a:1024 * par + 128 * (a + 1)], ao[:, 2048 * a + 512 * tc4:2048 * a + 512 * (tc4 + 1)], start=(a == 0), stop=(a == 7)),
                          wait=[tcst] + prev_ps, inc=(1 if a == 7 else 0), tag=f"gmm{c}_{tc4}") for a in range(8)]
                    ex = p.on("scalar", lambda e, c=c, ps=ps, sl=sl: e.activation(sh1[:, sl:sl + 512], ps[:, 0:512], AF.Exp, scale=cc[:, CC_NEG1:CC_NEG1 + 1], bias=gbn[:, c:c + 1]),
                              wait=[mm[-1]] + done[-1:], inc=1, tag=f"gex{c}_{tc4}")
                    ad = p.on("vector", lambda e, sl=sl: e.tensor_scalar_add(sh1[:, sl:sl + 512], sh1[:, sl:sl + 512], 1.0), wait=[ex], inc=1)
                    rc = p.on("vector", lambda e, sl=sl: e.reciprocal(sh2[:, sl:sl + 512], sh1[:, sl:sl + 512]), wait=[ad], inc=1, lowp=True)
                    mu = p.on("vector", lambda e, c=c, tc4=tc4, sl=sl: e.tensor_mul(ao[:, 2048 * c + 512 * tc4:2048 * c + 512 * (tc4 + 1)], ao[:, 2048 * c + 512 * tc4:2048 * c + 512 * (tc4 + 1)], sh2[:, sl:sl + 512]),
                              wait=[rc], inc=1, tag=f"gmu{c}_{tc4}")
                    prev_ps = [ex]
                    done.append(mu)
                    last_mm = mm[-1]
                wk_used[par] = last_mm
            st_ctx["psab"] = prev_ps
            st_ctx["wkfree"] = last_mm
            return done[-4:]

        def emit_ln_feature(s, t_t, affine, waits, tag):
            prev_ps = st_ctx["psab"]
            prev_psc = st_ctx["psc"]
            prev_psab_ln = st_ctx["psab"]
            out_toks = []
            sq_prev = None
            for hf in range(2):
                mmy, mms_ = [], []
                sq_toks = []
                for c in range(8):
                    sq = p.on("scalar", lambda e, c=c, hf=hf: e.activation(sh1[:, 0:1024], t_t[:, 2048 * c + 1024 * hf:2048 * c + 1024 * (hf + 1)], AF.Square),
                              wait=list(waits) + [sq_prev] + (mms_[-1:] if mms_ else []), inc=1, tag=f"{tag}sq{c}")
                    sq_prev = sq
                    for j in range(2):
                        mmy.append(p.on("tensor", lambda e, c=c, j=j, hf=hf: e.matmul(psC[0:1, 512 * j:512 * (j + 1)], onesc[:, :], t_t[:, 2048 * c + 1024 * hf + 512 * j:2048 * c + 1024 * hf + 512 * (j + 1)], start=(c == 0), stop=(c == 7)),
                                   wait=list(waits) + prev_psc, inc=1, tag=f"{tag}my{c}"))
                        mms_.append(p.on("tensor", lambda e, c=c, j=j: e.matmul(psC[0:1, 1024 + 512 * j:1024 + 512 * (j + 1)], onesc[:, :], sh1[:, 512 * j:512 * (j + 1)], start=(c == 0), stop=(c == 7)),
                                    wait=[sq], inc=1, tag=f"{tag}ms{c}"))
                r1 = p.on("vector", lambda e: e.tensor_scalar_mul(psC[0:1, 0:1024], psC[0:1, 0:1024], 1.0 / 1024.0), wait=mmy[-2:] + mms_[-2:], inc=1)
                r2 = p.on("vector", lambda e: e.tensor_scalar_mul(psC[0:1, 1024:2048], psC[0:1, 1024:2048], 1.0 / 1024.0), wait=[r1], inc=1)
                cmu = p.on("vector", lambda e: e.tensor_copy(rowb2[:, 0:1024], psC[0:1, 0:1024]), wait=[r2], inc=1)
                r3 = p.on("vector", lambda e: e.tensor_mul(rowb1[:, 1024:2048], psC[0:1, 0:1024], rowb2[0:1, 0:1024]), wait=[cmu], inc=1)
                r4 = p.on("vector", lambda e: e.tensor_sub(psC[0:1, 1024:2048], psC[0:1, 1024:2048], rowb1[0:1, 1024:2048]), wait=[r3], inc=1)
                r5 = p.on("scalar", lambda e: e.activation(psC[0:1, 1024:2048], psC[0:1, 1024:2048], AF.Ln, bias=cc[0:1, CC_EPS5:CC_EPS5 + 1]), wait=[r4], inc=1)
                r6 = p.on("scalar", lambda e: e.activation(psC[0:1, 1024:2048], psC[0:1, 1024:2048], AF.Exp, scale=cc[0:1, CC_NEGH:CC_NEGH + 1]), wait=[r5], inc=1)
                c1 = p.on("vector", lambda e: e.tensor_copy(rowb1[:, 0:1024], psC[0:1, 1024:2048]), wait=[r6], inc=1)
                c2 = p.on("vector", lambda e: e.tensor_mul(rowb2[:, 0:1024], psC[0:1, 0:1024], rowb1[0:1, 0:1024]), wait=[c1], inc=1)
                bca = [p.on("tensor", lambda e, j=j: e.matmul(psC[:, 512 * j:512 * (j + 1)], onesr[:, :], rowb1[0:1, 512 * j:512 * (j + 1)], start=True, stop=True),
                       wait=[c2, r3], inc=(1 if j else 0), tag=f"{tag}ba") for j in range(2)]
                bcb = [p.on("tensor", lambda e, j=j: e.matmul(psC[:, 1024 + 512 * j:1024 + 512 * (j + 1)], onesr[:, :], rowb2[0:1, 512 * j:512 * (j + 1)], start=True, stop=True),
                       wait=[c2], inc=(1 if j else 0), tag=f"{tag}bb") for j in range(2)]
                e1 = p.on("vector", lambda e: e.tensor_copy(sh1[:, 0:1024], psC[:, 0:1024]), wait=[bca[1]], inc=1)
                e2 = p.on("vector", lambda e: e.tensor_copy(sh2[:, 0:1024], psC[:, 1024:2048]), wait=[bcb[1]], inc=1)
                ap_prev = [e1, e2]
                for c in range(8):
                    a1 = p.on("vector", lambda e, c=c, hf=hf: e.tensor_mul(t_t[:, 2048 * c + 1024 * hf:2048 * c + 1024 * (hf + 1)], t_t[:, 2048 * c + 1024 * hf:2048 * c + 1024 * (hf + 1)], sh1[:, 0:1024]),
                              wait=ap_prev, inc=1, tag=f"{tag}a1{c}")
                    a2 = p.on("vector", lambda e, c=c, hf=hf: e.tensor_sub(t_t[:, 2048 * c + 1024 * hf:2048 * c + 1024 * (hf + 1)], t_t[:, 2048 * c + 1024 * hf:2048 * c + 1024 * (hf + 1)], sh2[:, 0:1024]),
                              wait=[a1], inc=1)
                    tk = a2
                    if affine is not None:
                        gb_, bb_ = affine
                        tk = p.on("vector", lambda e, c=c, hf=hf, gb_=gb_, bb_=bb_: e.tensor_scalar(t_t[:, 2048 * c + 1024 * hf:2048 * c + 1024 * (hf + 1)], t_t[:, 2048 * c + 1024 * hf:2048 * c + 1024 * (hf + 1)], pv[:, gb_ + c:gb_ + c + 1], pv[:, bb_ + c:bb_ + c + 1], ALU.mult, ALU.add),
                                  wait=[a2], inc=1, tag=f"{tag}af{c}")
                    ap_prev = [tk]
                    out_toks.append(tk)
                prev_psc = [e1, e2]
                prev_psab_ln = [r4]
                sq_prev = out_toks[-1]
            st_ctx["psc"] = prev_psc
            return out_toks

        def emit_routing(s, waits):
            mgwb = wk[:, 0:64]
            par_r = wpp["n"] % 2
            wpp["n"] += 1
            tl = p.on("sync", lambda e, par_r=par_r: e.dma_start(out=wst[:, 512 * par_r:512 * par_r + 64].rearrange("p (a j) -> p a j", a=8), in_=mgw.rearrange("(a p) j -> p a j", p=128)),
                      wait=list(waits) + [wpp[par_r]], inc=16, sem="ld_w0" if par_r == 0 else "ld_w1")
            tc_ = p.on("vector", lambda e, par_r=par_r: e.tensor_copy(mgwb[:, :], wst[:, 512 * par_r:512 * par_r + 64]), wait=[tl, st_ctx["wkfree"]], inc=1)
            wpp[par_r] = tc_
            prev_ps = st_ctx["psab"]
            toks = []
            for tc in range(16):
                mm = [p.on("tensor", lambda e, a=a, tc=tc: e.matmul(psB[:, 0:8], xA[:, 2048 * a + 128 * tc:2048 * a + 128 * (tc + 1)], mgwb[:, 8 * a:8 * (a + 1)], start=(a == 0), stop=False),
                      wait=[tc_, t_mgbc] + prev_ps + list(waits), inc=0, tag=f"rt{tc}") for a in range(8)]
                mm.append(p.on("tensor", lambda e, tc=tc: e.matmul(psB[:, 0:8], onesr[:, :], mgbr[:, :], start=False, stop=True), inc=1, tag=f"rtb{tc}"))
                cp = p.on("vector", lambda e: e.tensor_copy(lgt[:, :], psB[:, 0:8]), wait=[mm[-1]] + toks[-1:], inc=1)
                mx = p.on("vector", lambda e: e.max(v8t[:, :], lgt[:, :]), wait=[cp], inc=1)
                mi = p.on("vector", lambda e: e.max_index(i8t[:, :], v8t[:, :], lgt[:, :]), wait=[mx], inc=1)
                cf = p.on("vector", lambda e: e.tensor_copy(i8f[:, :], i8t[:, :]), wait=[mi], inc=1)
                ch = p.on("vector", lambda e, tc=tc: e.tensor_max(chosen[:, tc:tc + 1], i8f[:, 0:1], i8f[:, 1:2]), wait=[cf], inc=1, tag=f"ch{tc}")
                prev_ps = [cp]
                toks.append(ch)
            st_ctx["psab"] = prev_ps
            if "chosen" in tap_d:
                p.on("gpsimd", lambda e: e.dma_start(out=tap_d["chosen"][s], in_=chosen[:, :]), wait=toks, inc=16, sem="st")
            return toks

        def hh_ap(fc, lo, n, bank=0):
            if bank:
                return hhB[:, 512 * fc + lo:512 * fc + lo + n]
            if fc < 8:
                return qk[:, 512 * fc + lo:512 * fc + lo + n]
            if fc < 12:
                return pb[:, 512 * (fc - 8) + lo:512 * (fc - 8) + lo + n]
            return vab[:, 512 * (fc - 12) + lo:512 * (fc - 12) + lo + n]

        def emit_moe(s, waits, route_toks):
            z0 = p.on("vector", lambda e: e.memset(mz[:, 0:8192], 0.0), wait=list(waits), inc=1)
            z1 = p.on("vector", lambda e: e.memset(mz[:, 8192:16384], 0.0), wait=[z0], inc=1)
            acc_tok = {tc: z1 for tc in range(16)}
            mst = {"cast": None, "w1r": route_toks[-1], "w1s": {0: None, 1: None}, "w2": None,
                   "hh": {0: list(waits), 1: list(waits)}, "statcp": None, "stt": None, "apply": None}
            prev_ps = st_ctx["psab"] + st_ctx["psc"]
            last_ev = {0: None, 1: None}
            pend = {"w2": None}

            def flush_w2():
                if pend["w2"] is not None:
                    f = pend["w2"]
                    pend["w2"] = None
                    f()

            gidx = 0
            for ee in range(NE):
                flush_w2()
                res_cast = {}
                for jj in range(8):
                    tk = wload2(lambda hf, ee=ee, jj=jj: w1d[ee, 512 * hf:512 * (hf + 1), 128 * jj:128 * (jj + 1)].rearrange("(a p) f -> p a f", p=128),
                                lambda hf, jj=jj: wk[:, 1024 * jj + 512 * hf:1024 * jj + 512 * (hf + 1)],
                                [mst["w1r"]], tag=f"w1r{ee}_{jj}")
                    res_cast[jj] = tk[-1]
                w2_cast = {}
                for fc in range(16):
                    tk = wload2(lambda hf, ee=ee, fc=fc: w2d[ee, 128 * fc:128 * (fc + 1), 512 * hf:512 * (hf + 1)],
                                lambda hf, fc=fc: ao[:, 1024 * fc + 512 * hf:1024 * fc + 512 * (hf + 1)],
                                [mst["w2"]], tag=f"w2l{ee}_{fc}")
                    w2_cast[fc] = tk[-1]

                def stream_eighth(ei, ee):
                    par = ei % 2
                    toks = {}
                    for fco in range(2):
                        fc = 8 + 2 * ei + fco
                        tk = wload2(lambda hf, ee=ee, fc=fc: w1d[ee, 512 * hf:512 * (hf + 1), 128 * fc:128 * (fc + 1)].rearrange("(a p) f -> p a f", p=128),
                                    lambda hf, par=par, fco=fco: wk[:, 8192 + 2048 * par + 1024 * fco + 512 * hf:8192 + 2048 * par + 1024 * fco + 512 * (hf + 1)],
                                    [mst["w1s"][par]], tag=f"w1s{ee}_{fc}")
                        toks[fc] = tk[-1]
                    return toks

                for tc5 in range(4):
                    bank = gidx % 2
                    gidx += 1
                    scast = {}
                    scast.update(stream_eighth(0, ee))
                    scast.update(stream_eighth(1, ee))
                    sq_prev = None
                    my = ms = None
                    myt = {}
                    sqt_ = {}
                    evs = {}
                    for fc in range(16):
                        par = fc % 2
                        psl = psA[:, 512 * par:512 * (par + 1)]
                        if fc < 8:
                            base = 1024 * fc
                            wt = res_cast[fc]
                        else:
                            ei = (fc - 8) // 2
                            base = 8192 + 2048 * (ei % 2) + 1024 * ((fc - 8) % 2)
                            wt = scast[fc]
                        mm = [p.on("tensor", lambda e, a=a, base=base, tc5=tc5, psl=psl: e.matmul(psl, wk[:, base + 128 * a:base + 128 * (a + 1)], xA[:, 2048 * a + 512 * tc5:2048 * a + 512 * (tc5 + 1)], start=(a == 0), stop=(a == 7)),
                              wait=[wt, last_ev[par]] + ((prev_ps + list(waits)) if fc == 0 else []), inc=(1 if a == 7 else 0), tag=f"h{ee}_{tc5}_{fc}") for a in range(8)]
                        if fc == 9:
                            mst["w1s"][0] = mm[-1]
                            scast.update(stream_eighth(2, ee))
                        if fc == 11:
                            mst["w1s"][1] = mm[-1]
                            scast.update(stream_eighth(3, ee))
                        if fc == 13:
                            mst["w1s"][0] = mm[-1]
                        if fc == 15:
                            mst["w1s"][1] = mm[-1]
                        if fc == 7 and tc5 == 3:
                            mst["w1r"] = mm[-1]
                        ev = p.on("scalar", lambda e, fc=fc, ee=ee, psl=psl, bank=bank: e.activation(hh_ap(fc, 0, 512, bank), psl, AF.Identity, bias=pv[:, PC_B1 + 16 * ee + fc:PC_B1 + 16 * ee + fc + 1]),
                                  wait=[mm[-1]] + mst["hh"][bank], inc=1, tag=f"hev{ee}_{tc5}_{fc}")
                        evs[fc] = ev
                        last_ev[par] = ev
                        sq = p.on("vector", lambda e, fc=fc, bank=bank: e.tensor_mul(sh2[:, 0:512], hh_ap(fc, 0, 512, bank), hh_ap(fc, 0, 512, bank)), wait=[ev, sq_prev, ms], inc=1)
                        sqt_[fc] = sq
                        my = p.on("tensor", lambda e, fc=fc, bank=bank: e.matmul(psB[0:1, 0:512], onesc[:, :], hh_ap(fc, 0, 512, bank), start=(fc == 0), stop=(fc == 15)),
                                  wait=[ev] + ([mst["statcp"]] if fc == 0 else []), inc=1, tag=f"my{ee}_{tc5}_{fc}")
                        myt[fc] = my
                        ms = p.on("tensor", lambda e, fc=fc: e.matmul(psB[0:1, 512:1024], onesc[:, :], sh2[:, 0:512], start=(fc == 0), stop=(fc == 15)),
                                  wait=[sq], inc=1, tag=f"ms{ee}_{tc5}_{fc}")
                        sq_prev = sq
                    flush_w2()
                    r1 = p.on("vector", lambda e: e.tensor_scalar_mul(psB[0:1, 0:512], psB[0:1, 0:512], 1.0 / 2048.0), wait=[my, ms], inc=1)
                    r2 = p.on("vector", lambda e: e.tensor_scalar_mul(psB[0:1, 512:1024], psB[0:1, 512:1024], 1.0 / 2048.0), wait=[r1], inc=1)
                    cmu = p.on("vector", lambda e: e.tensor_copy(rowb1[:, 1024:1536], psB[0:1, 0:512]), wait=[r2], inc=1)
                    r3 = p.on("vector", lambda e: e.tensor_mul(rowb1[:, 1536:2048], psB[0:1, 0:512], rowb1[0:1, 1024:1536]), wait=[cmu], inc=1)
                    r4 = p.on("vector", lambda e: e.tensor_sub(psB[0:1, 512:1024], psB[0:1, 512:1024], rowb1[0:1, 1536:2048]), wait=[r3], inc=1)
                    r5 = p.on("scalar", lambda e: e.activation(psB[0:1, 512:1024], psB[0:1, 512:1024], AF.Ln, bias=cc[0:1, CC_EPS5:CC_EPS5 + 1]), wait=[r4], inc=1)
                    r6 = p.on("scalar", lambda e: e.activation(psB[0:1, 512:1024], psB[0:1, 512:1024], AF.Exp, scale=cc[0:1, CC_NEGH:CC_NEGH + 1]), wait=[r5], inc=1)
                    c1 = p.on("vector", lambda e: e.tensor_copy(rowb1[:, 0:512], psB[0:1, 512:1024]), wait=[r6], inc=1)
                    c2 = p.on("vector", lambda e: e.tensor_mul(rowb1[:, 512:1024], psB[0:1, 0:512], rowb1[0:1, 0:512]), wait=[c1], inc=1)
                    ba = p.on("tensor", lambda e: e.matmul(psB[:, 0:512], onesr[:, :], rowb1[0:1, 0:512], start=True, stop=True), wait=[c2], inc=1, tag="moeba")
                    bb = p.on("tensor", lambda e: e.matmul(psB[:, 512:1024], onesr[:, :], rowb1[0:1, 512:1024], start=True, stop=True), wait=[c2], inc=1, tag="moebb")
                    e1 = p.on("vector", lambda e: e.tensor_copy(sh1[:, 0:512], psB[:, 0:512]), wait=[ba, mst["statcp"], mst["apply"]], inc=1)
                    e2 = p.on("vector", lambda e: e.tensor_copy(sh1[:, 512:1024], psB[:, 512:1024]), wait=[bb, e1], inc=1)
                    mst["statcp"] = e2
                    glu = {}
                    for fc in range(16):
                        a1 = p.on("vector", lambda e, fc=fc, bank=bank: e.tensor_mul(hh_ap(fc, 0, 512, bank), hh_ap(fc, 0, 512, bank), sh1[:, 0:512]), wait=[e1, e2, myt[fc], sqt_[fc]], inc=1)
                        a2 = p.on("vector", lambda e, fc=fc, bank=bank: e.tensor_sub(hh_ap(fc, 0, 512, bank), hh_ap(fc, 0, 512, bank), sh1[:, 512:1024]), wait=[a1], inc=1)
                        gl = p.on("scalar", lambda e, fc=fc, ee=ee, bank=bank: e.activation(hh_ap(fc, 0, 512, bank), hh_ap(fc, 0, 512, bank), AF.Gelu, scale=pv[:, PC_LNG + 16 * ee + fc:PC_LNG + 16 * ee + fc + 1], bias=pv[:, PC_LNB + 16 * ee + fc:PC_LNB + 16 * ee + fc + 1]),
                                  wait=[a2], inc=1, tag=f"gl{ee}_{tc5}_{fc}")
                        glu[fc] = gl
                        mst["apply"] = a2

                    def do_w2(ee=ee, tc5=tc5, bank=bank, glu=glu, w2_cast=w2_cast):
                        wmm = None
                        for tsub in range(4):
                            tcg = 4 * tc5 + tsub
                            par2 = tsub % 2
                            mc = p.on("vector", lambda e, tcg=tcg, ee=ee: e.tensor_single_scalar(colf[:, 1:2], chosen[:, tcg:tcg + 1], float(ee), ALU.is_equal),
                                      wait=list(route_toks[-1:]) + [mst["stt"]], inc=1)
                            mr = p.on("vector", lambda e, ee=ee: e.tensor_mul(colf[:, 2:3], colf[:, 1:2], pv[:, PC_RSC + ee:PC_RSC + ee + 1]), wait=[mc], inc=1)
                            for fc in range(16):
                                for dh in range(2):
                                    wmm = p.on("tensor", lambda e, fc=fc, dh=dh, tsub=tsub, par2=par2, bank=bank: e.matmul(psC[:, 1024 * par2 + 512 * dh:1024 * par2 + 512 * (dh + 1)], hh_ap(fc, 128 * tsub, 128, bank), ao[:, 1024 * fc + 512 * dh:1024 * fc + 512 * (dh + 1)], start=(fc == 0), stop=(fc == 15)),
                                              wait=[glu[fc], w2_cast[fc]] + ([acc_tok[(tcg - 2) % 16]] if fc == 0 else []), inc=(1 if (fc == 15 and dh == 1) else 0), tag=f"w2{ee}_{tcg}")
                            stt = p.on("vector", lambda e, tcg=tcg, par2=par2: e.scalar_tensor_tensor(mz[:, 1024 * tcg:1024 * (tcg + 1)], psC[:, 1024 * par2:1024 * (par2 + 1)], colf[:, 2:3], mz[:, 1024 * tcg:1024 * (tcg + 1)], ALU.mult, ALU.add),
                                      wait=[wmm, mr, acc_tok[tcg]], inc=1, tag=f"stt{ee}_{tcg}")
                            acc_tok[tcg] = stt
                            mst["stt"] = stt
                        mst["w2"] = wmm
                        mst["hh"][bank] = [wmm]
                    pend["w2"] = do_w2
            flush_w2()
            st_ctx["psab"] = [last_ev[0], last_ev[1]]
            st_ctx["psc"] = [mst["stt"]]
            st_ctx["wkfree"] = mst["w1r"]
            return acc_tok

        def emit_final(s, waits, acc_tok):
            psAb = psA[:, :].bitcast(BF16)
            psCb = psC[:, :].bitcast(BF16)
            prev_psab = st_ctx["psab"]
            prev_psc = st_ctx["psc"]
            tn1 = p.on("sync", lambda e: e.dma_start(out=wst[0:1, 0:D], in_=n2g_d.rearrange("(a d) -> a d", a=1)), wait=list(waits) + [wpp[0], wpp[1]], inc=16)
            tn2 = p.on("gpsimd", lambda e: e.tensor_copy(rowb1[0:1, 0:1024], wst[0:1, 0:D]), wait=[tn1, acc_tok[15]] + list(waits), inc=1)
            bg = [p.on("tensor", lambda e, j=j: e.matmul(psC[:, 512 * j:512 * (j + 1)], onesr[:, :], rowb1[0:1, 512 * j:512 * (j + 1)], start=True, stop=True), wait=[tn2] + prev_psc, inc=1, tag="n2g") for j in range(2)]
            eg = p.on("vector", lambda e: e.tensor_copy(n2gbc[:, :], psC[:, 0:1024]), wait=bg + list(waits), inc=1)
            tn3 = p.on("sync", lambda e: e.dma_start(out=wst[0:1, 0:D], in_=n2b_d.rearrange("(a d) -> a d", a=1)), wait=[tn2], inc=16)
            tn4 = p.on("gpsimd", lambda e: e.tensor_copy(rowb1[0:1, 0:1024], wst[0:1, 0:D]), wait=[tn3, eg], inc=1)
            bb_ = [p.on("tensor", lambda e, j=j: e.matmul(psC[:, 1024 + 512 * j:1024 + 512 * (j + 1)], onesr[:, :], rowb1[0:1, 512 * j:512 * (j + 1)], start=True, stop=True), wait=[tn4], inc=1, tag="n2b") for j in range(2)]
            ebb = p.on("vector", lambda e: e.tensor_copy(n2bbc[:, :], psC[:, 1024:2048]), wait=bb_, inc=1)
            prev_psc = [eg, ebb]
            wpp[0] = tn4
            wpp[1] = tn4
            prev_evs = []
            st_prev = None
            out_toks = []
            for tcg in range(16):
                t3 = [p.on("tensor", lambda e, c=c, tcg=tcg: e.transpose(psAb[:, 128 * c:128 * (c + 1)], xA[:, 2048 * c + 128 * tcg:2048 * c + 128 * (tcg + 1)], identb[:, :]),
                      wait=list(waits) + prev_psab + prev_evs, inc=(1 if c == 7 else 0), tag=f"t3_{tcg}") for c in range(8)]
                t2 = [p.on("tensor", lambda e, c=c, tcg=tcg: e.transpose(psAb[:, 1024 + 128 * c:1024 + 128 * (c + 1)], xB[:, 2048 * c + 128 * tcg:2048 * c + 128 * (tcg + 1)], identb[:, :]),
                      wait=prev_evs + list(waits), inc=(1 if c == 7 else 0), tag=f"t2_{tcg}") for c in range(8)]
                ev3 = p.on("vector", lambda e: e.tensor_scalar_mul(x3tm[:, :], psAb[:, 0:1024], 2.0), wait=[t3[-1]], inc=1)
                ev2 = p.on("vector", lambda e: e.tensor_copy(x2tm[:, :], psAb[:, 1024:2048]), wait=[t2[-1]], inc=1)
                prev_evs = [ev3, ev2]
                cb = p.on("vector", lambda e, tcg=tcg: e.tensor_copy(chosb[:, 0:1], chosen[:, tcg:tcg + 1]), wait=[st_prev], inc=1)
                tch = p.on("tensor", lambda e: e.transpose(psCb[0:1, 0:128], chosb[:, 0:1], identb[:, :]), wait=[cb] + prev_psc, inc=1, tag=f"tch{tcg}")
                cr = p.on("vector", lambda e: e.tensor_copy(chrow[:, :], psCb[0:1, 0:128]), wait=[tch], inc=1)
                b8 = p.on("tensor", lambda e: e.matmul(psC[0:8, 512:640], onesr[0:1, 0:8], chrow[:, :], start=True, stop=True), wait=[cr], inc=1, tag=f"b8{tcg}")
                c8t = p.on("vector", lambda e: e.tensor_copy(ch8[:, :], psC[0:8, 512:640]), wait=[b8], inc=1)
                m8 = p.on("vector", lambda e: e.tensor_single_scalar(m8b[:, :], ch8[:, :], iota8[:, 0:1], ALU.is_equal), wait=[c8t], inc=1)
                bsel = [p.on("tensor", lambda e, dh=dh: e.matmul(psC[:, 1024 + 512 * dh:1024 + 512 * (dh + 1)], m8b[:, :], b2rs[:, 512 * dh:512 * (dh + 1)], start=True, stop=True),
                        wait=[m8, t_b2rs], inc=(1 if dh else 0), tag=f"bsel{tcg}") for dh in range(2)]
                zt1 = p.on("vector", lambda e, tcg=tcg: e.tensor_add(ztt[:, :], mz[:, 1024 * tcg:1024 * (tcg + 1)], x3tm[:, :]), wait=[ev3, acc_tok[tcg]], inc=1)
                zt2 = p.on("vector", lambda e: e.tensor_add(ztt[:, :], ztt[:, :], psC[:, 1024:2048]), wait=[zt1, bsel[-1]], inc=1)
                def ln_tok(src_tile, waitt, tagp):
                    sq = p.on("scalar", lambda e: e.activation(sqt[:, :], src_tile[:, :], AF.Square, accum_out=colf[:, 3:4]), wait=waitt, inc=1, tag=tagp + "sq")
                    idn = p.on("scalar", lambda e: e.activation(sqt[:, :], src_tile[:, :], AF.Identity, accum_out=colf[:, 4:5]), wait=[sq], inc=1)
                    m1_ = p.on("vector", lambda e: e.tensor_scalar_mul(colf[:, 5:6], colf[:, 4:5], 1.0 / 1024.0), wait=[idn], inc=1)
                    m2_ = p.on("vector", lambda e: e.tensor_mul(colf[:, 6:7], colf[:, 5:6], colf[:, 5:6]), wait=[m1_], inc=1)
                    v_ = p.on("vector", lambda e: e.scalar_tensor_tensor(colf[:, 6:7], colf[:, 3:4], 1.0 / 1024.0, colf[:, 6:7], ALU.mult, ALU.subtract), wait=[m2_], inc=1)
                    l_ = p.on("scalar", lambda e: e.activation(colf[:, 6:7], colf[:, 6:7], AF.Ln, bias=cc[:, CC_EPS5:CC_EPS5 + 1]), wait=[v_], inc=1)
                    r_ = p.on("scalar", lambda e: e.activation(colf[:, 6:7], colf[:, 6:7], AF.Exp, scale=cc[:, CC_NEGH:CC_NEGH + 1]), wait=[l_], inc=1)
                    b_ = p.on("vector", lambda e: e.tensor_mul(colf[:, 7:8], colf[:, 5:6], colf[:, 6:7]), wait=[r_], inc=1)
                    return b_
                b_ = ln_tok(ztt, [zt2], f"n2_{tcg}")
                f1 = p.on("vector", lambda e: e.tensor_scalar(f2t[:, :], ztt[:, :], colf[:, 6:7], colf[:, 7:8], ALU.mult, ALU.subtract), wait=[b_], inc=1)
                f2_ = p.on("vector", lambda e: e.tensor_mul(f2t[:, :], f2t[:, :], n2gbc[:, :]), wait=[f1, eg], inc=1)
                f3 = p.on("vector", lambda e: e.tensor_add(f2t[:, :], f2t[:, :], n2bbc[:, :]), wait=[f2_, ebb], inc=1)
                f4 = p.on("vector", lambda e: e.tensor_add(f2t[:, :], f2t[:, :], x2tm[:, :]), wait=[f3, ev2], inc=1)
                b2_ = ln_tok(f2t, [f4], f"fin_{tcg}")
                fo = p.on("vector", lambda e: e.tensor_scalar(ot[:, :], f2t[:, :], colf[:, 6:7], colf[:, 7:8], ALU.mult, ALU.subtract), wait=[b2_, st_prev, (acc_tok[15] if tcg == 0 else None)], inc=1, tag=f"fo{tcg}")
                st_prev = p.on("gpsimd", lambda e, s=s, tcg=tcg: e.dma_start(out=out_d[s, 128 * tcg:128 * (tcg + 1), :], in_=ot[:, :]), wait=[fo], inc=16, sem="st", tag=f"ost{tcg}")
                out_toks.append(st_prev)
                prev_psc = [zt2, c8t, cr]
            st_ctx["psab"] = prev_evs
            st_ctx["psc"] = prev_psc
            return out_toks

        def emit_tap(name, s_, src_handle, waits):
            if name not in tap_d:
                return waits
            tprev = list(waits)
            for c in range(8):
                tprev = [p.on("gpsimd", lambda e, c=c, s_=s_, name=name: e.dma_start(out=tap_d[name][s_, 128 * c:128 * (c + 1), :], in_=src_handle[:, 2048 * c:2048 * (c + 1)]), wait=list(waits) + tprev, inc=16, sem="st")]
            return tprev

        if upto == "mask":
            xload0 = emit_xload(0, wld)
            for h in range(NH):
                emit_mask_head(h)
            p.on("sync", lambda e: e.nop(), wait=[p.tok("st"), p.tok("gpsimd")], inc=0)
        else:
            prev_fin = []
            for s_ in range(SEQ_PER_CORE):
                xl = emit_xload(s_, wld + prev_fin + [t_b2rs])
                if s_ == 0:
                    def hook(pr):
                        mask_state["psab"] = st_ctx["psab"]
                        emit_mask_head(2 * pr)
                        emit_mask_head(2 * pr + 1)
                        st_ctx["psab"] = mask_state["psab"]
                    ao1 = emit_attention(0, True, Wq, Wk, Wv, xA, xl, pre_pair_hook=hook)
                    wo_w = ao1[-1:] + mask_state["uvfree"] + [mask_state["ln_prev"]]
                else:
                    st_ctx["mbf"] = prev_fin
                    ao1 = emit_attention(s_, True, Wq, Wk, Wv, xA, xl)
                    wo_w = ao1[-1:]
                if upto == "ao1":
                    break
                evs = emit_projphase(s_, Wo, ao, xA, xB, PC_BO, wo_w, f"wo{s_}")
                ln1 = emit_ln_feature(s_, xB, None, evs[-1:], f"ln1_{s_}")
                ln1 = emit_tap("x2", s_, xB, ln1)
                if upto == "x2":
                    break
                at2 = emit_attention(s_, False, qw, kw, vw, xB, ln1[-1:] + ln1)
                gt = emit_gate(s_, at2[-1:])
                at2t = emit_tap("ao2", s_, ao, gt)
                evs2 = emit_projphase(s_, ow, ao, xB, xA, None, at2t[-1:] + gt[-1:], f"ow{s_}")
                lnn1 = emit_ln_feature(s_, xA, (PC_N1G, PC_N1B), evs2[-1:], f"lnn1_{s_}")
                lnn1 = emit_tap("x3", s_, xA, lnn1)
                if upto == "x3":
                    break
                rt = emit_routing(s_, lnn1[-1:] + lnn1)
                acc = emit_moe(s_, lnn1[-1:] + [rt[-1]], rt)
                fin = emit_final(s_, lnn1[-1:] + [rt[-1]], acc)
                prev_fin = p.barrier()
            p.on("sync", lambda e: e.nop(), wait=[p.tok("st"), p.tok("gpsimd"), p.tok("vector"), p.tok("scalar"), p.tok("tensor"), p.tok("sync"), p.tok("ld_w"), p.tok("ld_w0"), p.tok("ld_w1"), p.tok("ld_m"), p.tok("ld_sg")], inc=0)

        @blk.sync
        def _(eng):
            for fn in p.q["sync"]:
                fn(eng)
        @blk.tensor
        def _(eng):
            for fn in p.q["tensor"]:
                fn(eng)
        @blk.vector
        def _(eng):
            for fn in p.q["vector"]:
                fn(eng)
        @blk.scalar
        def _(eng):
            for fn in p.q["scalar"]:
                fn(eng)
        @blk.gpsimd
        def _(eng):
            for fn in p.q["gpsimd"]:
                fn(eng)
    return nc


def _pack_vec(i):
    pvv = np.zeros((128, PC_N), dtype=np.float32)
    def chunks(v):
        return np.ascontiguousarray(v.reshape(8, 128).T)
    pvv[:, PC_BQ:PC_BQ + 8] = chunks(np.asarray(i["L0_attn_bq"], np.float32))
    pvv[:, PC_BK:PC_BK + 8] = chunks(np.asarray(i["L0_attn_bk"], np.float32))
    pvv[:, PC_BO:PC_BO + 8] = chunks(np.asarray(i["L0_attn_bo"], np.float32))
    pvv[:, PC_GB:PC_GB + 8] = chunks(np.asarray(i["L0_ffn_gate_b"], np.float32))
    pvv[:, PC_N1G:PC_N1G + 8] = chunks(np.asarray(i["L0_ffn_n1_g"], np.float32))
    pvv[:, PC_N1B:PC_N1B + 8] = chunks(np.asarray(i["L0_ffn_n1_b"], np.float32))
    pvv[:, PC_TAU] = np.float32(i["L0_attn_temperature"])
    pvv[:, PC_THR] = np.float32(i["L0_attn_threshold"])
    pvv[0:16, PC_PHASE:PC_PHASE + 3] = np.asarray(i["L0_attn_phase"], np.float32)
    sw = np.asarray(i["L0_ffn_scale_w"], np.float32)
    for h in range(16):
        r = (h % 2) * 64
        pvv[r:r + 64, PC_SW + h] = sw[h]
    rsc = np.asarray(i["L0_ffn_res_scale"], np.float32)
    pvv[:, PC_RSC:PC_RSC + 8] = rsc[None, :]
    pvv[0:8, PC_RSCP] = rsc
    pvv[0:8, PC_IOTA] = np.arange(8, dtype=np.float32)
    b1 = np.asarray(i["L0_ffn_b1"], np.float32)
    lng = np.asarray(i["L0_ffn_ln_g"], np.float32)
    lnb = np.asarray(i["L0_ffn_ln_b"], np.float32)
    for e in range(8):
        pvv[:, PC_B1 + 16 * e:PC_B1 + 16 * (e + 1)] = np.ascontiguousarray(b1[e].reshape(16, 128).T)
        pvv[:, PC_LNG + 16 * e:PC_LNG + 16 * (e + 1)] = np.ascontiguousarray(lng[e].reshape(16, 128).T)
        pvv[:, PC_LNB + 16 * e:PC_LNB + 16 * (e + 1)] = np.ascontiguousarray(lnb[e].reshape(16, 128).T)
    return pvv


def make_in_maps(inputs):
    i = inputs
    x = np.asarray(i["L0_attn_x"], np.float32)
    xT = np.ascontiguousarray(x.reshape(NCORE, SEQ_PER_CORE, T, D).transpose(0, 1, 3, 2))
    pvv = _pack_vec(i)
    shared = {
        "Wq": np.asarray(i["L0_attn_Wq"], np.float32), "Wk": np.asarray(i["L0_attn_Wk"], np.float32),
        "Wv": np.asarray(i["L0_attn_Wv"], np.float32), "Wo": np.asarray(i["L0_attn_Wo"], np.float32),
        "qw": np.asarray(i["L0_ffn_q_w"], np.float32), "kw": np.asarray(i["L0_ffn_k_w"], np.float32),
        "vw": np.asarray(i["L0_ffn_v_w"], np.float32), "ow": np.asarray(i["L0_ffn_out_w"], np.float32),
        "gw": np.asarray(i["L0_ffn_gate_w"], np.float32),
        "mgw": np.asarray(i["L0_ffn_moe_gate_w"], np.float32),
        "w1": np.asarray(i["L0_ffn_w1"], np.float32), "w2": np.asarray(i["L0_ffn_w2"], np.float32),
        "b2": np.asarray(i["L0_ffn_b2"], np.float32),
        "bvr": np.asarray(i["L0_attn_bv"], np.float32),
        "n2g": np.asarray(i["L0_ffn_n2_g"], np.float32), "n2b": np.asarray(i["L0_ffn_n2_b"], np.float32),
        "mgb": np.asarray(i["L0_ffn_moe_gate_b"], np.float32),
        "pv": pvv,
    }
    return [dict(shared, xT=xT[c]) for c in range(NCORE)]


def kernel(**inputs):
    in_maps = make_in_maps(inputs)
    nc = bass.Bass("TRN2", target_bir_lowering=False)
    build_core(nc)
    res = run_bass_kernel_spmd(nc, in_maps, core_ids=list(range(NCORE)))
    outs = [np.asarray(r["out"], np.float32) for r in res.results]
    return np.concatenate(outs, axis=0).reshape(16, T, D)


# version 05
# speedup vs baseline: 1.0727x; 1.0681x over previous
import numpy as np
import ml_dtypes
from contextlib import ExitStack

import concourse.bass as bass
import concourse.mybir as mybir
from concourse.bass_utils import run_bass_kernel_spmd

F32 = mybir.dt.float32
BF16 = mybir.dt.bfloat16
U32 = mybir.dt.uint32
AF = mybir.ActivationFunctionType
ALU = mybir.AluOpType
AX = mybir.AxisListType

T = 2048
D = 1024
NH = 16
HD = 64
NE = 8
FF = 2048
NCORE = 8
SEQ_PER_CORE = 2

PC_BQ, PC_BK, PC_BO, PC_GB = 0, 8, 16, 24
PC_N1G, PC_N1B = 32, 40
PC_TAU, PC_THR, PC_PHASE = 48, 49, 50
PC_SW, PC_RSC, PC_RSCP = 53, 69, 77
PC_B1, PC_LNG, PC_LNB = 78, 206, 334
PC_IOTA = 462
PC_N = 463

CC_03, CC_EPS6, CC_ONE, CC_NEG1, CC_NEG8, CC_NEGH = 0, 1, 2, 3, 4, 5
CC_I2048, CC_I1024, CC_EPS5, CC_QS1, CC_QS2 = 6, 7, 8, 9, 10
CC_VALS = {CC_03: 0.3, CC_EPS6: 1e-6, CC_ONE: 1.0, CC_NEG1: -1.0, CC_NEG8: -8.0,
           CC_NEGH: -0.5, CC_I2048: 1.0 / 2048.0, CC_I1024: 1.0 / 1024.0,
           CC_EPS5: 1e-5, CC_QS1: 1.0 / 2.4, CC_QS2: 0.125}

ENGS = ("sync", "tensor", "vector", "scalar", "gpsimd")


class Prog:

    def __init__(self, nc, stack):
        self.nc = nc
        self.stack = stack
        self.q = {e: [] for e in ENGS}
        self.sem = {}
        self.kind = {}
        self.owner = {}
        self.cnt = {}
        for e in ENGS:
            self._mk(e, e, "dma" if e == "sync" else "comp")
        for name in ("ld_w", "ld_w0", "ld_w1", "ld_m", "ld_sg", "st", "st2", "wcv_a", "wcv_m", "cv0", "cv1"):
            self._mk(name, None, "dma")
        self.waited = {c: {} for c in ENGS}

    def _mk(self, name, eng, kind):
        self.sem[name] = self.stack.enter_context(self.nc.semaphore(f"s_{name}"))
        self.kind[name] = kind
        self.owner[name] = eng
        self.cnt[name] = 0

    def on(self, eng, fn, wait=(), inc=0, tag="", sem=None, lowp=False):
        sn = sem or eng
        w = []
        for tok in wait:
            if tok is None:
                continue
            pe, v = tok
            assert v <= self.cnt[pe], f"future token {tag}"
            if self.kind[pe] == "dma":
                v = self.cnt[pe]
            if self.waited[eng].get(pe, 0) >= v:
                continue
            self.waited[eng][pe] = v
            w.append((pe, v))
        tok_out = None
        if inc:
            if self.kind[sn] == "dma" and self.cnt[sn] > 0:
                if self.waited[eng].get(sn, 0) < self.cnt[sn]:
                    self.waited[eng][sn] = self.cnt[sn]
                    w.append((sn, self.cnt[sn]))
            self.cnt[sn] += inc
            tok_out = (sn, self.cnt[sn])
        semd = self.sem

        nc_ = self.nc

        def run(engobj, w=w, fn=fn, inc=inc, sn=sn, tag=tag, lowp=lowp):
            for pe, v in w:
                engobj.wait_ge(semd[pe], v)
            if lowp:
                with nc_.allow_low_precision(reason="bf16 sigmoid reciprocal; within measured tolerance"):
                    inst = fn(engobj)
            else:
                inst = fn(engobj)
            if inc:
                assert inst is not None, f"step {tag} returned no instruction"
                inst.then_inc(semd[sn], inc)
        self.q[eng].append(run)
        return tok_out

    def tok(self, sn):
        return (sn, self.cnt[sn])

    def barrier(self):
        toks = [self.tok(e) for e in ENGS] + [self.tok(x) for x in ("ld_w", "ld_w0", "ld_w1", "ld_m", "ld_sg", "st", "st2", "wcv_a", "wcv_m", "cv0", "cv1")]
        out = []
        for e in ENGS:
            out.append(self.on(e, lambda en: en.nop(), wait=toks, inc=1, tag="barrier"))
        return out


def _f2b(a):
    return np.asarray(a, dtype=np.float32).astype(ml_dtypes.bfloat16)


def _host_consts():
    c = {}
    pos = np.arange(T, dtype=np.float64) / T * 2 * np.pi
    bases = np.stack([np.sin(pos), np.cos(pos), np.sin(pos / 2), np.cos(pos / 2),
                      np.ones_like(pos)])
    c["bases"] = _f2b(bases)
    c["identb"] = _f2b(np.eye(128))
    c["identf"] = np.eye(128, dtype=np.float32)
    E = np.zeros((128, 128))
    for h in range(16):
        for i in range(6):
            E[16 * i + h, 8 * h + i] = 1.0
    c["E128"] = _f2b(E)
    c["onesr"] = _f2b(np.ones((1, 128)))
    c["onesc"] = _f2b(np.ones((128, 1)))
    s2a = np.zeros((1, 128)); s2a[0, 0:64] = 1.0
    s2b = np.zeros((1, 128)); s2b[0, 64:128] = 1.0
    c["sel2a"] = _f2b(s2a)
    c["sel2b"] = _f2b(s2b)
    c["sel22"] = _f2b(np.concatenate([s2a, s2b], axis=0))
    c["iota8"] = np.arange(8, dtype=np.float32).reshape(8, 1)

    S1, C1, SH, CH, ON = range(5)
    def ent(h, j):
        return 7 * h + j
    E_NEGT, E_ONE = ent(0, 6), ent(1, 6)
    one = ("one", 1.0)

    def a_of(spec, h):
        kind, sg = spec
        if kind == "one":
            return E_ONE, sg
        if kind == "negt":
            return E_NEGT, sg
        trig, j = kind
        return ent(h, j if trig == "s" else 3 + j), sg

    sin_ = lambda j, s=1.0: (("s", j), s)
    cos_ = lambda j, s=1.0: (("c", j), s)
    TQ1 = [(S1, cos_(0), C1, sin_(0)),
           (CH, cos_(2), SH, sin_(2, -1.0)),
           (SH, cos_(2), CH, sin_(2)),
           (SH, cos_(2), CH, sin_(2)),
           (CH, cos_(2), SH, sin_(2, -1.0)),
           (ON, one, ON, None)]
    TQ2 = [(ON, one, ON, None), (ON, one, ON, None), (ON, one, ON, None),
           (C1, cos_(0), S1, sin_(0, -1.0)),
           (C1, cos_(0), S1, sin_(0, -1.0)),
           (ON, ("negt", 1.0), ON, None)]
    TK1 = [(C1, cos_(1), S1, sin_(1, -1.0)),
           (S1, cos_(1), C1, sin_(1)),
           (S1, cos_(1), C1, sin_(1)),
           (CH, one, ON, None), (SH, one, ON, None), (ON, one, ON, None)]
    TK2 = [(ON, one, ON, None), (CH, one, ON, None), (SH, ("one", -1.0), ON, None),
           (ON, one, ON, None), (ON, one, ON, None), (ON, one, ON, None)]
    tiles = [TQ1, TQ2, TK1, TK2]
    SELB1 = np.zeros((5, 4 * 128))
    SELB2 = np.zeros((5, 4 * 128))
    PA = np.zeros((112, 8 * 128))
    for b, tl in enumerate(tiles):
        for pp in range(128):
            g, h = pp // 16, pp % 16
            if g >= 6:
                continue
            b1, al, b2, be = tl[g]
            SELB1[b1, 128 * b + pp] = 1.0
            idx, sg = a_of(al, h)
            PA[idx, 128 * (2 * b) + pp] = sg
            if be is not None:
                SELB2[b2, 128 * b + pp] = 1.0
                idx2, sg2 = a_of(be, h)
                PA[idx2, 128 * (2 * b + 1) + pp] = sg2
    c["SELB1"] = _f2b(SELB1)
    c["SELB2"] = _f2b(SELB2)
    c["PA"] = _f2b(PA)
    return c


_HC = _host_consts()


def build_core(nc, taps=(), upto="all"):
    def din(name, shape, dtype=F32):
        return nc.dram_tensor(name, list(shape), dtype, kind="ExternalInput").ap()

    xTd = din("xT", [SEQ_PER_CORE, D, T])
    Wq, Wk, Wv, Wo = (din(n, [D, D]) for n in ("Wq", "Wk", "Wv", "Wo"))
    qw, kw, vw, ow, gw = (din(n, [D, D]) for n in ("qw", "kw", "vw", "ow", "gw"))
    mgw = din("mgw", [D, NE])
    w1d = din("w1", [NE, D, FF])
    w2d = din("w2", [NE, FF, D])
    b2d = din("b2", [NE, D])
    bvr_d = din("bvr", [D])
    n2g_d = din("n2g", [D])
    n2b_d = din("n2b", [D])
    mgb_d = din("mgb", [NE])
    pvd = din("pv", [128, PC_N])
    out_d = nc.dram_tensor("out", [SEQ_PER_CORE, T, D], F32, kind="ExternalOutput").ap()

    wbf_att = nc.dram_tensor("wbf_att", [9, 8, 128, 1024], BF16, kind="Internal").ap()
    wbf_w1 = nc.dram_tensor("wbf_w1", [NE, 16, 128, 1024], BF16, kind="Internal").ap()
    wbf_w2 = nc.dram_tensor("wbf_w2", [NE, 16, 128, 1024], BF16, kind="Internal").ap()
    mk_kind = "ExternalOutput" if "mln" in taps else "Internal"
    mln_d = nc.dram_tensor("mln", [NH, T, T], BF16, kind=mk_kind).ap()
    sig_d = nc.dram_tensor("sig", [2, T, T], BF16, kind="Internal").ap()
    scb_d = nc.dram_tensor("scb", [112], BF16, kind="Internal").ap()
    tap_d = {}
    for tname, shape, dt in (("x2", [SEQ_PER_CORE, D, T], BF16),
                             ("x3", [SEQ_PER_CORE, D, T], BF16),
                             ("ao1", [SEQ_PER_CORE, D, T], BF16),
                             ("ao2", [SEQ_PER_CORE, D, T], BF16),
                             ("chosen", [SEQ_PER_CORE, 128, 16], F32)):
        if tname in taps:
            tap_d[tname] = nc.dram_tensor("tap_" + tname, shape, dt, kind="ExternalOutput").ap()

    cst = {k: nc.inline_tensor(v, "c_" + k).ap() for k, v in _HC.items()}

    with ExitStack() as ctx:
        def sb(name, shape, dtype=BF16):
            return ctx.enter_context(nc.sbuf_tensor(name, list(shape), dtype))
        xA = sb("xA", [128, 16384])
        xB = sb("xB", [128, 16384])
        ao = sb("ao", [128, 16384])
        wk = sb("wk", [128, 12288])
        mz = sb("mz", [128, 16384])
        qk = sb("qk", [128, 4096])
        vab = sb("vab", [128, 2128])
        pb = sb("pb", [128, 2048])
        hhB = sb("hhB", [128, 8192])
        sh1 = sb("sh1", [128, 2048])
        sh2 = sb("sh2", [128, 1024])
        wst = sb("wst", [128, 1024], F32)
        pv = sb("pvt", [128, PC_N], F32)
        cc = sb("cc", [128, 16], F32)
        bvr = sb("bvrt", [1, D])
        mgbr = sb("mgbrt", [1, NE])
        n2gbc = sh1[:, 0:1024]
        n2bbc = sh1[:, 1024:2048]
        b2rs = sb("b2rst", [NE, D])
        rowb1 = sb("rowb1", [1, 2048])
        rowb2 = sb("rowb2", [1, 1024])
        sct = sb("sct", [16, 7], F32)
        sc112 = sb("sc112", [112, 1])
        scw = sb("scw", [16, 3], F32)
        ab8 = sb("ab8", [128, 8], F32)
        invt = sb("invt", [128, 1], F32)
        bqs = sb("bqs", [128, 8], F32)
        gbn = sb("gbn", [128, 8], F32)
        swb = sb("swb", [128, 16])
        lgt = sb("lgt", [128, 8], F32)
        v8t = sb("v8t", [128, 8], F32)
        i8t = sb("i8t", [128, 8], U32)
        i8f = sb("i8f", [128, 8], F32)
        chosen = sb("chosen", [128, 16], F32)
        sctb = lgt[0:16, 0:8].bitcast(BF16)[:, 0:7]
        iota8 = pv[0:8, PC_IOTA:PC_IOTA + 1]
        chosb = i8f[:, 0:8].bitcast(BF16)[:, 0:2]
        colf = sb("colf", [128, 12], F32)
        identb = sb("identb", [128, 128])
        E128 = sb("E128", [128, 128])
        onesr = sb("onesr", [1, 128])
        onesc = sb("onesc", [128, 1])
        sel2b = sb("sel2b", [1, 128])
        sel22 = sb("sel22", [2, 128])
        sel2a = sel22[0:1, 0:128]

        UT = xB[0:8, 0:2048]
        VT = xB[0:8, 2048:4096]
        ch8 = sh2[0:8, 0:256].bitcast(F32)
        chrow = sh2[0:1, 256:384]
        m8b = sh2[0:8, 384:512]
        x3tm = qk[:, 0:1024]
        x2tm = qk[:, 1024:2048]
        ztt = qk[:, 2048:3072]
        f2t = qk[:, 3072:4096]
        sqt = pb[:, 0:1024]
        ot = vab[:, 0:2048].bitcast(F32)
        s2f = sh1[0:2, 0:2048]
        s2b_ = s2f
        basesb = mz[0:5, 12288:14336]
        PAc = mz[0:112, 14336:15360]
        SELB1 = mz[0:5, 15360:15872]
        SELB2 = mz[0:5, 15872:16384]
        psA = ctx.enter_context(nc.psum_tensor("psA", [128, 1024], F32))
        psB = ctx.enter_context(nc.psum_tensor("psB", [128, 1024], F32))
        psC = ctx.enter_context(nc.psum_tensor("psC", [128, 2048], F32))

        p = Prog(nc, ctx)
        wpp = {0: None, 1: None, "n": 0}

        def wload2(src_fn, dst_fn, waits, scl=None, tag="wl"):
            toks = []
            for hf in (0, 1):
                par = wpp["n"] % 2
                wpp["n"] += 1
                tl = p.on("sync", lambda e, hf=hf, par=par: e.dma_start(out=wst[:, 512 * par:512 * (par + 1)], in_=src_fn(hf)),
                          wait=list(waits) + [wpp[par]], inc=16, sem="ld_w0" if par == 0 else "ld_w1", tag=tag + "d")
                if scl is None:
                    ct = p.on("vector", lambda e, hf=hf, par=par: e.tensor_copy(dst_fn(hf), wst[:, 512 * par:512 * (par + 1)]), wait=[tl], inc=1, tag=tag + "c")
                else:
                    ct = p.on("vector", lambda e, hf=hf, par=par, scl=scl: e.tensor_scalar_mul(dst_fn(hf), wst[:, 512 * par:512 * (par + 1)], scl), wait=[tl], inc=1, tag=tag + "c")
                wpp[par] = ct
                toks.append(ct)
            return toks
        blk = ctx.enter_context(nc.Block())
        mzsl = lambda o, n=2048: mz[:, o:o + n]

        ld = []
        for dst, src in ((pv, pvd), (identb, cst["identb"]),
                         (E128, cst["E128"]), (onesr, cst["onesr"]), (onesc, cst["onesc"]),
                         (sel2b, cst["sel2b"]), (sel22, cst["sel22"]),
                         (basesb, cst["bases"]),
                         (SELB1, cst["SELB1"]), (SELB2, cst["SELB2"]), (PAc, cst["PA"])):
            ld.append(p.on("sync", lambda e, d=dst, s=src: e.dma_start(out=d[:, :], in_=s),
                           inc=16, tag="ldc"))
        t_bv = p.on("sync", lambda e: e.dma_start(out=wst[0:1, 0:D], in_=bvr_d.rearrange("(a d) -> a d", a=1)), inc=16)
        t_bvc = p.on("gpsimd", lambda e: e.tensor_copy(bvr[:, :], wst[0:1, 0:D]), wait=[t_bv], inc=1)
        t_mgb = p.on("sync", lambda e: e.dma_start(out=wst[0:1, 0:NE], in_=mgb_d.rearrange("(a d) -> a d", a=1)), wait=[t_bvc], inc=16)
        t_mgbc = p.on("gpsimd", lambda e: e.tensor_copy(mgbr[:, :], wst[0:1, 0:NE]), wait=[t_mgb], inc=1)
        t_b2 = p.on("sync", lambda e: e.dma_start(out=wst[0:NE, 0:D], in_=b2d), wait=[t_mgbc], inc=16)
        for col, val in CC_VALS.items():
            p.on("vector", lambda e, c=col, v=val: e.memset(cc[:, c:c + 1], v), inc=1)
        t_cc = p.on("vector", lambda e: e.memset(cc[:, 15:16], 0.0), inc=1)
        wld = ld + [t_cc]

        cvst = {"p": 0, 0: None, 1: None}
        conv_tok = {}

        def emit_convert(src_fn, dst_ap, scl, key, sem):
            par2 = cvst["p"] % 2
            cvst["p"] += 1
            casts = []
            for hf in (0, 1):
                par = wpp["n"] % 2
                wpp["n"] += 1
                tl = p.on("scalar", lambda e, hf=hf, par=par: e.dma_start(out=wst[:, 512 * par:512 * (par + 1)], in_=src_fn(hf)),
                          wait=[wpp[par]], inc=16, sem="cv0" if par == 0 else "cv1", tag="cvd")
                if scl is None:
                    ct = p.on("vector", lambda e, hf=hf, par=par, par2=par2: e.tensor_copy(hhB[:, 1024 * par2 + 512 * hf:1024 * par2 + 512 * (hf + 1)], wst[:, 512 * par:512 * (par + 1)]),
                              wait=[tl, cvst[par2]], inc=1, tag="cvc")
                else:
                    ct = p.on("vector", lambda e, hf=hf, par=par, par2=par2, scl=scl: e.tensor_scalar_mul(hhB[:, 1024 * par2 + 512 * hf:1024 * par2 + 512 * (hf + 1)], wst[:, 512 * par:512 * (par + 1)], scl),
                              wait=[tl, cvst[par2]], inc=1, tag="cvc")
                wpp[par] = ct
                casts.append(ct)
            stt_ = p.on("gpsimd", lambda e, par2=par2: e.dma_start(out=dst_ap, in_=hhB[:, 1024 * par2:1024 * (par2 + 1)]),
                        wait=casts, inc=16, sem=sem, tag="cvs")
            cvst[par2] = stt_
            conv_tok[key] = stt_

        att_mats = [(Wq, 1.0 / 2.4), (Wk, None), (Wv, None), (Wo, None),
                    (qw, 0.125), (kw, None), (vw, None), (ow, None), (gw, None)]
        conv_all = []

        def emit_conversion():
            for pr_ in range(8):
                for mi_ in (0, 1, 2, 4, 5, 6):
                    Wd, scl = att_mats[mi_]
                    emit_convert(lambda hf, Wd=Wd, pr_=pr_: Wd[512 * hf:512 * (hf + 1), 128 * pr_:128 * (pr_ + 1)].rearrange("(a p) j -> p a j", p=128),
                                 wbf_att[mi_, pr_], scl, ("att", mi_, pr_), "wcv_a")
            for pr_ in range(8):
                for mi_ in (3, 7, 8):
                    Wd, scl = att_mats[mi_]
                    emit_convert(lambda hf, Wd=Wd, pr_=pr_: Wd[512 * hf:512 * (hf + 1), 128 * pr_:128 * (pr_ + 1)].rearrange("(a p) j -> p a j", p=128),
                                 wbf_att[mi_, pr_], scl, ("att", mi_, pr_), "wcv_a")
            for ee_ in range(NE):
                for jj_ in range(16):
                    emit_convert(lambda hf, ee_=ee_, jj_=jj_: w1d[ee_, 512 * hf:512 * (hf + 1), 128 * jj_:128 * (jj_ + 1)].rearrange("(a p) f -> p a f", p=128),
                                 wbf_w1[ee_, jj_], None, ("w1", ee_, jj_), "wcv_m")
                for fc_ in range(16):
                    emit_convert(lambda hf, ee_=ee_, fc_=fc_: w2d[ee_, 128 * fc_:128 * (fc_ + 1), 512 * hf:512 * (hf + 1)],
                                 wbf_w2[ee_, fc_], None, ("w2", ee_, fc_), "wcv_m")
            conv_all.extend([cvst[0], cvst[1]])

        t_b2rs = p.on("vector", lambda e: e.tensor_scalar_mul(b2rs[:, :], wst[0:NE, 0:D], pv[0:NE, PC_RSCP:PC_RSCP + 1]),
                      wait=wld + [t_b2], inc=1)
        wpp[0] = t_b2rs
        wpp[1] = t_b2rs
        t_bqs = p.on("vector", lambda e: e.tensor_scalar_mul(bqs[:, :], pv[:, PC_BQ:PC_BQ + 8], 1.0 / 2.4), wait=wld, inc=1)
        t_gbn = p.on("vector", lambda e: e.tensor_scalar_mul(gbn[:, :], pv[:, PC_GB:PC_GB + 8], -1.0), wait=wld, inc=1)
        t_swb = p.on("gpsimd", lambda e: e.tensor_copy(swb[:, :], pv[:, PC_SW:PC_SW + 16]), wait=wld, inc=1)
        t_i1 = p.on("vector", lambda e: e.tensor_scalar_add(colf[:, 0:1], pv[:, PC_TAU:PC_TAU + 1], 1e-6), wait=wld, inc=1)
        t_invt = p.on("vector", lambda e: e.reciprocal(invt[:, :], colf[:, 0:1]), wait=[t_i1], inc=1)

        psc_free = [t_b2rs]

        PI = float(np.pi)
        phs = pv[0:16, PC_PHASE:PC_PHASE + 3]
        t1 = p.on("vector", lambda e: e.tensor_single_scalar(scw[:, :], phs, PI, ALU.is_gt), wait=wld, inc=1)
        t2 = p.on("vector", lambda e: e.scalar_tensor_tensor(sct[:, 3:6], scw[:, :], -2 * PI, phs, ALU.mult, ALU.add), wait=[t1], inc=1)
        t3 = p.on("vector", lambda e: e.tensor_single_scalar(scw[:, :], sct[:, 3:6], -PI, ALU.is_lt), wait=[t2], inc=1)
        t4 = p.on("vector", lambda e: e.scalar_tensor_tensor(sct[:, 3:6], scw[:, :], 2 * PI, sct[:, 3:6], ALU.mult, ALU.add), wait=[t3], inc=1)
        t5 = p.on("vector", lambda e: e.tensor_scalar_add(scw[:, :], sct[:, 3:6], PI / 2), wait=[t4], inc=1)
        t6 = p.on("vector", lambda e: e.tensor_single_scalar(sct[:, 0:3], scw[:, :], PI, ALU.is_gt), wait=[t5], inc=1)
        t7 = p.on("vector", lambda e: e.scalar_tensor_tensor(scw[:, :], sct[:, 0:3], -2 * PI, scw[:, :], ALU.mult, ALU.add), wait=[t6], inc=1)
        t8 = p.on("scalar", lambda e: e.activation(sct[:, 3:6], sct[:, 3:6], AF.Sin), wait=[t7], inc=1)
        t9 = p.on("scalar", lambda e: e.activation(sct[:, 0:3], scw[:, :], AF.Sin), wait=[t8], inc=1)
        t10 = p.on("vector", lambda e: e.tensor_copy(scw[:, :], sct[:, 3:6]), wait=[t9], inc=1)
        t11 = p.on("vector", lambda e: e.tensor_copy(sct[:, 3:6], sct[:, 0:3]), wait=[t10], inc=1)
        t12 = p.on("vector", lambda e: e.tensor_copy(sct[:, 0:3], scw[:, :]), wait=[t11], inc=1)
        t13 = p.on("vector", lambda e: e.memset(sct[:, 6:7], 1.0), wait=[t12], inc=1)
        t14 = p.on("vector", lambda e: e.tensor_scalar_mul(sct[0:1, 6:7], pv[0:1, PC_THR:PC_THR + 1], -1.0), wait=[t13], inc=1)
        t14b = p.on("gpsimd", lambda e: e.tensor_copy(sctb[:, :], sct[:, :]), wait=[t14], inc=1)
        t15 = p.on("gpsimd", lambda e: e.dma_start(out=scb_d.rearrange("(h j) -> h j", j=7), in_=sctb[:, :]), wait=[t14b], inc=16, sem="st")
        t16 = p.on("sync", lambda e: e.dma_start(out=sc112[:, :], in_=scb_d.rearrange("(p o) -> p o", o=1)), wait=[t15], inc=16)
        tab = [p.on("tensor", lambda e, b=b: e.matmul(psB[:, b:b + 1], PAc[:, 128 * b:128 * (b + 1)], sc112[:, :], start=True, stop=True),
               wait=[t16] + wld, inc=1) for b in range(8)]
        t_ab = p.on("vector", lambda e: e.tensor_copy(ab8[:, :], psB[:, 0:8]), wait=tab, inc=1)

        t_prev = t_ab
        tdone = []
        for b in range(4):
            m1 = [p.on("tensor", lambda e, b=b, j=j: e.matmul((psA if j < 2 else psB)[:, 512 * (j % 2):512 * (j % 2 + 1)], SELB1[:, 128 * b:128 * (b + 1)], basesb[:, 512 * j:512 * (j + 1)], start=True, stop=True),
                  wait=[t_prev] + wld, inc=(1 if j in (1, 3) else 0), tag=f"selb1_{b}") for j in range(4)]
            c1 = p.on("vector", lambda e, b=b: e.tensor_scalar_mul(mzsl(8192)[:, 0:1024], psA[:, :], ab8[:, 2 * b:2 * b + 1]), wait=[m1[1], m1[3]], inc=1)
            c2 = p.on("vector", lambda e, b=b: e.tensor_scalar_mul(mzsl(8192)[:, 1024:2048], psB[:, :], ab8[:, 2 * b:2 * b + 1]), wait=[m1[1], m1[3]], inc=1)
            m2 = [p.on("tensor", lambda e, b=b, j=j: e.matmul((psA if j < 2 else psB)[:, 512 * (j % 2):512 * (j % 2 + 1)], SELB2[:, 128 * b:128 * (b + 1)], basesb[:, 512 * j:512 * (j + 1)], start=True, stop=True),
                  wait=[c1, c2], inc=(1 if j in (1, 3) else 0), tag=f"selb2_{b}") for j in range(4)]
            d1 = p.on("vector", lambda e, b=b: e.scalar_tensor_tensor(mzsl(2048 * b)[:, 0:1024], psA[:, :], ab8[:, 2 * b + 1:2 * b + 2], mzsl(8192)[:, 0:1024], ALU.mult, ALU.add), wait=[m2[1], m2[3]], inc=1)
            d2 = p.on("vector", lambda e, b=b: e.scalar_tensor_tensor(mzsl(2048 * b)[:, 1024:2048], psB[:, :], ab8[:, 2 * b + 1:2 * b + 2], mzsl(8192)[:, 1024:2048], ALU.mult, ALU.add), wait=[m2[1], m2[3]], inc=1)
            t_prev = d2
            tdone += [d1, d2]
        tU = p.on("vector", lambda e: e.tensor_mul(mzsl(0)[:, :], mzsl(0)[:, :], mzsl(2048)[:, :]), wait=tdone[0:4], inc=1)
        tU2 = p.on("vector", lambda e: e.tensor_scalar_mul(mzsl(0)[:, :], mzsl(0)[:, :], invt[:, :]), wait=[tU, t_invt], inc=1)
        tV = p.on("vector", lambda e: e.tensor_mul(mzsl(4096)[:, :], mzsl(4096)[:, :], mzsl(6144)[:, :]), wait=tdone[4:8], inc=1)
        psab_free = [tU2, tV]

        mask_chunk_tok = {}
        mask_state = {"uvfree": [tU2, tV], "psab": psab_free, "st_prev": None, "ln_prev": tdone[-1]}

        def emit_mask_head(h):
            sl = h % 2
            exu = [p.on("tensor", lambda e, j=j: e.matmul((psA if j < 2 else psB)[0:8, 512 * (j % 2):512 * (j % 2 + 1)], E128[:, 8 * h:8 * h + 8], mzsl(0)[:, 512 * j:512 * (j + 1)], start=True, stop=True),
                   wait=[tU2] + mask_state["uvfree"] + mask_state["psab"], inc=(1 if j in (1, 3) else 0), tag=f"exu{h}") for j in range(4)]
            cu1 = p.on("scalar", lambda e: e.activation(UT[:, 0:1024], psA[0:8, :], AF.Copy), wait=[exu[1], exu[3]], inc=1)
            cu2 = p.on("scalar", lambda e: e.activation(UT[:, 1024:2048], psB[0:8, :], AF.Copy), wait=[exu[1], exu[3]], inc=1)
            exv = [p.on("tensor", lambda e, j=j: e.matmul((psA if j < 2 else psB)[0:8, 512 * (j % 2):512 * (j % 2 + 1)], E128[:, 8 * h:8 * h + 8], mzsl(4096)[:, 512 * j:512 * (j + 1)], start=True, stop=True),
                   wait=[tV, cu1, cu2], inc=(1 if j in (1, 3) else 0), tag=f"exv{h}") for j in range(4)]
            cv1 = p.on("scalar", lambda e: e.activation(VT[:, 0:1024], psA[0:8, :], AF.Copy), wait=[exv[1], exv[3]], inc=1)
            cv2 = p.on("scalar", lambda e: e.activation(VT[:, 1024:2048], psB[0:8, :], AF.Copy), wait=[exv[1], exv[3]], inc=1)
            prev_sig = [cv1, cv2]
            st_prev = mask_state["st_prev"]
            sg_store = []
            for kc in range(16):
                ms = [p.on("tensor", lambda e, kc=kc, j=j: e.matmul((psA if j < 2 else psB)[:, 512 * (j % 2):512 * (j % 2 + 1)], VT[:, 128 * kc:128 * (kc + 1)], UT[:, 512 * j:512 * (j + 1)], start=True, stop=True),
                      wait=prev_sig, inc=(1 if j in (1, 3) else 0), tag=f"vmm{h}_{kc}") for j in range(4)]
                s1t = p.on("scalar", lambda e: e.activation(mzsl(10240)[:, 0:1024], psA[:, :], AF.Sigmoid), wait=[ms[1], ms[3], st_prev], inc=1)
                s2t = p.on("scalar", lambda e: e.activation(mzsl(10240)[:, 1024:2048], psB[:, :], AF.Sigmoid), wait=[ms[1], ms[3], st_prev], inc=1)
                st_prev = p.on("gpsimd", lambda e, kc=kc, sl=sl: e.dma_start(out=sig_d[sl, 128 * kc:128 * (kc + 1), :], in_=mzsl(10240)[:, :]), wait=[s1t, s2t], inc=16, tag=f"sgst{h}_{kc}", sem="st")
                sg_store.append(st_prev)
                prev_sig = [s1t, s2t]
            mask_state["uvfree"] = [ms[1], ms[3]]
            mask_state["psab"] = prev_sig
            mask_state["st_prev"] = st_prev
            ln_prev = mask_state["ln_prev"]
            lnld = {}
            for kc in range(2):
                lnld[kc] = p.on("sync", lambda e, kc=kc, sl=sl: e.dma_start(out=mzsl(12288 + 2048 * (kc % 2))[:, :], in_=sig_d[sl, 128 * kc:128 * (kc + 1), :]), wait=[sg_store[kc], ln_prev], inc=16, sem="ld_sg")
            for kc in range(16):
                lt = p.on("scalar", lambda e, kc=kc: e.activation(mzsl(2048)[:, :], mzsl(12288 + 2048 * (kc % 2))[:, :], AF.Ln, bias=cc[:, CC_EPS6:CC_EPS6 + 1]),
                          wait=[lnld[kc], ln_prev], inc=1, tag=f"ln{h}_{kc}")
                ln_prev = p.on("gpsimd", lambda e, kc=kc: e.dma_start(out=mln_d[h, 128 * kc:128 * (kc + 1), :], in_=mzsl(2048)[:, :]), wait=[lt], inc=16, tag=f"mlnst{h}_{kc}", sem="st2")
                mask_chunk_tok[(h, kc)] = ln_prev
                if kc + 2 < 16:
                    lnld[kc + 2] = p.on("sync", lambda e, kc=kc, sl=sl: e.dma_start(out=mzsl(12288 + 2048 * (kc % 2))[:, :], in_=sig_d[sl, 128 * (kc + 2):128 * (kc + 3), :]), wait=[sg_store[kc + 2], lt], inc=16, sem="ld_sg")
            mask_state["ln_prev"] = ln_prev

        def emit_xload(s, waits):
            prev = list(waits)
            for c in range(8):
                for hlf in range(2):
                    tk = wload2(lambda hf, c=c, h=hlf, s=s: xTd[s, 128 * c:128 * (c + 1), 1024 * h + 512 * hf:1024 * h + 512 * (hf + 1)],
                                lambda hf, c=c, h=hlf: xA[:, 2048 * c + 1024 * h + 512 * hf:2048 * c + 1024 * h + 512 * (hf + 1)],
                                prev, tag="xld")
                    prev = tk[-1:]
            return prev

        st_ctx = {"psab": psab_free, "psc": psc_free, "mbf": [tV, tU2], "wkfree": None, "wstfree": None}

        def emit_attention(s, masked, x_t, waits, pre_pair_hook=None):
            mats = (0, 1, 2) if masked else (4, 5, 6)
            wbuf_free = {0: list(waits), 1: list(waits)}
            pair_done = []
            vset = None
            for tc in range(16):
                vset = p.on("vector", lambda e, tc=tc: e.memset(vab[:, 65 * tc + 64:65 * tc + 65], 1.0), wait=list(waits) + [vset], inc=1)
                vset = p.on("vector", lambda e, tc=tc: e.memset(vab[:, 1088 + 65 * tc + 64:1088 + 65 * tc + 65], 1.0), wait=[vset], inc=1)
            t_von = vset
            for pr in range(8):
                if pre_pair_hook is not None:
                    pre_pair_hook(pr)
                par = pr % 2
                wo_ = 3072 * par
                wts = []
                for wi, mat in enumerate(mats):
                    tl = p.on("sync", lambda e, mat=mat, pr=pr, wi=wi, wo_=wo_: e.dma_start(out=wk[:, wo_ + 1024 * wi:wo_ + 1024 * (wi + 1)], in_=wbf_att[mat, pr]),
                              wait=wbuf_free[par] + list(waits) + [conv_tok[("att", mat, pr)]], inc=16, sem="ld_w0" if par == 0 else "ld_w1", tag=f"wld{pr}_{wi}")
                    wts.append(tl)
                proj_prev = st_ctx["psab"]
                proj_last = None
                for mi in (0, 1):
                    for tc4 in range(4):
                        ps = psA if tc4 % 2 == 0 else psB
                        mm = [p.on("tensor", lambda e, a=a, mi=mi, tc4=tc4, ps=ps, wo_=wo_: e.matmul(ps[:, 0:512], wk[:, wo_ + 1024 * mi + 128 * a:wo_ + 1024 * mi + 128 * (a + 1)], x_t[:, 2048 * a + 512 * tc4:2048 * a + 512 * (tc4 + 1)], start=(a == 0), stop=(a == 7)),
                              wait=[wts[mi]] + proj_prev, inc=(1 if a == 7 else 0), tag=f"prj{pr}_{mi}_{tc4}") for a in range(8)]
                        if masked:
                            bias = bqs[:, pr:pr + 1] if mi == 0 else pv[:, PC_BK + pr:PC_BK + pr + 1]
                        else:
                            bias = 0.0
                        ev = p.on("scalar", lambda e, mi=mi, tc4=tc4, ps=ps, bias=bias: e.activation(qk[:, 2048 * mi + 512 * tc4:2048 * mi + 512 * (tc4 + 1)], ps[:, 0:512], AF.Identity, bias=bias),
                                  wait=[mm[-1]], inc=1, tag=f"prjev{pr}_{mi}_{tc4}")
                        proj_prev = [ev]
                        proj_last = ev
                vprev = [proj_last]
                vdone = []
                for tc in range(16):
                    ps = psA if tc % 2 == 0 else psB
                    mm = [p.on("tensor", lambda e, a=a, tc=tc, ps=ps, wo_=wo_: e.matmul(ps[:, 0:128], x_t[:, 2048 * a + 128 * tc:2048 * a + 128 * (tc + 1)], wk[:, wo_ + 2048 + 128 * a:wo_ + 2048 + 128 * (a + 1)], start=(a == 0), stop=(a == 7 and not masked)),
                          wait=[wts[2]] + vprev, inc=(1 if (a == 7 and not masked) else 0), tag=f"vprj{pr}_{tc}") for a in range(8)]
                    if masked:
                        mm.append(p.on("tensor", lambda e, tc=tc, ps=ps, pr=pr: e.matmul(ps[:, 0:128], onesr[:, :], bvr[:, 128 * pr:128 * (pr + 1)], start=False, stop=True),
                                       wait=[t_bvc], inc=1, tag=f"vbias{pr}_{tc}"))
                    ev1 = p.on("vector", lambda e, tc=tc, ps=ps: e.tensor_copy(vab[:, 65 * tc:65 * tc + 64], ps[:, 0:64]), wait=[mm[-1]], inc=1)
                    ev2 = p.on("vector", lambda e, tc=tc, ps=ps: e.tensor_copy(vab[:, 1088 + 65 * tc:1088 + 65 * tc + 64], ps[:, 64:128]), wait=[mm[-1]], inc=1)
                    vprev = [ev1, ev2]
                    vdone += [ev1, ev2]
                vready = t_von if False else vdone[-1]
                qkready = proj_last
                psab_now = vdone[-2:]

                if not masked:
                    sdot = [p.on("tensor", lambda e, j=j, pr=pr: e.matmul((psA if j < 2 else psB)[0:2, 512 * (j % 2):512 * (j % 2 + 1)], swb[:, 2 * pr:2 * pr + 2], qk[:, 512 * j:512 * (j + 1)], start=True, stop=True),
                            wait=[qkready, t_swb] + psab_now, inc=(1 if j in (1, 3) else 0), tag=f"sdot{pr}") for j in range(4)]
                    e1 = p.on("scalar", lambda e: e.activation(s2f[:, 0:1024], psA[0:2, :], AF.Exp, scale=cc[0:2, CC_NEG8:CC_NEG8 + 1]), wait=[sdot[1], sdot[3]], inc=1)
                    e2 = p.on("scalar", lambda e: e.activation(s2f[:, 1024:2048], psB[0:2, :], AF.Exp, scale=cc[0:2, CC_NEG8:CC_NEG8 + 1]), wait=[sdot[1], sdot[3]], inc=1)
                    a1 = p.on("vector", lambda e: e.tensor_scalar_add(s2f[:, :], s2f[:, :], 1.0), wait=[e1, e2], inc=1)
                    r1 = p.on("vector", lambda e: e.reciprocal(s2b_[:, :], s2f[:, :]), wait=[a1], inc=1, lowp=True)
                    sbm = [p.on("tensor", lambda e, j=j: e.matmul((psA if j < 2 else psB)[:, 512 * (j % 2):512 * (j % 2 + 1)], sel22[:, :], s2b_[:, 512 * j:512 * (j + 1)], start=True, stop=True),
                           wait=[r1, e1, e2], inc=(1 if j in (1, 3) else 0), tag=f"sbc{pr}") for j in range(4)]
                    c1 = p.on("vector", lambda e: e.tensor_copy(sh1[:, 0:1024], psA[:, :]), wait=[sbm[1], sbm[3]], inc=1)
                    c2 = p.on("vector", lambda e: e.tensor_copy(sh1[:, 1024:2048], psB[:, :]), wait=[sbm[1], sbm[3]], inc=1)
                    qkready = p.on("vector", lambda e: e.scalar_tensor_tensor(qk[:, 0:2048], qk[:, 0:2048], 2.0, sh1[:, :], ALU.mult, ALU.mult), wait=[c1, c2], inc=1, tag=f"qs{pr}")
                    psab_now = [c1, c2]

                recip_rows = []
                head_ev = []
                psc_prev = st_ctx["psc"]
                for h2 in range(2):
                    h = 2 * pr + h2
                    rb = 64 * h2
                    vo = 1088 * h2
                    exp_tok = {}
                    v_tok = {}
                    mln_ld = {}
                    mmM_tok = {}
                    if masked:
                        for kc in range(2):
                            mln_ld[kc] = p.on("sync", lambda e, kc=kc, h=h: e.dma_start(out=mz[:, 6144 + 2048 * (kc % 2):6144 + 2048 * (kc % 2 + 1)], in_=mln_d[h, 128 * kc:128 * (kc + 1), :]),
                                              wait=[mask_chunk_tok[(h, kc)], qkready] + st_ctx["mbf"], inc=16, tag=f"mld{h}_{kc}", sem="ld_m")
                    for kc in range(16):
                        for half in range(2):
                            ps = psA if half == 0 else psB
                            w0 = [qkready, vready, t_von, exp_tok.get((kc - 1, half))] + (psab_now if kc == 0 else [])
                            smm = [p.on("tensor", lambda e, kc=kc, half=half, j=j, ps=ps, rb=rb: e.matmul(ps[:, 512 * j:512 * (j + 1)], qk[rb:rb + 64, 2048 + 128 * kc:2048 + 128 * (kc + 1)], qk[rb:rb + 64, 1024 * half + 512 * j:1024 * half + 512 * (j + 1)], start=True, stop=(not masked)),
                                   wait=w0, inc=(0 if masked else (1 if j == 1 else 0)), tag=f"smm{h}_{kc}_{half}") for j in range(2)]
                            if masked:
                                for j in range(2):
                                    smm.append(p.on("tensor", lambda e, kc=kc, half=half, j=j, ps=ps: e.matmul(ps[:, 512 * j:512 * (j + 1)], identb[:, :], mz[:, 6144 + 2048 * (kc % 2) + 1024 * half + 512 * j:6144 + 2048 * (kc % 2) + 1024 * half + 512 * (j + 1)], start=False, stop=True),
                                                   wait=[mln_ld[kc]], inc=(1 if j == 1 else 0), tag=f"mmm{h}_{kc}_{half}"))
                                mmM_tok[(kc, half)] = smm[-1]
                            et = p.on("scalar", lambda e, half=half, ps=ps, masked=masked: e.activation(pb[:, 1024 * half:1024 * (half + 1)], ps[:, :], AF.Exp, scale=(cc[:, CC_03:CC_03 + 1] if masked else 1.0)),
                                      wait=[smm[-1], v_tok.get((kc - 1, half))], inc=1, tag=f"exp{h}_{kc}_{half}")
                            exp_tok[(kc, half)] = et
                            vmm = [p.on("tensor", lambda e, kc=kc, half=half, j=j, vo=vo: e.matmul(psC[0:65, 1024 * half + 512 * j:1024 * half + 512 * (j + 1)], vab[:, vo + 65 * kc:vo + 65 * (kc + 1)], pb[:, 1024 * half + 512 * j:1024 * half + 512 * (j + 1)], start=(kc == 0), stop=(kc == 15)),
                                   wait=[et] + (psc_prev if kc == 0 else []), inc=(1 if j == 1 else 0), tag=f"vmm2{h}_{kc}_{half}") for j in range(2)]
                            v_tok[(kc, half)] = vmm[-1]
                        if masked and kc + 2 < 16:
                            mln_ld[kc + 2] = p.on("sync", lambda e, kc=kc, h=h: e.dma_start(out=mz[:, 6144 + 2048 * (kc % 2):6144 + 2048 * (kc % 2 + 1)], in_=mln_d[h, 128 * (kc + 2):128 * (kc + 3), :]),
                                                  wait=[mask_chunk_tok[(h, kc + 2)], mmM_tok[(kc, 1)]], inc=16, tag=f"mld{h}_{kc + 2}", sem="ld_m")
                    if masked:
                        st_ctx["mbf"] = [mmM_tok[(14, 0)], mmM_tok[(14, 1)], mmM_tok[(15, 0)], mmM_tok[(15, 1)]]
                    vfin = [v_tok[(15, 0)], v_tok[(15, 1)]]
                    ev = p.on("vector", lambda e, pr=pr, rb=rb: e.tensor_copy(ao[rb:rb + 64, 2048 * pr:2048 * (pr + 1)], psC[0:64, :]), wait=vfin, inc=1, tag=f"aoev{h}")
                    rbt = rowb1 if h2 == 0 else sh1[0:1, 0:2048]
                    rc = None
                    for hf in range(2):
                        rc = p.on("vector", lambda e, hf=hf: e.reciprocal(psA[0:1, 0:1024], psC[64:65, 1024 * hf:1024 * (hf + 1)]), wait=vfin + [ev, rc, exp_tok[(15, 0)], exp_tok[(15, 1)]], inc=1, tag=f"recip{h}")
                        rc = p.on("vector", lambda e, hf=hf, rbt=rbt: e.tensor_copy(rbt[:, 1024 * hf:1024 * (hf + 1)], psA[0:1, 0:1024]), wait=[rc], inc=1)
                    rcb = rc
                    recip_rows.append(rcb)
                    head_ev.append(ev)
                    psc_prev = [rc, ev]
                    psab_now = [exp_tok[(15, 0)], exp_tok[(15, 1)], rcb]
                bmm = []
                for h2 in range(2):
                    sel = sel2a if h2 == 0 else sel2b
                    rbt = rowb1 if h2 == 0 else sh1[0:1, 0:2048]
                    for j in range(4):
                        bmm.append(p.on("tensor", lambda e, j=j, sel=sel, rbt=rbt, h2=h2: e.matmul(psC[:, 512 * j:512 * (j + 1)], sel[:, :], rbt[:, 512 * j:512 * (j + 1)], start=(h2 == 0), stop=(h2 == 1)),
                                        wait=recip_rows + psc_prev, inc=(1 if j == 3 else 0), tag=f"rbc{pr}_{h2}"))
                nm = None
                for hf in range(2):
                    eb = p.on("vector", lambda e, hf=hf: e.tensor_copy(sh2[:, 0:1024], psC[:, 1024 * hf:1024 * (hf + 1)]), wait=[t for t in bmm if t] + [nm], inc=1, tag=f"rbev{pr}_{hf}")
                    nm = p.on("vector", lambda e, pr=pr, hf=hf: e.tensor_mul(ao[:, 2048 * pr + 1024 * hf:2048 * pr + 1024 * (hf + 1)], ao[:, 2048 * pr + 1024 * hf:2048 * pr + 1024 * (hf + 1)], sh2[:, 0:1024]), wait=[eb] + head_ev, inc=1, tag=f"nrm{pr}_{hf}")
                eb = nm
                st_ctx["psc"] = [eb]
                st_ctx["psab"] = psab_now
                wbuf_free[par] = [nm]
                pair_done.append(nm)
            return pair_done


        def emit_projphase(s, mat, src_t, res_t, dst_t, bias_base, waits, tag):
            wk_used = {0: st_ctx["wkfree"], 1: st_ctx["wkfree"]}
            prev_ps = st_ctx["psab"]
            evs = []
            for c in range(8):
                par = c % 2
                tcst = p.on("sync", lambda e, mat=mat, c=c, par=par: e.dma_start(out=wk[:, 1024 * par:1024 * (par + 1)], in_=wbf_att[mat, c]),
                            wait=list(waits) + [wk_used[par], conv_tok[("att", mat, c)]], inc=16, sem="ld_w0" if par == 0 else "ld_w1", tag=f"{tag}wc{c}")
                last_mm = None
                for tc4 in range(4):
                    ps = psA if tc4 % 2 == 0 else psB
                    mm = [p.on("tensor", lambda e, a=a, c=c, tc4=tc4, ps=ps, par=par: e.matmul(ps[:, 0:512], wk[:, 1024 * par + 128 * a:1024 * par + 128 * (a + 1)], src_t[:, 2048 * a + 512 * tc4:2048 * a + 512 * (tc4 + 1)], start=(a == 0), stop=False),
                          wait=[tcst] + prev_ps + list(waits), inc=0, tag=f"{tag}mm{c}_{tc4}") for a in range(8)]
                    mm.append(p.on("tensor", lambda e, c=c, tc4=tc4, ps=ps: e.matmul(ps[:, 0:512], identb[:, :], res_t[:, 2048 * c + 512 * tc4:2048 * c + 512 * (tc4 + 1)], start=False, stop=True),
                                   wait=list(waits), inc=1, tag=f"{tag}res{c}_{tc4}"))
                    bias = pv[:, bias_base + c:bias_base + c + 1] if bias_base is not None else 0.0
                    ev = p.on("scalar", lambda e, c=c, tc4=tc4, ps=ps, bias=bias: e.activation(dst_t[:, 2048 * c + 512 * tc4:2048 * c + 512 * (tc4 + 1)], ps[:, 0:512], AF.Identity, bias=bias) if bias_base is not None else e.activation(dst_t[:, 2048 * c + 512 * tc4:2048 * c + 512 * (tc4 + 1)], ps[:, 0:512], AF.Copy),
                              wait=[mm[-1]], inc=1, tag=f"{tag}ev{c}_{tc4}")
                    prev_ps = [ev]
                    last_mm = mm[-1]
                    evs.append(ev)
                wk_used[par] = last_mm
            st_ctx["psab"] = prev_ps
            st_ctx["wkfree"] = last_mm
            return evs

        def emit_gate(s, waits):
            wk_used = {0: st_ctx["wkfree"], 1: st_ctx["wkfree"]}
            prev_ps = st_ctx["psab"]
            done = []
            for c in range(8):
                par = c % 2
                tcst = p.on("sync", lambda e, c=c, par=par: e.dma_start(out=wk[:, 1024 * par:1024 * (par + 1)], in_=wbf_att[8, c]),
                            wait=list(waits) + [wk_used[par], conv_tok[("att", 8, c)]], inc=16, sem="ld_w0" if par == 0 else "ld_w1", tag=f"gw{c}")
                last_mm = None
                for tc4 in range(4):
                    ps = psA if tc4 % 2 == 0 else psB
                    sl = 512 * (tc4 % 2)
                    mm = [p.on("tensor", lambda e, a=a, tc4=tc4, ps=ps, par=par: e.matmul(ps[:, 0:512], wk[:, 1024 * par + 128 * a:1024 * par + 128 * (a + 1)], ao[:, 2048 * a + 512 * tc4:2048 * a + 512 * (tc4 + 1)], start=(a == 0), stop=(a == 7)),
                          wait=[tcst] + prev_ps, inc=(1 if a == 7 else 0), tag=f"gmm{c}_{tc4}") for a in range(8)]
                    ex = p.on("scalar", lambda e, c=c, ps=ps, sl=sl: e.activation(sh1[:, sl:sl + 512], ps[:, 0:512], AF.Exp, scale=cc[:, CC_NEG1:CC_NEG1 + 1], bias=gbn[:, c:c + 1]),
                              wait=[mm[-1]] + done[-1:], inc=1, tag=f"gex{c}_{tc4}")
                    ad = p.on("vector", lambda e, sl=sl: e.tensor_scalar_add(sh1[:, sl:sl + 512], sh1[:, sl:sl + 512], 1.0), wait=[ex], inc=1)
                    rc = p.on("vector", lambda e, sl=sl: e.reciprocal(sh2[:, sl:sl + 512], sh1[:, sl:sl + 512]), wait=[ad], inc=1, lowp=True)
                    mu = p.on("vector", lambda e, c=c, tc4=tc4, sl=sl: e.tensor_mul(ao[:, 2048 * c + 512 * tc4:2048 * c + 512 * (tc4 + 1)], ao[:, 2048 * c + 512 * tc4:2048 * c + 512 * (tc4 + 1)], sh2[:, sl:sl + 512]),
                              wait=[rc], inc=1, tag=f"gmu{c}_{tc4}")
                    prev_ps = [ex]
                    done.append(mu)
                    last_mm = mm[-1]
                wk_used[par] = last_mm
            st_ctx["psab"] = prev_ps
            st_ctx["wkfree"] = last_mm
            return done[-4:]

        def emit_ln_feature(s, t_t, affine, waits, tag):
            prev_ps = st_ctx["psab"]
            prev_psc = st_ctx["psc"]
            prev_psab_ln = st_ctx["psab"]
            out_toks = []
            sq_prev = None
            for hf in range(2):
                mmy, mms_ = [], []
                sq_toks = []
                for c in range(8):
                    sq = p.on("scalar", lambda e, c=c, hf=hf: e.activation(sh1[:, 0:1024], t_t[:, 2048 * c + 1024 * hf:2048 * c + 1024 * (hf + 1)], AF.Square),
                              wait=list(waits) + [sq_prev] + (mms_[-1:] if mms_ else []), inc=1, tag=f"{tag}sq{c}")
                    sq_prev = sq
                    for j in range(2):
                        mmy.append(p.on("tensor", lambda e, c=c, j=j, hf=hf: e.matmul(psC[0:1, 512 * j:512 * (j + 1)], onesc[:, :], t_t[:, 2048 * c + 1024 * hf + 512 * j:2048 * c + 1024 * hf + 512 * (j + 1)], start=(c == 0), stop=(c == 7)),
                                   wait=list(waits) + prev_psc, inc=1, tag=f"{tag}my{c}"))
                        mms_.append(p.on("tensor", lambda e, c=c, j=j: e.matmul(psC[0:1, 1024 + 512 * j:1024 + 512 * (j + 1)], onesc[:, :], sh1[:, 512 * j:512 * (j + 1)], start=(c == 0), stop=(c == 7)),
                                    wait=[sq], inc=1, tag=f"{tag}ms{c}"))
                r1 = p.on("vector", lambda e: e.tensor_scalar_mul(psC[0:1, 0:1024], psC[0:1, 0:1024], 1.0 / 1024.0), wait=mmy[-2:] + mms_[-2:], inc=1)
                r2 = p.on("vector", lambda e: e.tensor_scalar_mul(psC[0:1, 1024:2048], psC[0:1, 1024:2048], 1.0 / 1024.0), wait=[r1], inc=1)
                cmu = p.on("vector", lambda e: e.tensor_copy(rowb2[:, 0:1024], psC[0:1, 0:1024]), wait=[r2], inc=1)
                r3 = p.on("vector", lambda e: e.tensor_mul(rowb1[:, 1024:2048], psC[0:1, 0:1024], rowb2[0:1, 0:1024]), wait=[cmu], inc=1)
                r4 = p.on("vector", lambda e: e.tensor_sub(psC[0:1, 1024:2048], psC[0:1, 1024:2048], rowb1[0:1, 1024:2048]), wait=[r3], inc=1)
                r5 = p.on("scalar", lambda e: e.activation(psC[0:1, 1024:2048], psC[0:1, 1024:2048], AF.Ln, bias=cc[0:1, CC_EPS5:CC_EPS5 + 1]), wait=[r4], inc=1)
                r6 = p.on("scalar", lambda e: e.activation(psC[0:1, 1024:2048], psC[0:1, 1024:2048], AF.Exp, scale=cc[0:1, CC_NEGH:CC_NEGH + 1]), wait=[r5], inc=1)
                c1 = p.on("vector", lambda e: e.tensor_copy(rowb1[:, 0:1024], psC[0:1, 1024:2048]), wait=[r6], inc=1)
                c2 = p.on("vector", lambda e: e.tensor_mul(rowb2[:, 0:1024], psC[0:1, 0:1024], rowb1[0:1, 0:1024]), wait=[c1], inc=1)
                bca = [p.on("tensor", lambda e, j=j: e.matmul(psC[:, 512 * j:512 * (j + 1)], onesr[:, :], rowb1[0:1, 512 * j:512 * (j + 1)], start=True, stop=True),
                       wait=[c2, r3], inc=(1 if j else 0), tag=f"{tag}ba") for j in range(2)]
                bcb = [p.on("tensor", lambda e, j=j: e.matmul(psC[:, 1024 + 512 * j:1024 + 512 * (j + 1)], onesr[:, :], rowb2[0:1, 512 * j:512 * (j + 1)], start=True, stop=True),
                       wait=[c2], inc=(1 if j else 0), tag=f"{tag}bb") for j in range(2)]
                e1 = p.on("vector", lambda e: e.tensor_copy(sh1[:, 0:1024], psC[:, 0:1024]), wait=[bca[1]], inc=1)
                e2 = p.on("vector", lambda e: e.tensor_copy(sh2[:, 0:1024], psC[:, 1024:2048]), wait=[bcb[1]], inc=1)
                ap_prev = [e1, e2]
                for c in range(8):
                    a1 = p.on("vector", lambda e, c=c, hf=hf: e.tensor_mul(t_t[:, 2048 * c + 1024 * hf:2048 * c + 1024 * (hf + 1)], t_t[:, 2048 * c + 1024 * hf:2048 * c + 1024 * (hf + 1)], sh1[:, 0:1024]),
                              wait=ap_prev, inc=1, tag=f"{tag}a1{c}")
                    a2 = p.on("vector", lambda e, c=c, hf=hf: e.tensor_sub(t_t[:, 2048 * c + 1024 * hf:2048 * c + 1024 * (hf + 1)], t_t[:, 2048 * c + 1024 * hf:2048 * c + 1024 * (hf + 1)], sh2[:, 0:1024]),
                              wait=[a1], inc=1)
                    tk = a2
                    if affine is not None:
                        gb_, bb_ = affine
                        tk = p.on("vector", lambda e, c=c, hf=hf, gb_=gb_, bb_=bb_: e.tensor_scalar(t_t[:, 2048 * c + 1024 * hf:2048 * c + 1024 * (hf + 1)], t_t[:, 2048 * c + 1024 * hf:2048 * c + 1024 * (hf + 1)], pv[:, gb_ + c:gb_ + c + 1], pv[:, bb_ + c:bb_ + c + 1], ALU.mult, ALU.add),
                                  wait=[a2], inc=1, tag=f"{tag}af{c}")
                    ap_prev = [tk]
                    out_toks.append(tk)
                prev_psc = [e1, e2]
                prev_psab_ln = [r4]
                sq_prev = out_toks[-1]
            st_ctx["psc"] = prev_psc
            return out_toks

        def emit_routing(s, waits):
            mgwb = wk[:, 0:64]
            par_r = wpp["n"] % 2
            wpp["n"] += 1
            tl = p.on("sync", lambda e, par_r=par_r: e.dma_start(out=wst[:, 512 * par_r:512 * par_r + 64].rearrange("p (a j) -> p a j", a=8), in_=mgw.rearrange("(a p) j -> p a j", p=128)),
                      wait=list(waits) + [wpp[par_r]], inc=16, sem="ld_w0" if par_r == 0 else "ld_w1")
            tc_ = p.on("vector", lambda e, par_r=par_r: e.tensor_copy(mgwb[:, :], wst[:, 512 * par_r:512 * par_r + 64]), wait=[tl, st_ctx["wkfree"]], inc=1)
            wpp[par_r] = tc_
            prev_ps = st_ctx["psab"]
            toks = []
            for tc in range(16):
                mm = [p.on("tensor", lambda e, a=a, tc=tc: e.matmul(psB[:, 0:8], xA[:, 2048 * a + 128 * tc:2048 * a + 128 * (tc + 1)], mgwb[:, 8 * a:8 * (a + 1)], start=(a == 0), stop=False),
                      wait=[tc_, t_mgbc] + prev_ps + list(waits), inc=0, tag=f"rt{tc}") for a in range(8)]
                mm.append(p.on("tensor", lambda e, tc=tc: e.matmul(psB[:, 0:8], onesr[:, :], mgbr[:, :], start=False, stop=True), inc=1, tag=f"rtb{tc}"))
                cp = p.on("vector", lambda e: e.tensor_copy(lgt[:, :], psB[:, 0:8]), wait=[mm[-1]] + toks[-1:], inc=1)
                mx = p.on("vector", lambda e: e.max(v8t[:, :], lgt[:, :]), wait=[cp], inc=1)
                mi = p.on("vector", lambda e: e.max_index(i8t[:, :], v8t[:, :], lgt[:, :]), wait=[mx], inc=1)
                cf = p.on("vector", lambda e: e.tensor_copy(i8f[:, :], i8t[:, :]), wait=[mi], inc=1)
                ch = p.on("vector", lambda e, tc=tc: e.tensor_max(chosen[:, tc:tc + 1], i8f[:, 0:1], i8f[:, 1:2]), wait=[cf], inc=1, tag=f"ch{tc}")
                prev_ps = [cp]
                toks.append(ch)
            st_ctx["psab"] = prev_ps
            if "chosen" in tap_d:
                p.on("gpsimd", lambda e: e.dma_start(out=tap_d["chosen"][s], in_=chosen[:, :]), wait=toks, inc=16, sem="st")
            return toks

        def hh_ap(fc, lo, n, bank=0):
            if bank:
                return hhB[:, 512 * fc + lo:512 * fc + lo + n]
            if fc < 8:
                return qk[:, 512 * fc + lo:512 * fc + lo + n]
            if fc < 12:
                return pb[:, 512 * (fc - 8) + lo:512 * (fc - 8) + lo + n]
            return vab[:, 512 * (fc - 12) + lo:512 * (fc - 12) + lo + n]

        def emit_moe(s, waits, route_toks):
            z0 = p.on("vector", lambda e: e.memset(mz[:, 0:8192], 0.0), wait=list(waits), inc=1)
            z1 = p.on("vector", lambda e: e.memset(mz[:, 8192:16384], 0.0), wait=[z0], inc=1)
            acc_tok = {tc: z1 for tc in range(16)}
            mst = {"cast": None, "w1r": route_toks[-1], "w1s": {0: None, 1: None}, "w2": None,
                   "hh": {0: list(waits), 1: list(waits) + conv_all}, "statcp": None, "stt": None, "apply": None}
            prev_ps = st_ctx["psab"] + st_ctx["psc"]
            last_ev = {0: None, 1: None}
            pend = {"w2": None}

            def flush_w2():
                if pend["w2"] is not None:
                    f = pend["w2"]
                    pend["w2"] = None
                    f()

            gidx = 0
            for ee in range(NE):
                flush_w2()
                res_cast = {}
                for jj in range(8):
                    res_cast[jj] = p.on("sync", lambda e, ee=ee, jj=jj: e.dma_start(out=wk[:, 1024 * jj:1024 * (jj + 1)], in_=wbf_w1[ee, jj]),
                                        wait=[mst["w1r"], conv_tok[("w1", ee, jj)]], inc=16, sem="ld_w0" if jj % 2 == 0 else "ld_w1", tag=f"w1r{ee}_{jj}")
                w2_cast = {}
                for fc in range(16):
                    w2_cast[fc] = p.on("sync", lambda e, ee=ee, fc=fc: e.dma_start(out=ao[:, 1024 * fc:1024 * (fc + 1)], in_=wbf_w2[ee, fc]),
                                       wait=[mst["w2"], conv_tok[("w2", ee, fc)]], inc=16, sem="ld_w0" if fc % 2 == 0 else "ld_w1", tag=f"w2l{ee}_{fc}")

                def stream_eighth(ei, ee):
                    par = ei % 2
                    toks = {}
                    for fco in range(2):
                        fc = 8 + 2 * ei + fco
                        toks[fc] = p.on("sync", lambda e, ee=ee, fc=fc, par=par, fco=fco: e.dma_start(out=wk[:, 8192 + 2048 * par + 1024 * fco:8192 + 2048 * par + 1024 * (fco + 1)], in_=wbf_w1[ee, fc]),
                                        wait=[mst["w1s"][par], conv_tok[("w1", ee, fc)]], inc=16, sem="ld_w0" if par == 0 else "ld_w1", tag=f"w1s{ee}_{fc}")
                    return toks

                for tc5 in range(4):
                    bank = gidx % 2
                    gidx += 1
                    scast = {}
                    scast.update(stream_eighth(0, ee))
                    scast.update(stream_eighth(1, ee))
                    sq_prev = None
                    my = ms = None
                    myt = {}
                    sqt_ = {}
                    evs = {}
                    for fc in range(16):
                        par = fc % 2
                        psl = psA[:, 512 * par:512 * (par + 1)]
                        if fc < 8:
                            base = 1024 * fc
                            wt = res_cast[fc]
                        else:
                            ei = (fc - 8) // 2
                            base = 8192 + 2048 * (ei % 2) + 1024 * ((fc - 8) % 2)
                            wt = scast[fc]
                        mm = [p.on("tensor", lambda e, a=a, base=base, tc5=tc5, psl=psl: e.matmul(psl, wk[:, base + 128 * a:base + 128 * (a + 1)], xA[:, 2048 * a + 512 * tc5:2048 * a + 512 * (tc5 + 1)], start=(a == 0), stop=(a == 7)),
                              wait=[wt, last_ev[par]] + ((prev_ps + list(waits)) if fc == 0 else []), inc=(1 if a == 7 else 0), tag=f"h{ee}_{tc5}_{fc}") for a in range(8)]
                        if fc == 9:
                            mst["w1s"][0] = mm[-1]
                            scast.update(stream_eighth(2, ee))
                        if fc == 11:
                            mst["w1s"][1] = mm[-1]
                            scast.update(stream_eighth(3, ee))
                        if fc == 13:
                            mst["w1s"][0] = mm[-1]
                        if fc == 15:
                            mst["w1s"][1] = mm[-1]
                        if fc == 7 and tc5 == 3:
                            mst["w1r"] = mm[-1]
                        ev = p.on("scalar", lambda e, fc=fc, ee=ee, psl=psl, bank=bank: e.activation(hh_ap(fc, 0, 512, bank), psl, AF.Identity, bias=pv[:, PC_B1 + 16 * ee + fc:PC_B1 + 16 * ee + fc + 1]),
                                  wait=[mm[-1]] + mst["hh"][bank], inc=1, tag=f"hev{ee}_{tc5}_{fc}")
                        evs[fc] = ev
                        last_ev[par] = ev
                        sq = p.on("vector", lambda e, fc=fc, bank=bank: e.tensor_mul(sh2[:, 0:512], hh_ap(fc, 0, 512, bank), hh_ap(fc, 0, 512, bank)), wait=[ev, sq_prev, ms], inc=1)
                        sqt_[fc] = sq
                        my = p.on("tensor", lambda e, fc=fc, bank=bank: e.matmul(psB[0:1, 0:512], onesc[:, :], hh_ap(fc, 0, 512, bank), start=(fc == 0), stop=(fc == 15)),
                                  wait=[ev] + ([mst["statcp"]] if fc == 0 else []), inc=1, tag=f"my{ee}_{tc5}_{fc}")
                        myt[fc] = my
                        ms = p.on("tensor", lambda e, fc=fc: e.matmul(psB[0:1, 512:1024], onesc[:, :], sh2[:, 0:512], start=(fc == 0), stop=(fc == 15)),
                                  wait=[sq], inc=1, tag=f"ms{ee}_{tc5}_{fc}")
                        sq_prev = sq
                    flush_w2()
                    r1 = p.on("vector", lambda e: e.tensor_scalar_mul(psB[0:1, 0:512], psB[0:1, 0:512], 1.0 / 2048.0), wait=[my, ms], inc=1)
                    r2 = p.on("vector", lambda e: e.tensor_scalar_mul(psB[0:1, 512:1024], psB[0:1, 512:1024], 1.0 / 2048.0), wait=[r1], inc=1)
                    cmu = p.on("vector", lambda e: e.tensor_copy(rowb1[:, 1024:1536], psB[0:1, 0:512]), wait=[r2], inc=1)
                    r3 = p.on("vector", lambda e: e.tensor_mul(rowb1[:, 1536:2048], psB[0:1, 0:512], rowb1[0:1, 1024:1536]), wait=[cmu], inc=1)
                    r4 = p.on("vector", lambda e: e.tensor_sub(psB[0:1, 512:1024], psB[0:1, 512:1024], rowb1[0:1, 1536:2048]), wait=[r3], inc=1)
                    r5 = p.on("scalar", lambda e: e.activation(psB[0:1, 512:1024], psB[0:1, 512:1024], AF.Ln, bias=cc[0:1, CC_EPS5:CC_EPS5 + 1]), wait=[r4], inc=1)
                    r6 = p.on("scalar", lambda e: e.activation(psB[0:1, 512:1024], psB[0:1, 512:1024], AF.Exp, scale=cc[0:1, CC_NEGH:CC_NEGH + 1]), wait=[r5], inc=1)
                    c1 = p.on("vector", lambda e: e.tensor_copy(rowb1[:, 0:512], psB[0:1, 512:1024]), wait=[r6], inc=1)
                    c2 = p.on("vector", lambda e: e.tensor_mul(rowb1[:, 512:1024], psB[0:1, 0:512], rowb1[0:1, 0:512]), wait=[c1], inc=1)
                    ba = p.on("tensor", lambda e: e.matmul(psB[:, 0:512], onesr[:, :], rowb1[0:1, 0:512], start=True, stop=True), wait=[c2], inc=1, tag="moeba")
                    bb = p.on("tensor", lambda e: e.matmul(psB[:, 512:1024], onesr[:, :], rowb1[0:1, 512:1024], start=True, stop=True), wait=[c2], inc=1, tag="moebb")
                    e1 = p.on("vector", lambda e: e.tensor_copy(sh1[:, 0:512], psB[:, 0:512]), wait=[ba, mst["statcp"], mst["apply"]], inc=1)
                    e2 = p.on("vector", lambda e: e.tensor_copy(sh1[:, 512:1024], psB[:, 512:1024]), wait=[bb, e1], inc=1)
                    mst["statcp"] = e2
                    glu = {}
                    for fc in range(16):
                        a1 = p.on("vector", lambda e, fc=fc, bank=bank: e.tensor_mul(hh_ap(fc, 0, 512, bank), hh_ap(fc, 0, 512, bank), sh1[:, 0:512]), wait=[e1, e2, myt[fc], sqt_[fc]], inc=1)
                        a2 = p.on("vector", lambda e, fc=fc, bank=bank: e.tensor_sub(hh_ap(fc, 0, 512, bank), hh_ap(fc, 0, 512, bank), sh1[:, 512:1024]), wait=[a1], inc=1)
                        gl = p.on("scalar", lambda e, fc=fc, ee=ee, bank=bank: e.activation(hh_ap(fc, 0, 512, bank), hh_ap(fc, 0, 512, bank), AF.Gelu, scale=pv[:, PC_LNG + 16 * ee + fc:PC_LNG + 16 * ee + fc + 1], bias=pv[:, PC_LNB + 16 * ee + fc:PC_LNB + 16 * ee + fc + 1]),
                                  wait=[a2], inc=1, tag=f"gl{ee}_{tc5}_{fc}")
                        glu[fc] = gl
                        mst["apply"] = a2

                    def do_w2(ee=ee, tc5=tc5, bank=bank, glu=glu, w2_cast=w2_cast):
                        wmm = None
                        for tsub in range(4):
                            tcg = 4 * tc5 + tsub
                            par2 = tsub % 2
                            mc = p.on("vector", lambda e, tcg=tcg, ee=ee: e.tensor_single_scalar(colf[:, 1:2], chosen[:, tcg:tcg + 1], float(ee), ALU.is_equal),
                                      wait=list(route_toks[-1:]) + [mst["stt"]], inc=1)
                            mr = p.on("vector", lambda e, ee=ee: e.tensor_mul(colf[:, 2:3], colf[:, 1:2], pv[:, PC_RSC + ee:PC_RSC + ee + 1]), wait=[mc], inc=1)
                            for fc in range(16):
                                for dh in range(2):
                                    wmm = p.on("tensor", lambda e, fc=fc, dh=dh, tsub=tsub, par2=par2, bank=bank: e.matmul(psC[:, 1024 * par2 + 512 * dh:1024 * par2 + 512 * (dh + 1)], hh_ap(fc, 128 * tsub, 128, bank), ao[:, 1024 * fc + 512 * dh:1024 * fc + 512 * (dh + 1)], start=(fc == 0), stop=(fc == 15)),
                                              wait=[glu[fc], w2_cast[fc]] + ([acc_tok[(tcg - 2) % 16]] if fc == 0 else []), inc=(1 if (fc == 15 and dh == 1) else 0), tag=f"w2{ee}_{tcg}")
                            stt = p.on("vector", lambda e, tcg=tcg, par2=par2: e.scalar_tensor_tensor(mz[:, 1024 * tcg:1024 * (tcg + 1)], psC[:, 1024 * par2:1024 * (par2 + 1)], colf[:, 2:3], mz[:, 1024 * tcg:1024 * (tcg + 1)], ALU.mult, ALU.add),
                                      wait=[wmm, mr, acc_tok[tcg]], inc=1, tag=f"stt{ee}_{tcg}")
                            acc_tok[tcg] = stt
                            mst["stt"] = stt
                        mst["w2"] = wmm
                        mst["hh"][bank] = [wmm]
                    pend["w2"] = do_w2
            flush_w2()
            st_ctx["psab"] = [last_ev[0], last_ev[1]]
            st_ctx["psc"] = [mst["stt"]]
            st_ctx["wkfree"] = mst["w1r"]
            return acc_tok

        def emit_final(s, waits, acc_tok):
            psAb = psA[:, :].bitcast(BF16)
            psCb = psC[:, :].bitcast(BF16)
            prev_psab = st_ctx["psab"]
            prev_psc = st_ctx["psc"]
            tn1 = p.on("sync", lambda e: e.dma_start(out=wst[0:1, 0:D], in_=n2g_d.rearrange("(a d) -> a d", a=1)), wait=list(waits) + [wpp[0], wpp[1]], inc=16)
            tn2 = p.on("gpsimd", lambda e: e.tensor_copy(rowb1[0:1, 0:1024], wst[0:1, 0:D]), wait=[tn1, acc_tok[15]] + list(waits), inc=1)
            bg = [p.on("tensor", lambda e, j=j: e.matmul(psC[:, 512 * j:512 * (j + 1)], onesr[:, :], rowb1[0:1, 512 * j:512 * (j + 1)], start=True, stop=True), wait=[tn2] + prev_psc, inc=1, tag="n2g") for j in range(2)]
            eg = p.on("vector", lambda e: e.tensor_copy(n2gbc[:, :], psC[:, 0:1024]), wait=bg + list(waits), inc=1)
            tn3 = p.on("sync", lambda e: e.dma_start(out=wst[0:1, 0:D], in_=n2b_d.rearrange("(a d) -> a d", a=1)), wait=[tn2], inc=16)
            tn4 = p.on("gpsimd", lambda e: e.tensor_copy(rowb1[0:1, 0:1024], wst[0:1, 0:D]), wait=[tn3, eg], inc=1)
            bb_ = [p.on("tensor", lambda e, j=j: e.matmul(psC[:, 1024 + 512 * j:1024 + 512 * (j + 1)], onesr[:, :], rowb1[0:1, 512 * j:512 * (j + 1)], start=True, stop=True), wait=[tn4], inc=1, tag="n2b") for j in range(2)]
            ebb = p.on("vector", lambda e: e.tensor_copy(n2bbc[:, :], psC[:, 1024:2048]), wait=bb_, inc=1)
            prev_psc = [eg, ebb]
            wpp[0] = tn4
            wpp[1] = tn4
            prev_evs = []
            st_prev = None
            out_toks = []
            for tcg in range(16):
                t3 = [p.on("tensor", lambda e, c=c, tcg=tcg: e.transpose(psAb[:, 128 * c:128 * (c + 1)], xA[:, 2048 * c + 128 * tcg:2048 * c + 128 * (tcg + 1)], identb[:, :]),
                      wait=list(waits) + prev_psab + prev_evs, inc=(1 if c == 7 else 0), tag=f"t3_{tcg}") for c in range(8)]
                t2 = [p.on("tensor", lambda e, c=c, tcg=tcg: e.transpose(psAb[:, 1024 + 128 * c:1024 + 128 * (c + 1)], xB[:, 2048 * c + 128 * tcg:2048 * c + 128 * (tcg + 1)], identb[:, :]),
                      wait=prev_evs + list(waits), inc=(1 if c == 7 else 0), tag=f"t2_{tcg}") for c in range(8)]
                ev3 = p.on("vector", lambda e: e.tensor_scalar_mul(x3tm[:, :], psAb[:, 0:1024], 2.0), wait=[t3[-1]], inc=1)
                ev2 = p.on("vector", lambda e: e.tensor_copy(x2tm[:, :], psAb[:, 1024:2048]), wait=[t2[-1]], inc=1)
                prev_evs = [ev3, ev2]
                cb = p.on("vector", lambda e, tcg=tcg: e.tensor_copy(chosb[:, 0:1], chosen[:, tcg:tcg + 1]), wait=[st_prev], inc=1)
                tch = p.on("tensor", lambda e: e.transpose(psCb[0:1, 0:128], chosb[:, 0:1], identb[:, :]), wait=[cb] + prev_psc, inc=1, tag=f"tch{tcg}")
                cr = p.on("vector", lambda e: e.tensor_copy(chrow[:, :], psCb[0:1, 0:128]), wait=[tch], inc=1)
                b8 = p.on("tensor", lambda e: e.matmul(psC[0:8, 512:640], onesr[0:1, 0:8], chrow[:, :], start=True, stop=True), wait=[cr], inc=1, tag=f"b8{tcg}")
                c8t = p.on("vector", lambda e: e.tensor_copy(ch8[:, :], psC[0:8, 512:640]), wait=[b8], inc=1)
                m8 = p.on("vector", lambda e: e.tensor_single_scalar(m8b[:, :], ch8[:, :], iota8[:, 0:1], ALU.is_equal), wait=[c8t], inc=1)
                bsel = [p.on("tensor", lambda e, dh=dh: e.matmul(psC[:, 1024 + 512 * dh:1024 + 512 * (dh + 1)], m8b[:, :], b2rs[:, 512 * dh:512 * (dh + 1)], start=True, stop=True),
                        wait=[m8, t_b2rs], inc=(1 if dh else 0), tag=f"bsel{tcg}") for dh in range(2)]
                zt1 = p.on("vector", lambda e, tcg=tcg: e.tensor_add(ztt[:, :], mz[:, 1024 * tcg:1024 * (tcg + 1)], x3tm[:, :]), wait=[ev3, acc_tok[tcg]], inc=1)
                zt2 = p.on("vector", lambda e: e.tensor_add(ztt[:, :], ztt[:, :], psC[:, 1024:2048]), wait=[zt1, bsel[-1]], inc=1)
                def ln_tok(src_tile, waitt, tagp):
                    sq = p.on("scalar", lambda e: e.activation(sqt[:, :], src_tile[:, :], AF.Square, accum_out=colf[:, 3:4]), wait=waitt, inc=1, tag=tagp + "sq")
                    idn = p.on("scalar", lambda e: e.activation(sqt[:, :], src_tile[:, :], AF.Identity, accum_out=colf[:, 4:5]), wait=[sq], inc=1)
                    m1_ = p.on("vector", lambda e: e.tensor_scalar_mul(colf[:, 5:6], colf[:, 4:5], 1.0 / 1024.0), wait=[idn], inc=1)
                    m2_ = p.on("vector", lambda e: e.tensor_mul(colf[:, 6:7], colf[:, 5:6], colf[:, 5:6]), wait=[m1_], inc=1)
                    v_ = p.on("vector", lambda e: e.scalar_tensor_tensor(colf[:, 6:7], colf[:, 3:4], 1.0 / 1024.0, colf[:, 6:7], ALU.mult, ALU.subtract), wait=[m2_], inc=1)
                    l_ = p.on("scalar", lambda e: e.activation(colf[:, 6:7], colf[:, 6:7], AF.Ln, bias=cc[:, CC_EPS5:CC_EPS5 + 1]), wait=[v_], inc=1)
                    r_ = p.on("scalar", lambda e: e.activation(colf[:, 6:7], colf[:, 6:7], AF.Exp, scale=cc[:, CC_NEGH:CC_NEGH + 1]), wait=[l_], inc=1)
                    b_ = p.on("vector", lambda e: e.tensor_mul(colf[:, 7:8], colf[:, 5:6], colf[:, 6:7]), wait=[r_], inc=1)
                    return b_
                b_ = ln_tok(ztt, [zt2], f"n2_{tcg}")
                f1 = p.on("vector", lambda e: e.tensor_scalar(f2t[:, :], ztt[:, :], colf[:, 6:7], colf[:, 7:8], ALU.mult, ALU.subtract), wait=[b_], inc=1)
                f2_ = p.on("vector", lambda e: e.tensor_mul(f2t[:, :], f2t[:, :], n2gbc[:, :]), wait=[f1, eg], inc=1)
                f3 = p.on("vector", lambda e: e.tensor_add(f2t[:, :], f2t[:, :], n2bbc[:, :]), wait=[f2_, ebb], inc=1)
                f4 = p.on("vector", lambda e: e.tensor_add(f2t[:, :], f2t[:, :], x2tm[:, :]), wait=[f3, ev2], inc=1)
                b2_ = ln_tok(f2t, [f4], f"fin_{tcg}")
                fo = p.on("vector", lambda e: e.tensor_scalar(ot[:, :], f2t[:, :], colf[:, 6:7], colf[:, 7:8], ALU.mult, ALU.subtract), wait=[b2_, st_prev, (acc_tok[15] if tcg == 0 else None)], inc=1, tag=f"fo{tcg}")
                st_prev = p.on("gpsimd", lambda e, s=s, tcg=tcg: e.dma_start(out=out_d[s, 128 * tcg:128 * (tcg + 1), :], in_=ot[:, :]), wait=[fo], inc=16, sem="st", tag=f"ost{tcg}")
                out_toks.append(st_prev)
                prev_psc = [zt2, c8t, cr]
            st_ctx["psab"] = prev_evs
            st_ctx["psc"] = prev_psc
            return out_toks

        def emit_tap(name, s_, src_handle, waits):
            if name not in tap_d:
                return waits
            tprev = list(waits)
            for c in range(8):
                tprev = [p.on("gpsimd", lambda e, c=c, s_=s_, name=name: e.dma_start(out=tap_d[name][s_, 128 * c:128 * (c + 1), :], in_=src_handle[:, 2048 * c:2048 * (c + 1)]), wait=list(waits) + tprev, inc=16, sem="st")]
            return tprev

        if upto == "mask":
            xload0 = emit_xload(0, wld)
            for h in range(NH):
                emit_mask_head(h)
            p.on("sync", lambda e: e.nop(), wait=[p.tok("st"), p.tok("gpsimd")], inc=0)
        else:
            prev_fin = []
            for s_ in range(SEQ_PER_CORE):
                xl = emit_xload(s_, wld + prev_fin + [t_b2rs])
                if s_ == 0:
                    emit_conversion()
                    def hook(pr):
                        mask_state["psab"] = st_ctx["psab"]
                        emit_mask_head(2 * pr)
                        emit_mask_head(2 * pr + 1)
                        st_ctx["psab"] = mask_state["psab"]
                    ao1 = emit_attention(0, True, xA, xl, pre_pair_hook=hook)
                    wo_w = ao1[-1:] + mask_state["uvfree"] + [mask_state["ln_prev"]]
                else:
                    st_ctx["mbf"] = prev_fin
                    ao1 = emit_attention(s_, True, xA, xl)
                    wo_w = ao1[-1:]
                if upto == "ao1":
                    break
                evs = emit_projphase(s_, 3, ao, xA, xB, PC_BO, wo_w, f"wo{s_}")
                ln1 = emit_ln_feature(s_, xB, None, evs[-1:], f"ln1_{s_}")
                ln1 = emit_tap("x2", s_, xB, ln1)
                if upto == "x2":
                    break
                at2 = emit_attention(s_, False, xB, ln1[-1:] + ln1)
                gt = emit_gate(s_, at2[-1:])
                at2t = emit_tap("ao2", s_, ao, gt)
                evs2 = emit_projphase(s_, 7, ao, xB, xA, None, at2t[-1:] + gt[-1:], f"ow{s_}")
                lnn1 = emit_ln_feature(s_, xA, (PC_N1G, PC_N1B), evs2[-1:], f"lnn1_{s_}")
                lnn1 = emit_tap("x3", s_, xA, lnn1)
                if upto == "x3":
                    break
                rt = emit_routing(s_, lnn1[-1:] + lnn1)
                acc = emit_moe(s_, lnn1[-1:] + [rt[-1]], rt)
                fin = emit_final(s_, lnn1[-1:] + [rt[-1]], acc)
                prev_fin = p.barrier()
            p.on("sync", lambda e: e.nop(), wait=[p.tok("st"), p.tok("st2"), p.tok("wcv_a"), p.tok("wcv_m"), p.tok("cv0"), p.tok("cv1"), p.tok("gpsimd"), p.tok("vector"), p.tok("scalar"), p.tok("tensor"), p.tok("sync"), p.tok("ld_w"), p.tok("ld_w0"), p.tok("ld_w1"), p.tok("ld_m"), p.tok("ld_sg")], inc=0)

        @blk.sync
        def _(eng):
            for fn in p.q["sync"]:
                fn(eng)
        @blk.tensor
        def _(eng):
            for fn in p.q["tensor"]:
                fn(eng)
        @blk.vector
        def _(eng):
            for fn in p.q["vector"]:
                fn(eng)
        @blk.scalar
        def _(eng):
            for fn in p.q["scalar"]:
                fn(eng)
        @blk.gpsimd
        def _(eng):
            for fn in p.q["gpsimd"]:
                fn(eng)
    return nc


def _pack_vec(i):
    pvv = np.zeros((128, PC_N), dtype=np.float32)
    def chunks(v):
        return np.ascontiguousarray(v.reshape(8, 128).T)
    pvv[:, PC_BQ:PC_BQ + 8] = chunks(np.asarray(i["L0_attn_bq"], np.float32))
    pvv[:, PC_BK:PC_BK + 8] = chunks(np.asarray(i["L0_attn_bk"], np.float32))
    pvv[:, PC_BO:PC_BO + 8] = chunks(np.asarray(i["L0_attn_bo"], np.float32))
    pvv[:, PC_GB:PC_GB + 8] = chunks(np.asarray(i["L0_ffn_gate_b"], np.float32))
    pvv[:, PC_N1G:PC_N1G + 8] = chunks(np.asarray(i["L0_ffn_n1_g"], np.float32))
    pvv[:, PC_N1B:PC_N1B + 8] = chunks(np.asarray(i["L0_ffn_n1_b"], np.float32))
    pvv[:, PC_TAU] = np.float32(i["L0_attn_temperature"])
    pvv[:, PC_THR] = np.float32(i["L0_attn_threshold"])
    pvv[0:16, PC_PHASE:PC_PHASE + 3] = np.asarray(i["L0_attn_phase"], np.float32)
    sw = np.asarray(i["L0_ffn_scale_w"], np.float32)
    for h in range(16):
        r = (h % 2) * 64
        pvv[r:r + 64, PC_SW + h] = sw[h]
    rsc = np.asarray(i["L0_ffn_res_scale"], np.float32)
    pvv[:, PC_RSC:PC_RSC + 8] = rsc[None, :]
    pvv[0:8, PC_RSCP] = rsc
    pvv[0:8, PC_IOTA] = np.arange(8, dtype=np.float32)
    b1 = np.asarray(i["L0_ffn_b1"], np.float32)
    lng = np.asarray(i["L0_ffn_ln_g"], np.float32)
    lnb = np.asarray(i["L0_ffn_ln_b"], np.float32)
    for e in range(8):
        pvv[:, PC_B1 + 16 * e:PC_B1 + 16 * (e + 1)] = np.ascontiguousarray(b1[e].reshape(16, 128).T)
        pvv[:, PC_LNG + 16 * e:PC_LNG + 16 * (e + 1)] = np.ascontiguousarray(lng[e].reshape(16, 128).T)
        pvv[:, PC_LNB + 16 * e:PC_LNB + 16 * (e + 1)] = np.ascontiguousarray(lnb[e].reshape(16, 128).T)
    return pvv


def make_in_maps(inputs):
    i = inputs
    x = np.asarray(i["L0_attn_x"], np.float32)
    xT = np.ascontiguousarray(x.reshape(NCORE, SEQ_PER_CORE, T, D).transpose(0, 1, 3, 2))
    pvv = _pack_vec(i)
    shared = {
        "Wq": np.asarray(i["L0_attn_Wq"], np.float32), "Wk": np.asarray(i["L0_attn_Wk"], np.float32),
        "Wv": np.asarray(i["L0_attn_Wv"], np.float32), "Wo": np.asarray(i["L0_attn_Wo"], np.float32),
        "qw": np.asarray(i["L0_ffn_q_w"], np.float32), "kw": np.asarray(i["L0_ffn_k_w"], np.float32),
        "vw": np.asarray(i["L0_ffn_v_w"], np.float32), "ow": np.asarray(i["L0_ffn_out_w"], np.float32),
        "gw": np.asarray(i["L0_ffn_gate_w"], np.float32),
        "mgw": np.asarray(i["L0_ffn_moe_gate_w"], np.float32),
        "w1": np.asarray(i["L0_ffn_w1"], np.float32), "w2": np.asarray(i["L0_ffn_w2"], np.float32),
        "b2": np.asarray(i["L0_ffn_b2"], np.float32),
        "bvr": np.asarray(i["L0_attn_bv"], np.float32),
        "n2g": np.asarray(i["L0_ffn_n2_g"], np.float32), "n2b": np.asarray(i["L0_ffn_n2_b"], np.float32),
        "mgb": np.asarray(i["L0_ffn_moe_gate_b"], np.float32),
        "pv": pvv,
    }
    return [dict(shared, xT=xT[c]) for c in range(NCORE)]


def kernel(**inputs):
    in_maps = make_in_maps(inputs)
    nc = bass.Bass("TRN2", target_bir_lowering=False)
    build_core(nc)
    res = run_bass_kernel_spmd(nc, in_maps, core_ids=list(range(NCORE)))
    outs = [np.asarray(r["out"], np.float32) for r in res.results]
    return np.concatenate(outs, axis=0).reshape(16, T, D)
